# version 01
# baseline (speedup 1.0000x reference)
import numpy as np
import ml_dtypes
from contextlib import ExitStack

import concourse.bass as bass
import concourse.mybir as mybir
from concourse.bass_utils import run_bass_kernel_spmd

F32 = mybir.dt.float32
BF16 = mybir.dt.bfloat16
AF = mybir.ActivationFunctionType
ALU = mybir.AluOpType
AX = mybir.AxisListType
NPBF = ml_dtypes.bfloat16

T = 2048
D = 2048
NT = 16
INTER = 4096
HEADS = 64
PD = 64
ND = 64
Q = 128
NCH = 16
HFF = 8192
AH = 16
ADH = 128
EPS_LN = 1e-5
EPS_MN = 1e-6
ASCALE = float(1.0 / np.sqrt(np.float32(ADH)))

ENGS = ("tensor", "vector", "scalar", "gpsimd", "sync")
QSEM = {"sync": "sd", "scalar": "ad", "gpsimd": "gd"}
ESEM = {"tensor": "te", "vector": "ve", "scalar": "se", "gpsimd": "ge"}


class Ctx:
    def __init__(self, nc, es):
        self.nc = nc
        self.es = es
        self.sems = {}
        self.cnt = {}
        self.last = {}
        self.lastinc = {}
        for name in ("sd", "ad", "gd", "te", "ve", "se", "ge"):
            self.sems[name] = es.enter_context(nc.semaphore(name))
            self.cnt[name] = 0

    def sb(self, name, shape, dtype):
        return self.es.enter_context(self.nc.sbuf_tensor(name, shape, dtype))

    def ps(self, name, shape, dtype=F32):
        return self.es.enter_context(self.nc.psum_tensor(name, shape, dtype))

    def eng(self, name):
        return getattr(self.nc, name)

    def inc(self, instr, sem, amt=1):
        instr.then_inc(self.sems[sem], amt)
        self.cnt[sem] += amt
        for e, s in ESEM.items():
            if s == sem and instr is self.last.get(e):
                self.lastinc[e] = True
        return self.cnt[sem]

    def wait(self, engine, sem, val):
        if val > 0:
            self.eng(engine).wait_ge(self.sems[sem], val)

    def dma(self, q, out, in_, inc=True, **kw):
        instr = self.eng(q).dma_start(out=out, in_=in_, **kw)
        self.last[q] = instr
        if inc:
            return self.inc(instr, QSEM[q], 16)
        return self.cnt[QSEM[q]]

    def op(self, engine, fname, *args, **kw):
        instr = getattr(self.eng(engine), fname)(*args, **kw)
        self.last[engine] = instr
        self.lastinc[engine] = False
        return instr

    def opi(self, engine, fname, *args, **kw):
        instr = self.op(engine, fname, *args, **kw)
        return self.inc(instr, ESEM[engine])

    def barrier(self):
        targets = [(s, self.cnt[s]) for s in ("sd", "ad", "gd", "te", "ve", "se", "ge")
                   if self.cnt[s] > 0]
        for e in ENGS:
            own = ESEM.get(e)
            for s, v in targets:
                if s == own:
                    continue
                self.eng(e).wait_ge(self.sems[s], v)

    def seal(self):
        for e in ("tensor", "vector", "scalar", "gpsimd"):
            ins = self.last.get(e)
            if ins is not None and not self.lastinc.get(e, True):
                self.inc(ins, ESEM[e])
            self.last[e] = None


WEIGHTS = {
    "ipT": ([D, 8384], np.float32, F32),
    "opT": ([INTER, D], np.float32, F32),
    "w1T": ([D, HFF], np.float32, F32),
    "w3T": ([D, HFF], np.float32, F32),
    "w2Tp": ([64, 128, D], np.float32, F32),
    "wkT": ([D, AH * ADH], np.float32, F32),
    "wqT": ([D, AH * ADH], np.float32, F32),
    "wvT": ([D, AH * ADH], np.float32, F32),
    "woT": ([AH * ADH, D], np.float32, F32),
    "wgT": ([D, HFF], np.float32, F32),
    "wuT": ([D, HFF], np.float32, F32),
    "wdT": ([HFF, D], np.float32, F32),
    "convw": ([33, 128, 4], np.float32, F32),
    "convb": ([128, 33], np.float32, F32),
    "dtb": ([64, 1], np.float32, F32),
    "alog": ([64, 1], np.float32, F32),
    "dvec": ([1, 64], np.float32, F32),
    "nw": ([128, 32], np.float32, F32),
    "c_id_bf": ([128, 128], NPBF, BF16),
    "c_id_f": ([128, 128], np.float32, F32),
    "c_h128": ([128, 128], NPBF, BF16),
    "c_hp": ([128, 128], NPBF, BF16),
    "c_triu_f": ([128, 128], np.float32, F32),
    "c_triu_bf": ([128, 128], NPBF, BF16),
    "c_ones_f": ([1, 128], np.float32, F32),
    "c_onescol_bf": ([128, 1], NPBF, BF16),
}
SPILLS = {
    "xn0": ([T, D], NPBF, BF16),
    "gateT": ([INTER, T], NPBF, BF16),
    "xconvT": ([INTER, T], NPBF, BF16),
    "bcT": ([64, 2 * T], NPBF, BF16),
    "csT": ([64, T], np.float32, F32),
    "dtT": ([64, T], np.float32, F32),
    "yfT": ([INTER, T], NPBF, BF16),
    "mham": ([T, D], np.float32, F32),
    "vsp": ([64, 128, T], NPBF, BF16),
    "x1": ([T, D], np.float32, F32),
    "xn1": ([T, D], NPBF, BF16),
    "zT": ([AH, ADH, T], NPBF, BF16),
    "matt": ([T, D], np.float32, F32),
    "hid1": ([64, 128, T], NPBF, BF16),
}
XIN = {"x0": ([T, D], np.float32, F32)}
XOUT = {"out": ([T, D], np.float32, F32)}

STAGE_IO = {
    "s0": (["x0"], ["xn0"]),
    "s1": (["xn0"], ["gateT"]),
    "s2": (["xn0"], ["xconvT", "bcT", "csT", "dtT"]),
    "s3a": (["xconvT", "bcT", "csT", "dtT", "gateT"], ["yfT"]),
    "s3b": (["yfT"], ["mham"]),
    "s4": (["xn0"], ["vsp"]),
    "s5": (["vsp", "x0", "mham"], ["x1"]),
    "s6": (["x1"], ["xn1"]),
    "s7": (["xn1"], ["zT"]),
    "s8": (["zT"], ["matt"]),
    "s9": (["xn1"], ["hid1"]),
    "s10": (["hid1", "x1", "matt"], ["out"]),
}
STAGE_W = {
    "s0": [],
    "s1": ["ipT"],
    "s2": ["ipT", "convw", "convb", "dtb", "alog"],
    "s3a": ["dvec", "c_ones_f", "c_onescol_bf", "c_id_bf", "c_id_f", "c_triu_f"],
    "s3b": ["opT", "nw"],
    "s4": ["w1T", "w3T", "c_h128"],
    "s5": ["w2Tp", "c_hp"],
    "s6": [],
    "s7": ["wkT", "wqT", "wvT", "c_onescol_bf", "c_ones_f", "c_triu_bf"],
    "s8": ["woT"],
    "s9": ["wgT", "wuT"],
    "s10": ["wdT"],
}
STAGE_X = {"s0": ["x0"], "s5": ["x0"]}


def load_xnT(c, src_dram, name):
    xnT = c.sb(name, [128, NT * T], BF16)
    v = 0
    for dt_i in range(NT):
        v = c.dma("sync", xnT[:, dt_i * T:(dt_i + 1) * T],
                  src_dram[:, dt_i * 128:(dt_i + 1) * 128], transpose=True)

    def tile(dt_i, lo, hi):
        return xnT[:, dt_i * T + lo: dt_i * T + hi]
    return tile, v


def emit_ln(c, b, src, dst):
    nbuf = 3
    xts = [c.sb(f"ln_x{i}", [128, D], F32) for i in range(nbuf)]
    scr = c.sb("ln_scr", [128, D], F32)
    xnb = [c.sb(f"ln_o{i}", [128, D], BF16) for i in range(2)]
    st = c.sb("ln_st", [128, 8], F32)
    se_done = [0] * nbuf
    out_done = [0, 0]
    for tt in range(NT):
        bi = tt % nbuf
        oi = tt % 2
        if tt >= nbuf:
            c.wait("sync", "ve", se_done[bi])
        ld = c.dma("sync", xts[bi][:], src[tt * 128:(tt + 1) * 128, :])
        c.wait("scalar", "sd", ld)
        sq = c.opi("scalar", "activation", scr[:], xts[bi][:], AF.Square,
                   accum_out=st[:, 1:2])
        c.wait("vector", "sd", ld)
        c.op("vector", "reduce_sum", st[:, 0:1], xts[bi][:], axis=AX.X)
        c.wait("vector", "se", sq)
        c.op("vector", "tensor_scalar", st[:, 2:3], st[:, 0:1], 1.0 / D, None, ALU.mult)
        c.op("vector", "tensor_scalar", st[:, 3:4], st[:, 1:2], 1.0 / D, None, ALU.mult)
        c.op("vector", "tensor_mul", st[:, 6:7], st[:, 2:3], st[:, 2:3])
        c.op("vector", "tensor_sub", st[:, 3:4], st[:, 3:4], st[:, 6:7])
        vv = c.opi("vector", "tensor_scalar", st[:, 3:4], st[:, 3:4], EPS_LN, None, ALU.add)
        c.wait("scalar", "ve", vv)
        c.op("scalar", "activation", st[:, 4:5], st[:, 3:4], AF.Ln)
        ss = c.opi("scalar", "activation", st[:, 4:5], st[:, 4:5], AF.Exp, scale=-0.5)
        c.wait("vector", "se", ss)
        if tt >= 2:
            c.wait("vector", "sd", out_done[oi])
        se_done[bi] = c.opi("vector", "tensor_scalar", xnb[oi][:], xts[bi][:],
                            st[:, 2:3], st[:, 4:5], ALU.subtract, op1=ALU.mult)
        c.wait("sync", "ve", se_done[bi])
        out_done[oi] = c.dma("sync", dst[tt * 128:(tt + 1) * 128, :], xnb[oi][:])
    c.wait("sync", "sd", c.cnt["sd"])


def emit_colproj(c, b, wname, wcols, dst, act, nchtiles, xn_src="xn0"):
    xnt, xn_rdy = load_xnT(c, b[xn_src], "cp_xnT")
    wbufs = [c.sb(f"cp_w{i}", [128, NT * 128], BF16) for i in range(2)]
    obufs = [c.sb(f"cp_o{i}", [128, 512], BF16) for i in range(4)]
    pss = [c.ps(f"cp_p{i}", [128, 512]) for i in range(4)]
    wsrc = b[wname].rearrange("(dt p) ch -> p dt ch", p=128)
    wld = [0, 0]
    wdone = [0, 0]
    ev_done = [0] * 4
    ob_done = [0] * 4
    c.wait("tensor", "sd", xn_rdy)
    for ct in range(nchtiles):
        wb = ct % 2
        if ct >= 2:
            c.wait("gpsimd", "te", wdone[wb])
        wld[wb] = c.dma("gpsimd",
                        wbufs[wb][:].rearrange("p (dt ch) -> p dt ch", ch=128),
                        wsrc[:, :, wcols + ct * 128: wcols + (ct + 1) * 128])
        c.wait("tensor", "gd", wld[wb])
        for tc in range(4):
            pi = (ct * 4 + tc) % 4
            c.wait("tensor", "se", ev_done[pi])
            for dt_i in range(NT):
                ins = c.op("tensor", "matmul", pss[pi][:],
                           wbufs[wb][:, dt_i * 128:(dt_i + 1) * 128],
                           xnt(dt_i, tc * 512, (tc + 1) * 512),
                           start=(dt_i == 0), stop=(dt_i == NT - 1))
            mm = c.inc(ins, "te")
            if tc == 3:
                wdone[wb] = mm
            c.wait("scalar", "te", mm)
            c.wait("scalar", "sd", ob_done[pi])
            ev_done[pi] = c.opi("scalar", "activation", obufs[pi][:], pss[pi][:], act)
            c.wait("sync", "se", ev_done[pi])
            ob_done[pi] = c.dma("sync",
                                dst[ct * 128:(ct + 1) * 128, tc * 512:(tc + 1) * 512],
                                obufs[pi][:])
    c.wait("sync", "sd", c.cnt["sd"])


def emit_s0(c, b):
    emit_ln(c, b, b["x0"], b["xn0"])


def emit_s6(c, b):
    emit_ln(c, b, b["x1"], b["xn1"])


def emit_s1(c, b):
    emit_colproj(c, b, "ipT", 0, b["gateT"], AF.Silu, 32)


def emit_s2(c, b):
    xnt, xn_rdy = load_xnT(c, b["xn0"], "s2_xnT")
    wbufs = [c.sb(f"s2_w{i}", [128, NT * 128], BF16) for i in range(2)]
    cibufs = [c.sb(f"s2_ci{i}", [128, 4 + T], BF16) for i in range(2)]
    cvf = c.sb("s2_cvf", [128, T], F32)
    cvo = [c.sb(f"s2_cvo{i}", [128, T], BF16) for i in range(2)]
    cw = c.sb("s2_cw", [128, 33 * 4], F32)
    cb = c.sb("s2_cb", [128, 33], F32)
    pss = [c.ps(f"s2_p{i}", [128, 512]) for i in range(4)]
    wsrc = b["ipT"].rearrange("(dt p) ch -> p dt ch", p=128)

    c.dma("sync", cw[:].rearrange("p (k c) -> p k c", c=4),
          b["convw"].rearrange("k p c -> p k c"))
    c.dma("sync", cb[:], b["convb"])
    c.opi("vector", "memset", cibufs[0][:, 0:4], 0)
    z1 = c.opi("vector", "memset", cibufs[1][:, 0:4], 0)

    c.wait("tensor", "sd", xn_rdy)
    wld = [0, 0]
    wdone = [0, 0]
    ev_done = [0] * 4
    ci_done = [0, 0]
    cv_done = [0, 0]
    cvs_done = [0, 0]
    for ct in range(33):
        wb = ct % 2
        if ct >= 2:
            c.wait("gpsimd", "te", wdone[wb])
        wld[wb] = c.dma("gpsimd",
                        wbufs[wb][:].rearrange("p (dt ch) -> p dt ch", ch=128),
                        wsrc[:, :, 4096 + ct * 128: 4096 + (ct + 1) * 128])
        c.wait("tensor", "gd", wld[wb])
        cib = ct % 2
        for tc in range(4):
            pi = tc
            c.wait("tensor", "se", ev_done[pi])
            for dt_i in range(NT):
                ins = c.op("tensor", "matmul", pss[pi][:],
                           wbufs[wb][:, dt_i * 128:(dt_i + 1) * 128],
                           xnt(dt_i, tc * 512, (tc + 1) * 512),
                           start=(dt_i == 0), stop=(dt_i == NT - 1))
            mm = c.inc(ins, "te")
            if tc == 3:
                wdone[wb] = mm
            c.wait("scalar", "te", mm)
            if ct >= 2 and tc == 0:
                c.wait("scalar", "ve", ci_done[cib])
            ev_done[pi] = c.opi("scalar", "activation",
                                cibufs[cib][:, 4 + tc * 512: 4 + (tc + 1) * 512],
                                pss[pi][:], AF.Copy)
        c.wait("vector", "se", ev_done[3])
        if ct >= 2:
            c.wait("vector", "se", cv_done[cib])
        c.op("vector", "tensor_scalar", cvf[:], cibufs[cib][:, 1:1 + T],
             cw[:, 4 * ct: 4 * ct + 1], cb[:, ct:ct + 1], ALU.mult, op1=ALU.add)
        ins = None
        for k in range(1, 4):
            ins = c.op("vector", "scalar_tensor_tensor", cvf[:],
                       cibufs[cib][:, 1 + k:1 + k + T],
                       cw[:, 4 * ct + k: 4 * ct + k + 1], cvf[:], ALU.mult, ALU.add)
        ci_done[cib] = c.inc(ins, "ve")
        co = ct % 2
        c.wait("scalar", "ve", ci_done[cib])
        if ct >= 2:
            c.wait("scalar", "sd", cvs_done[co])
        cv_done[co] = c.opi("scalar", "activation", cvo[co][:], cvf[:], AF.Silu)
        c.wait("sync", "se", cv_done[co])
        if ct < 32:
            cvs_done[co] = c.dma("sync", b["xconvT"][ct * 128:(ct + 1) * 128, :],
                                 cvo[co][:])
        else:
            c.dma("sync", b["bcT"][:, 0:T], cvo[co][0:64, :])
            cvs_done[co] = c.dma("sync", b["bcT"][:, T:2 * T], cvo[co][64:128, :])

    dwb = c.sb("s2_dw", [128, NT * 64], BF16)
    dtraw = c.sb("s2_dtraw", [64, T], F32)
    sc1 = c.sb("s2_sc1", [64, T], F32)
    sc2 = c.sb("s2_sc2", [64, T], F32)
    dts = c.sb("s2_dts", [64, T], F32)
    csb = c.sb("s2_csb", [64, T], F32)
    smal = c.sb("s2_smal", [64, 2], F32)
    dld = c.dma("gpsimd", dwb[:].rearrange("p (dt ch) -> p dt ch", ch=64),
                wsrc[:, :, 8320:8384])
    c.dma("sync", smal[:, 0:1], b["dtb"])
    bld = c.dma("sync", smal[:, 1:2], b["alog"])
    c.wait("scalar", "sd", bld)
    na = c.opi("scalar", "activation", smal[:, 1:2], smal[:, 1:2], AF.Exp)
    c.wait("vector", "se", na)
    nv = c.opi("vector", "tensor_scalar", smal[:, 1:2], smal[:, 1:2], -1.0, None, ALU.mult)
    c.wait("tensor", "gd", dld)
    for tc in range(4):
        c.wait("tensor", "se", ev_done[tc])
        for dt_i in range(NT):
            ins = c.op("tensor", "matmul", pss[tc][0:64, :],
                       dwb[:, dt_i * 64:(dt_i + 1) * 64],
                       xnt(dt_i, tc * 512, (tc + 1) * 512),
                       start=(dt_i == 0), stop=(dt_i == NT - 1))
        mm = c.inc(ins, "te")
        c.wait("vector", "te", mm)
        c.op("vector", "tensor_copy", dtraw[:, tc * 512:(tc + 1) * 512], pss[tc][0:64, :])
    av = c.opi("vector", "tensor_scalar", sc1[:], dtraw[:], smal[:, 0:1], None, ALU.add)
    c.wait("scalar", "ve", av)
    c.op("scalar", "activation", sc2[:], sc1[:], AF.Abs)
    c.op("scalar", "activation", sc2[:], sc2[:], AF.Exp, scale=-1.0)
    ee = c.opi("scalar", "activation", sc2[:], sc2[:], AF.Ln, bias=1.0)
    c.wait("vector", "se", ee)
    c.op("vector", "tensor_scalar", dts[:], sc1[:], 0.0, None, ALU.max)
    dt_ok = c.opi("vector", "tensor_add", dts[:], dts[:], sc2[:])
    c.wait("sync", "ve", dt_ok)
    c.dma("sync", b["dtT"], dts[:])
    c.op("vector", "tensor_scalar", sc1[:], dts[:], smal[:, 1:2], None, ALU.mult)
    ins = None
    for ch in range(NCH):
        ins = c.op("vector", "tensor_tensor_scan", csb[:, ch * Q:(ch + 1) * Q],
                   sc1[:, ch * Q:(ch + 1) * Q], sc1[:, ch * Q:(ch + 1) * Q],
                   0.0, ALU.add, ALU.bypass)
    cs_ok = c.inc(ins, "ve")
    c.wait("sync", "ve", cs_ok)
    c.dma("sync", b["csT"], csb[:])
    c.wait("sync", "sd", c.cnt["sd"])


def emit_s3a(c, b):
    bc = c.sb("sa_bc", [64, 2 * T], BF16)
    csT = c.sb("sa_csT", [64, T], F32)
    dtT = c.sb("sa_dtT", [64, T], F32)
    dexp = c.sb("sa_dexp", [128, 64], F32)
    ones = c.sb("sa_ones", [1, 128], F32)
    onecb = c.sb("sa_onecb", [128, 1], BF16)
    idbf = c.sb("sa_idbf", [128, 128], BF16)
    idf = c.sb("sa_idf", [128, 128], F32)
    triu = c.sb("sa_triu", [128, 128], F32)
    drow = c.sb("sa_drow", [1, 64], F32)
    c.dma("sync", bc[:], b["bcT"])
    c.dma("sync", csT[:], b["csT"])
    c.dma("sync", dtT[:], b["dtT"])
    c.dma("sync", ones[:], b["c_ones_f"])
    c.dma("sync", onecb[:], b["c_onescol_bf"])
    c.dma("sync", idbf[:], b["c_id_bf"])
    c.dma("sync", idf[:], b["c_id_f"])
    c.dma("sync", triu[:], b["c_triu_f"])
    v = c.dma("sync", drow[:], b["dvec"])
    p_bc = c.ps("sa_pbc", [128, 1024])
    p_A = c.ps("sa_pA", [128, 512])
    p_B = c.ps("sa_pB", [128, 512])
    p_st = c.ps("sa_pst", [128, 512])
    p_m = c.ps("sa_pm", [128, 512])
    p_bf = c.ps("sa_pbf", [128, 128], BF16)
    xnat = [c.sb(f"sa_xn{i}", [128, INTER], BF16) for i in range(2)]
    xtil = c.sb("sa_xtil", [128, INTER], BF16)
    xdd = c.sb("sa_xdd", [128, INTER], BF16)
    gbuf = [c.sb(f"sa_g{i}", [128, INTER], BF16) for i in range(2)]
    csn = c.sb("sa_csn", [128, 64], F32)
    dtn = c.sb("sa_dtn", [128, 64], F32)
    csrow = c.sb("sa_csrow", [1, 1024], F32)
    cschk = c.sb("sa_cschk", [64, 128], F32)
    dtchk = c.sb("sa_dtchk", [64, 128], F32)
    csrT = c.sb("sa_csrT", [1, 64], F32)
    ebcT = c.sb("sa_ebcT", [64, 64], F32)
    dltT = c.sb("sa_dltT", [64, 128], F32)
    dln = c.sb("sa_dln", [128, 64], F32)
    bnat = c.sb("sa_bnat", [128, 64], BF16)
    bhat = c.sb("sa_bhat", [128, 512], BF16)
    mtp = c.sb("sa_mtp", [128, 128], BF16)
    rall = c.sb("sa_rall", [128, 1024], F32)
    lall = c.sb("sa_lall", [128, 1024], F32)
    mall = c.sb("sa_mall", [128, 8 * 128], BF16)
    epair = c.sb("sa_ep", [128, 512], F32)
    yt = c.sb("sa_yt", [128, INTER], F32)
    y2 = c.sb("sa_y2", [128, INTER], BF16)
    tei = c.sb("sa_tei", [128, 512], F32)
    s_sb = [c.sb(f"sa_s{i}", [64, HEADS * PD], F32) for i in range(2)]
    stmp = c.sb("sa_stmp", [64, 512], F32)
    sbfg = c.sb("sa_sbfg", [64, 512], BF16)
    rstd = c.sb("sa_rstd", [1, 128], F32)
    rbc = c.sb("sa_rbc", [128, 128], F32)
    yfb = [c.sb(f"sa_yf{i}", [128, INTER], BF16) for i in range(2)]

    c.opi("vector", "memset", s_sb[0][:], 0)
    c.wait("tensor", "sd", v)
    t0_ = c.inc(c.op("tensor", "matmul", p_m[:, 0:64], ones[0:1, :], drow[0:1, :],
                     start=True, stop=True), "te")
    c.wait("vector", "te", t0_)
    pm_free = c.opi("vector", "tensor_copy", dexp[:], p_m[:, 0:64])

    xl = [0, 0]; gl = [0, 0]
    ymm_prev = 0
    ymm_prev_g = 0
    mv_prev = 0
    st1_prev = 0
    yAB_prev = 0
    y2_prev = 0
    xcons = [0, 0]
    gcons = [0, 0]
    crow_cons = 0
    chk_cons = 0
    bn_free = 0
    bc_free = 0
    yAB_free = 0
    st_free = 0
    yst = [0, 0]
    for ch in range(NCH):
        par = ch % 2
        t0, t1 = ch * Q, (ch + 1) * Q
        if ch >= 2:
            c.wait("sync", "te", xcons[par])
            c.wait("sync", "ve", gcons[par])
        xl[par] = c.dma("sync", xnat[par][:], b["xconvT"][:, t0:t1], transpose=True)
        gl[par] = c.dma("sync", gbuf[par][:].rearrange("p (k t) -> p k t", t=Q),
                        b["gateT"].rearrange("(k p) t -> p k t", p=128)[:, :, t0:t1])
        c.wait("vector", "te", chk_cons)
        c.wait("vector", "se", bc_free)
        c.op("vector", "tensor_copy", cschk[:], csT[:, t0:t1])
        cd = c.opi("vector", "tensor_copy", dtchk[:], dtT[:, t0:t1])
        c.wait("tensor", "ve", cd)
        c.wait("tensor", "ve", pm_free)
        c.op("tensor", "transpose", p_m[:, 0:64], cschk[:], idf[0:64, 0:64])
        c.op("tensor", "transpose", p_m[:, 64:128], dtchk[:], idf[0:64, 0:64])
        tr1 = c.inc(c.op("tensor", "transpose", p_m[0:1, 128:192],
                         csT[:, t1 - 1:t1], idf[0:64, 0:64]), "te")
        chk_cons = tr1
        c.wait("vector", "te", tr1)
        c.op("vector", "tensor_copy", csn[:], p_m[:, 0:64])
        c.op("vector", "tensor_copy", dtn[:], p_m[:, 64:128])
        pm_free = c.opi("vector", "tensor_copy", csrT[:], p_m[0:1, 128:192])
        c.wait("tensor", "ve", pm_free)
        tb = c.inc(c.op("tensor", "matmul", p_m[0:64, 192:256], ones[0:1, 0:64],
                        csrT[0:1, :], start=True, stop=True), "te")
        c.wait("scalar", "te", tb)
        c.wait("scalar", "ve", st1_prev)
        eb = c.opi("scalar", "activation", ebcT[:], p_m[0:64, 192:256], AF.Exp)
        dl1 = c.opi("vector", "tensor_scalar", dltT[:], cschk[:],
                    csT[:, t1 - 1:t1], None, ALU.subtract)
        c.wait("scalar", "ve", dl1)
        dl2 = c.opi("scalar", "activation", dltT[:], dltT[:], AF.Exp, scale=-1.0)
        c.wait("vector", "se", dl2)
        dl3 = c.opi("vector", "tensor_mul", dltT[:], dltT[:], dtchk[:])
        c.wait("tensor", "ve", dl3)
        c.wait("tensor", "se", eb)
        tr2 = c.inc(c.op("tensor", "transpose", p_m[:, 256:320], dltT[:], idf[0:64, 0:64]), "te")
        c.wait("vector", "te", tr2)
        pm_free = c.opi("vector", "tensor_copy", dln[:], p_m[:, 256:320])
        c.wait("tensor", "ve", bn_free)
        trb = c.inc(c.op("tensor", "transpose", p_bf[:, 0:64], bc[:, t0:t1],
                         idbf[0:64, 0:64]), "te")
        c.wait("vector", "te", trb)
        bn_free = c.opi("vector", "tensor_copy", bnat[:], p_bf[:, 0:64])
        c.wait("tensor", "ve", pm_free)
        mt1 = c.inc(c.op("tensor", "matmul", p_m[:, 320:448], bc[:, t0:t1],
                         bc[:, T + t0:T + t1], start=True, stop=True), "te")
        c.wait("vector", "te", mt1)
        pm_free = c.opi("vector", "tensor_mul", mtp[:], p_m[:, 320:448], triu[:])
        mt2 = pm_free
        c.wait("vector", "sd", xl[par])
        c.wait("vector", "te", ymm_prev)
        xt1 = c.opi("vector", "tensor_tensor",
                    xtil[:].rearrange("p (h q) -> p h q", q=PD),
                    xnat[par][:].rearrange("p (h q) -> p h q", q=PD),
                    dtn[:].unsqueeze(2).broadcast_to([128, HEADS, PD]), ALU.mult)
        xd1 = c.opi("vector", "tensor_tensor",
                    xdd[:].rearrange("p (h q) -> p h q", q=PD),
                    xnat[par][:].rearrange("p (h q) -> p h q", q=PD),
                    dexp[:, 0:HEADS].unsqueeze(2).broadcast_to([128, HEADS, PD]), ALU.mult)
        sprev = s_sb[ch % 2]
        snew = s_sb[(ch + 1) % 2]
        yt_guard = y2_prev
        for g in range(8):
            h0 = g * 8
            c.wait("sync", "te", crow_cons)
            crl = c.dma("sync", csrow[:].rearrange("o (h t) -> o h t", t=Q),
                        b["csT"][h0:h0 + 8, t0:t1].unsqueeze(0))
            c.wait("tensor", "se", bc_free)
            c.wait("tensor", "sd", crl)
            c.op("tensor", "matmul", p_bc[:, 0:512], ones[0:1, :],
                 csrow[0:1, 0:512], start=True, stop=True)
            bcv = c.inc(c.op("tensor", "matmul", p_bc[:, 512:1024], ones[0:1, :],
                             csrow[0:1, 512:1024],
                             start=True, stop=True), "te")
            crow_cons = bcv
            c.wait("scalar", "te", bcv)
            c.wait("scalar", "ve", yAB_prev)
            ins = None
            for hh in range(8):
                ins = c.op("scalar", "activation", rall[:, hh * Q:(hh + 1) * Q],
                           p_bc[:, hh * Q:(hh + 1) * Q], AF.Relu, scale=-1.0,
                           bias=csn[:, h0 + hh: h0 + hh + 1])
            c.inc(ins, "se")
            c.op("scalar", "activation",
                 epair[0:64, :].rearrange("p (k t) -> p k t", t=Q),
                 p_bc[0:64, :].rearrange("p (hh t) -> p hh t", t=Q)[:, 0:8:2, :],
                 AF.Exp)
            epv = c.inc(c.op("scalar", "activation",
                             epair[64:128, :].rearrange("p (k t) -> p k t", t=Q),
                             p_bc[64:128, :].rearrange("p (hh t) -> p hh t", t=Q)[:, 1:8:2, :],
                             AF.Exp), "se")
            c.wait("scalar", "ve", mv_prev)
            lv = c.opi("scalar", "activation", lall[:], rall[:], AF.Exp, scale=-1.0)
            bc_free = lv
            c.wait("vector", "se", lv)
            c.wait("vector", "ve", mt2)
            c.wait("vector", "te", ymm_prev_g)
            mv = c.opi("vector", "tensor_tensor",
                       mall[:].rearrange("p (hh t) -> p hh t", t=Q),
                       lall[:].rearrange("p (hh t) -> p hh t", t=Q),
                       mtp[:].unsqueeze(1).broadcast_to([128, 8, Q]), ALU.mult)
            c.wait("vector", "te", trb)
            c.wait("vector", "te", ymm_prev_g)
            c.wait("vector", "se", eb)
            bh1 = c.opi("vector", "tensor_tensor",
                        bhat[:].rearrange("p (h n) -> p h n", n=ND),
                        bnat[:].unsqueeze(1).broadcast_to([128, 8, ND]),
                        dln[:, h0:h0 + 8].unsqueeze(2).broadcast_to([128, 8, ND]),
                        ALU.mult)
            st1 = c.opi("vector", "tensor_tensor",
                        stmp[:].rearrange("n (h q) -> n h q", q=PD),
                        sprev[:, h0 * PD:(h0 + 8) * PD].rearrange("n (h q) -> n h q", q=PD),
                        ebcT[:, h0:h0 + 8].unsqueeze(2).broadcast_to([64, 8, PD]),
                        ALU.mult)
            st1 = c.opi("vector", "tensor_copy", sbfg[:],
                        sprev[:, h0 * PD:(h0 + 8) * PD])
            st1_prev = st1
            c.wait("tensor", "ve", mv)
            c.wait("tensor", "ve", max(xt1, xd1, bh1, st1, yAB_free, st_free))
            insA = insB = None
            for kk in range(4):
                for e in range(2):
                    h = h0 + 2 * kk + e
                    tp = (0, 64 * e)
                    rows = slice(64 * e, 64 * e + 64)
                    c.op("tensor", "matmul", p_A[rows, kk * 128:(kk + 1) * 128],
                         xtil[:, h * PD:(h + 1) * PD],
                         mall[:, (2 * kk + e) * Q:(2 * kk + e + 1) * Q],
                         start=True, stop=False, tile_position=tp)
                    insA = c.op("tensor", "matmul", p_A[rows, kk * 128:(kk + 1) * 128],
                                xdd[:, h * PD:(h + 1) * PD], idbf[:],
                                start=False, stop=True, tile_position=tp)
                    insB = c.op("tensor", "matmul", p_B[rows, kk * 128:(kk + 1) * 128],
                                sbfg[:, (2 * kk + e) * PD:(2 * kk + e + 1) * PD],
                                bc[:, T + t0:T + t1],
                                start=True, stop=True, tile_position=tp)
            c.inc(insA, "te")
            yB = c.inc(insB, "te")
            mv_prev = mv
            inss = None
            for hh in range(8):
                h = h0 + hh
                inss = c.op("tensor", "matmul", p_st[0:64, hh * 64:(hh + 1) * 64],
                            bhat[:, hh * ND:(hh + 1) * ND],
                            xnat[par][:, h * PD:(h + 1) * PD],
                            start=True, stop=True)
            yS = c.inc(inss, "te")
            ymm_prev_g = yS
            if g == 7:
                xcons[par] = yS
                ymm_prev = yS
            c.wait("vector", "te", yS)
            c.wait("vector", "se", epv)
            c.wait("vector", "se", yt_guard)
            c.op("vector", "tensor_tensor", tei[:], p_B[:, 0:512],
                 epair[:], ALU.mult)
            yAB_free = c.opi("vector", "tensor_tensor", yt[:, h0 * PD:(h0 + 8) * PD],
                             p_A[:, 0:512], tei[:], ALU.add)
            yAB_prev = yAB_free
            st_free = c.opi("vector", "tensor_add", snew[:, h0 * PD:(h0 + 8) * PD],
                            stmp[:], p_st[0:64, :])
        c.wait("scalar", "ve", yAB_free)
        y2v = c.opi("scalar", "activation", y2[:], yt[:], AF.Square)
        y2_prev = y2v
        c.wait("tensor", "se", y2v)
        c.wait("tensor", "ve", pm_free)
        insv = None
        for kk in range(32):
            insv = c.op("tensor", "matmul", p_m[0:1, 0:128], onecb[:],
                        y2[:, kk * 128:(kk + 1) * 128], start=(kk == 0), stop=(kk == 31))
        vv = c.inc(insv, "te")
        c.wait("vector", "te", vv)
        r1 = c.opi("vector", "tensor_scalar", rstd[:], p_m[0:1, 0:128],
                   1.0 / INTER, EPS_MN, ALU.mult, op1=ALU.add)
        c.wait("scalar", "ve", r1)
        c.op("scalar", "activation", rstd[:], rstd[:], AF.Ln)
        r2 = c.opi("scalar", "activation", rstd[:], rstd[:], AF.Exp, scale=-0.5)
        c.wait("tensor", "se", r2)
        rb = c.inc(c.op("tensor", "matmul", p_m[:, 128:256], ones[0:1, :],
                        rstd[0:1, :], start=True, stop=True), "te")
        c.wait("vector", "te", rb)
        pm_free = c.opi("vector", "tensor_copy", rbc[:], p_m[:, 128:256])
        c.wait("vector", "sd", gl[par])
        if ch >= 2:
            c.wait("vector", "sd", yst[par])
        c.op("vector", "tensor_tensor", yt[:].rearrange("p (k t2) -> p k t2", t2=128),
             yt[:].rearrange("p (k t2) -> p k t2", t2=128),
             rbc[:].unsqueeze(1).broadcast_to([128, 32, 128]), ALU.mult)
        gcons[par] = c.opi("vector", "tensor_mul", yfb[par][:], yt[:], gbuf[par][:])
        c.wait("sync", "ve", gcons[par])
        yst[par] = c.dma("sync",
                         b["yfT"].rearrange("(k p) t -> p k t", p=128)[:, :, t0:t1],
                         yfb[par][:].rearrange("p (k t2) -> p k t2", t2=128))
    c.wait("sync", "sd", c.cnt["sd"])


def emit_s3b(c, b):
    nw = c.sb("sb_nw", [128, 32], F32)
    wsb = c.sb("sb_w", [128, 32 * D], BF16)
    ybufs = [c.sb(f"sb_y{i}", [128, 32 * 128], BF16) for i in range(2)]
    obufs = [c.sb(f"sb_o{i}", [128, D], F32) for i in range(2)]
    pss = [c.ps(f"sb_p{i}", [128, 512]) for i in range(8)]
    v = c.dma("sync", nw[:], b["nw"])
    wld = c.dma("gpsimd", wsb[:].rearrange("p (k d) -> p k d", d=D),
                b["opT"].rearrange("(k p) d -> p k d", p=128))
    c.wait("vector", "sd", v)
    c.wait("vector", "gd", wld)
    ins = None
    for k in range(32):
        ins = c.op("vector", "tensor_scalar", wsb[:, k * D:(k + 1) * D],
                   wsb[:, k * D:(k + 1) * D], nw[:, k:k + 1], None, ALU.mult)
    wok = c.inc(ins, "ve")
    c.wait("tensor", "ve", wok)
    yld = [0, 0]
    ycons = [0, 0]
    ev = [0] * 8
    ost = [0, 0]
    for tt in range(NT):
        par = tt % 2
        if tt >= 2:
            c.wait("sync", "te", ycons[par])
        yld[par] = c.dma("sync", ybufs[par][:].rearrange("p (k t) -> p k t", t=128),
                         b["yfT"].rearrange("(k p) t -> p k t", p=128)[:, :, tt * 128:(tt + 1) * 128])
        c.wait("tensor", "sd", yld[par])
        for dc in range(4):
            pi = par * 4 + dc
            c.wait("tensor", "ve", ev[pi])
            ins = None
            for k in range(32):
                ins = c.op("tensor", "matmul", pss[pi][:],
                           ybufs[par][:, k * 128:(k + 1) * 128],
                           wsb[:, k * D + dc * 512: k * D + (dc + 1) * 512],
                           start=(k == 0), stop=(k == 31))
            mm = c.inc(ins, "te")
            if dc == 3:
                ycons[par] = mm
            c.wait("vector", "te", mm)
            if dc == 0 and tt >= 2:
                c.wait("vector", "sd", ost[par])
            ev[pi] = c.opi("vector", "tensor_copy",
                           obufs[par][:, dc * 512:(dc + 1) * 512], pss[pi][:])
        c.wait("sync", "ve", ev[par * 4 + 3])
        ost[par] = c.dma("sync", b["mham"][tt * 128:(tt + 1) * 128, :], obufs[par][:])
    c.wait("sync", "sd", c.cnt["sd"])


def emit_dualproj(c, b, wa, wb_, dst, fwht, xn_src):
    xnt, xn_rdy = load_xnT(c, b[xn_src], "dp_xnT")
    h128 = c.sb("dp_h128", [128, 128], BF16)
    if fwht:
        c.dma("sync", h128[:], b["c_h128"])
    wbufs = [c.sb(f"dp_w{i}", [128, NT * 128], BF16) for i in range(4)]
    sbuf = c.sb("dp_s", [128, 512], F32)
    hbuf = c.sb("dp_h", [128, 512], BF16)
    obufs = [c.sb(f"dp_o{i}", [128, 512], BF16) for i in range(4)]
    psu = [c.ps(f"dp_pu{i}", [128, 512]) for i in range(2)]
    psv = [c.ps(f"dp_pv{i}", [128, 512]) for i in range(2)]
    psw = [c.ps(f"dp_pw{i}", [128, 512]) for i in range(2)]
    wsrca = b[wa].rearrange("(dt p) hh -> p dt hh", p=128)
    wsrcb = b[wb_].rearrange("(dt p) hh -> p dt hh", p=128)
    c.wait("tensor", "sd", xn_rdy)
    wld = [0, 0]
    wdone = [0, 0]
    sil_done = [0, 0]
    mul_done = [0, 0]
    st1_done = [0, 0]
    last_mul = 0
    last_st1 = 0
    ost = [0] * 4
    for a in range(64):
        par = a % 2
        if a >= 2:
            c.wait("gpsimd", "te", wdone[par])
        c.dma("gpsimd", wbufs[par][:].rearrange("p (dt hh) -> p dt hh", hh=128),
              wsrca[:, :, a * 128:(a + 1) * 128])
        wld[par] = c.dma("gpsimd", wbufs[2 + par][:].rearrange("p (dt hh) -> p dt hh", hh=128),
                         wsrcb[:, :, a * 128:(a + 1) * 128])
        c.wait("tensor", "gd", wld[par])
        for tc in range(4):
            oi = (a * 4 + tc) % 4
            c.wait("tensor", "se", sil_done[par])
            ins = None
            for dt_i in range(NT):
                ins = c.op("tensor", "matmul", psu[par][:],
                           wbufs[par][:, dt_i * 128:(dt_i + 1) * 128],
                           xnt(dt_i, tc * 512, (tc + 1) * 512),
                           start=(dt_i == 0), stop=(dt_i == NT - 1))
            mmu = c.inc(ins, "te")
            c.wait("tensor", "ve", mul_done[par])
            for dt_i in range(NT):
                ins = c.op("tensor", "matmul", psv[par][:],
                           wbufs[2 + par][:, dt_i * 128:(dt_i + 1) * 128],
                           xnt(dt_i, tc * 512, (tc + 1) * 512),
                           start=(dt_i == 0), stop=(dt_i == NT - 1))
            mmv = c.inc(ins, "te")
            if tc == 3:
                wdone[par] = mmv
            c.wait("scalar", "te", mmu)
            c.wait("scalar", "ve", last_mul)
            sil_done[par] = c.opi("scalar", "activation", sbuf[:], psu[par][:], AF.Silu)
            c.wait("vector", "se", sil_done[par])
            c.wait("vector", "te", mmv)
            if fwht:
                c.wait("vector", "te", last_st1)
                mul_done[par] = c.opi("vector", "tensor_mul", hbuf[:], sbuf[:], psv[par][:])
                last_mul = mul_done[par]
                c.wait("tensor", "ve", mul_done[par])
                c.wait("tensor", "ve", st1_done[par])
                mmw = c.inc(c.op("tensor", "matmul", psw[par][:], h128[:], hbuf[:],
                                 start=True, stop=True), "te")
                last_st1 = mmw
                c.wait("vector", "te", mmw)
                c.wait("vector", "sd", ost[oi])
                st1_done[par] = c.opi("vector", "tensor_copy", obufs[oi][:], psw[par][:])
                dv = st1_done[par]
            else:
                c.wait("vector", "sd", ost[oi])
                mul_done[par] = c.opi("vector", "tensor_mul", obufs[oi][:], sbuf[:], psv[par][:])
                last_mul = mul_done[par]
                dv = mul_done[par]
            c.wait("sync", "ve", dv)
            ost[oi] = c.dma("sync", dst[a, :, tc * 512:(tc + 1) * 512], obufs[oi][:])
    c.wait("sync", "sd", c.cnt["sd"])


def emit_s4(c, b):
    emit_dualproj(c, b, "w1T", "w3T", b["vsp"], True, "xn0")


def emit_s9(c, b):
    emit_dualproj(c, b, "wgT", "wuT", b["hid1"], False, "xn1")


def emit_s5(c, b):
    hp = c.sb("s5_hp", [128, 128], BF16)
    v0 = c.dma("sync", hp[:], b["c_hp"])
    w2 = c.sb("s5_w2", [128, 64 * 1024], BF16)
    vt = [c.sb(f"s5_vt{i}", [128, 64 * 128], BF16) for i in range(2)]
    s2 = c.sb("s5_s2", [128, 64 * 128], BF16)
    xb = [c.sb(f"s5_x{i}", [128, 1024], F32) for i in range(2)]
    mb = [c.sb(f"s5_m{i}", [128, 1024], F32) for i in range(2)]
    ob = [c.sb(f"s5_o{i}", [128, 1024], F32) for i in range(2)]
    ps_s2 = [c.ps(f"s5_ps{i}", [128, 512]) for i in range(2)]
    ps_o = [c.ps(f"s5_po{i}", [128, 512]) for i in range(4)]
    vcons = [0, 0]
    s2done = [0, 0]
    evd = [0] * 4
    ostd = [0, 0]
    s2cons = 0
    for half in range(2):
        d0 = half * 1024
        if half > 0:
            c.wait("gpsimd", "te", c.cnt["te"])
        wld = c.dma("gpsimd", w2[:].rearrange("p (k d) -> p k d", d=1024),
                    b["w2Tp"][:, :, d0:d0 + 1024].rearrange("k p d -> p k d"))
        c.wait("tensor", "gd", wld)
        c.wait("tensor", "sd", v0)
        for tt in range(NT):
            par = tt % 2
            t0 = tt * 128
            if half + tt >= 2:
                c.wait("sync", "te", vcons[par])
            c.dma("sync", vt[par][0:64, :].rearrange("p (k t) -> p k t", t=128),
                  b["vsp"][:, 0:128:2, t0:t0 + 128])
            vld = c.dma("sync", vt[par][64:128, :].rearrange("p (k t) -> p k t", t=128),
                        b["vsp"][:, 1:128:2, t0:t0 + 128])
            c.wait("tensor", "sd", vld)
            c.wait("vector", "te", s2cons)
            for bp in range(64):
                pi = bp % 2
                c.wait("tensor", "ve", s2done[pi])
                mm = c.inc(c.op("tensor", "matmul", ps_s2[pi][:, 0:128], hp[:],
                                vt[par][:, bp * 128:(bp + 1) * 128],
                                start=True, stop=True), "te")
                if bp == 63:
                    vcons[par] = mm
                c.wait("vector", "te", mm)
                s2done[pi] = c.opi("vector", "tensor_copy",
                                   s2[:, bp * 128:(bp + 1) * 128], ps_s2[pi][:, 0:128])
            s2v = max(s2done)
            c.wait("tensor", "ve", s2v)
            for dc in range(2):
                pi = par * 2 + dc
                c.wait("tensor", "ve", evd[pi])
                ins = None
                for bp in range(64):
                    ins = c.op("tensor", "matmul", ps_o[pi][:],
                               s2[:, bp * 128:(bp + 1) * 128],
                               w2[:, bp * 1024 + dc * 512: bp * 1024 + (dc + 1) * 512],
                               start=(bp == 0), stop=(bp == 63))
                mmo = c.inc(ins, "te")
                if dc == 1:
                    s2cons = mmo
                c.wait("sync", "ve", evd[pi])
                xl = c.dma("sync", xb[par][:, dc * 512:(dc + 1) * 512],
                           b["x0"][t0:t0 + 128, d0 + dc * 512: d0 + (dc + 1) * 512])
                ml = c.dma("sync", mb[par][:, dc * 512:(dc + 1) * 512],
                           b["mham"][t0:t0 + 128, d0 + dc * 512: d0 + (dc + 1) * 512])
                c.wait("vector", "te", mmo)
                c.wait("vector", "sd", ml)
                if half + tt >= 2 and dc == 0:
                    c.wait("vector", "sd", ostd[par])
                c.op("vector", "tensor_add", ob[par][:, dc * 512:(dc + 1) * 512],
                     xb[par][:, dc * 512:(dc + 1) * 512],
                     mb[par][:, dc * 512:(dc + 1) * 512])
                evd[pi] = c.opi("vector", "tensor_add",
                                ob[par][:, dc * 512:(dc + 1) * 512],
                                ob[par][:, dc * 512:(dc + 1) * 512], ps_o[pi][:])
            c.wait("sync", "ve", evd[par * 2 + 1])
            ostd[par] = c.dma("sync", b["x1"][t0:t0 + 128, d0:d0 + 1024], ob[par][:])
    c.wait("sync", "sd", c.cnt["sd"])


def emit_s7(c, b):
    xnt, xn_rdy = load_xnT(c, b["xn1"], "s7_xnT")
    onescol = c.sb("s7_oc", [128, 1], BF16)
    onesf = c.sb("s7_on", [1, 128], F32)
    triub = c.sb("s7_tr", [128, 128], BF16)
    c.dma("sync", onescol[:], b["c_onescol_bf"])
    c.dma("sync", onesf[:], b["c_ones_f"])
    v0 = c.dma("sync", triub[:], b["c_triu_bf"])
    wk = [c.sb(f"s7_wk{i}", [128, NT * 128], BF16) for i in range(2)]
    wq = [c.sb(f"s7_wq{i}", [128, NT * 128], BF16) for i in range(2)]
    wv = [c.sb(f"s7_wv{i}", [128, NT * 128], BF16) for i in range(2)]
    kT = c.sb("s7_kT", [128, T], BF16)
    qT = c.sb("s7_qT", [128, T], BF16)
    vn = c.sb("s7_vn", [128, NT * 128], BF16)
    pb = c.sb("s7_pb", [128, 8 * 128], BF16)
    rn = c.sb("s7_rn", [1, 128], F32)
    rnb = c.sb("s7_rnb", [128, 128], F32)
    zb = [c.sb(f"s7_z{i}", [128, T], BF16) for i in range(2)]
    ps_s = [c.ps(f"s7_ps{i}", [128, 1024]) for i in range(2)]
    ps_z = c.ps("s7_pz", [128, 512])
    ps_d = c.ps("s7_pd", [128, 512])
    ps_q = [c.ps(f"s7_pq{i}", [128, 512]) for i in range(2)]
    wks = b["wkT"].rearrange("(dt p) f -> p dt f", p=128)
    wqs = b["wqT"].rearrange("(dt p) f -> p dt f", p=128)
    wvs = b["wvT"].rearrange("(dt p) f -> p dt f", p=128)
    c.wait("tensor", "sd", max(xn_rdy, v0))
    wld = [0, 0]
    wcons = [0, 0]
    kq_ev = 0
    st_free = [0, 0]
    pb_free = 0
    z_free = 0
    d_free = 0
    zst = [0, 0]
    z_ev = [0, 0]
    for h in range(AH):
        par = h % 2
        if h >= 2:
            c.wait("gpsimd", "te", wcons[par])
        c.dma("gpsimd", wk[par][:].rearrange("p (dt f) -> p dt f", f=128),
              wks[:, :, h * 128:(h + 1) * 128])
        c.dma("gpsimd", wq[par][:].rearrange("p (dt f) -> p dt f", f=128),
              wqs[:, :, h * 128:(h + 1) * 128])
        wld[par] = c.dma("gpsimd", wv[par][:].rearrange("p (dt f) -> p dt f", f=128),
                         wvs[:, :, h * 128:(h + 1) * 128])
        c.wait("tensor", "gd", wld[par])
        for wbuf, dstb in ((wk, kT), (wq, qT)):
            for tc in range(4):
                pi = tc % 2
                c.wait("tensor", "ve", kq_ev)
                ins = None
                for dt_i in range(NT):
                    ins = c.op("tensor", "matmul", ps_q[pi][:],
                               wbuf[par][:, dt_i * 128:(dt_i + 1) * 128],
                               xnt(dt_i, tc * 512, (tc + 1) * 512),
                               start=(dt_i == 0), stop=(dt_i == NT - 1))
                mm = c.inc(ins, "te")
                c.wait("vector", "te", mm)
                kq_ev = c.opi("vector", "tensor_copy",
                              dstb[:, tc * 512:(tc + 1) * 512], ps_q[pi][:])
        for tt2 in range(NT):
            pi = tt2 % 2
            c.wait("tensor", "ve", kq_ev)
            ins = None
            for dt_i in range(NT):
                ins = c.op("tensor", "matmul", ps_q[pi][0:128, 0:128],
                           xnt(dt_i, tt2 * 128, (tt2 + 1) * 128),
                           wv[par][:, dt_i * 128:(dt_i + 1) * 128],
                           start=(dt_i == 0), stop=(dt_i == NT - 1))
            mm = c.inc(ins, "te")
            c.wait("vector", "te", mm)
            kq_ev = c.opi("vector", "tensor_copy",
                          vn[:, tt2 * 128:(tt2 + 1) * 128], ps_q[pi][0:128, 0:128])
        wcons[par] = c.cnt["te"]
        c.wait("tensor", "ve", kq_ev)
        zo = h % 2
        if h >= 2:
            c.wait("vector", "sd", zst[zo])
        for qt in range(NT):
            nblk = (qt + 8) // 8
            for blk in range(nblk):
                j0 = blk * 8
                nj = min(8, qt + 1 - j0)
                sb_i = blk % 2
                c.wait("tensor", "se", st_free[sb_i])
                ins = None
                for jj in range(nj):
                    ins = c.op("tensor", "matmul", ps_s[sb_i][:, jj * 128:(jj + 1) * 128],
                               kT[:, (j0 + jj) * 128:(j0 + jj + 1) * 128],
                               qT[:, qt * 128:(qt + 1) * 128], start=True, stop=True)
                sv = c.inc(ins, "te")
                c.wait("scalar", "te", max(sv, pb_free))
                ex = c.opi("scalar", "activation", pb[:, 0:nj * 128],
                           ps_s[sb_i][:, 0:nj * 128], AF.Exp, scale=ASCALE)
                st_free[sb_i] = ex
                if j0 + nj - 1 == qt:
                    c.wait("vector", "se", ex)
                    exv = c.opi("vector", "tensor_mul", pb[:, (nj - 1) * 128:nj * 128],
                                pb[:, (nj - 1) * 128:nj * 128], triub[:])
                    c.wait("tensor", "ve", exv)
                else:
                    c.wait("tensor", "se", ex)
                if blk == 0:
                    c.wait("tensor", "ve", max(z_free, d_free))
                insd = None
                insz = None
                for jj in range(nj):
                    last = (blk == nblk - 1 and jj == nj - 1)
                    insz = c.op("tensor", "matmul", ps_z[:, 0:128],
                                vn[:, (j0 + jj) * 128:(j0 + jj + 1) * 128],
                                pb[:, jj * 128:(jj + 1) * 128],
                                start=(blk == 0 and jj == 0), stop=last)
                    insd = c.op("tensor", "matmul", ps_d[0:1, 0:128], onescol[:],
                                pb[:, jj * 128:(jj + 1) * 128],
                                start=(blk == 0 and jj == 0), stop=last)
                pb_free = c.inc(insd, "te")
                zdone = c.inc(insz, "te")
            c.wait("vector", "te", pb_free)
            rv = c.opi("vector", "reciprocal", rn[:], ps_d[0:1, 0:128])
            d_free = rv
            c.wait("tensor", "ve", rv)
            rb = c.inc(c.op("tensor", "matmul", ps_d[:, 128:256], onesf[0:1, :],
                            rn[0:1, :], start=True, stop=True), "te")
            c.wait("vector", "te", max(rb, zdone))
            c.op("vector", "tensor_copy", rnb[:], ps_d[:, 128:256])
            zev = c.opi("vector", "tensor_tensor", zb[zo][:, qt * 128:(qt + 1) * 128],
                        ps_z[:, 0:128], rnb[:], ALU.mult)
            z_free = zev
            d_free = zev
        z_ev[zo] = z_free
        c.wait("sync", "ve", z_ev[zo])
        zst[zo] = c.dma("sync", b["zT"][h], zb[zo][:])
    c.wait("sync", "sd", c.cnt["sd"])


def emit_s8(c, b):
    wsb = c.sb("s8_w", [128, AH * D], BF16)
    zbufs = [c.sb(f"s8_z{i}", [128, AH * 128], BF16) for i in range(2)]
    obufs = [c.sb(f"s8_o{i}", [128, D], F32) for i in range(2)]
    pss = [c.ps(f"s8_p{i}", [128, 512]) for i in range(8)]
    wld = c.dma("gpsimd", wsb[:].rearrange("p (k d) -> p k d", d=D),
                b["woT"].rearrange("(k p) d -> p k d", p=128))
    c.wait("tensor", "gd", wld)
    zld = [0, 0]
    zcons = [0, 0]
    ev = [0] * 8
    ost = [0, 0]
    for tt in range(NT):
        par = tt % 2
        if tt >= 2:
            c.wait("sync", "te", zcons[par])
        zld[par] = c.dma("sync", zbufs[par][:].rearrange("p (k t) -> p k t", t=128),
                         b["zT"][:, :, tt * 128:(tt + 1) * 128].rearrange("k p t -> p k t"))
        c.wait("tensor", "sd", zld[par])
        for dc in range(4):
            pi = par * 4 + dc
            c.wait("tensor", "ve", ev[pi])
            ins = None
            for k in range(AH):
                ins = c.op("tensor", "matmul", pss[pi][:],
                           zbufs[par][:, k * 128:(k + 1) * 128],
                           wsb[:, k * D + dc * 512: k * D + (dc + 1) * 512],
                           start=(k == 0), stop=(k == AH - 1))
            mm = c.inc(ins, "te")
            if dc == 3:
                zcons[par] = mm
            c.wait("vector", "te", mm)
            if dc == 0 and tt >= 2:
                c.wait("vector", "sd", ost[par])
            ev[pi] = c.opi("vector", "tensor_copy",
                           obufs[par][:, dc * 512:(dc + 1) * 512], pss[pi][:])
        c.wait("sync", "ve", ev[par * 4 + 3])
        ost[par] = c.dma("sync", b["matt"][tt * 128:(tt + 1) * 128, :], obufs[par][:])
    c.wait("sync", "sd", c.cnt["sd"])


def emit_s10(c, b):
    wd = c.sb("s10_wd", [128, 64 * 1024], BF16)
    hb = [c.sb(f"s10_h{i}", [128, 64 * 128], BF16) for i in range(2)]
    xb = [c.sb(f"s10_x{i}", [128, 1024], F32) for i in range(2)]
    mb = [c.sb(f"s10_m{i}", [128, 1024], F32) for i in range(2)]
    ob = [c.sb(f"s10_o{i}", [128, 1024], F32) for i in range(2)]
    ps_o = [c.ps(f"s10_p{i}", [128, 512]) for i in range(4)]
    hcons = [0, 0]
    evd = [0] * 4
    ostd = [0, 0]
    for half in range(2):
        d0 = half * 1024
        if half > 0:
            c.wait("gpsimd", "te", c.cnt["te"])
        wld = c.dma("gpsimd", wd[:].rearrange("p (k d) -> p k d", d=1024),
                    b["wdT"][:, d0:d0 + 1024].rearrange("(k p) d -> p k d", p=128))
        c.wait("tensor", "gd", wld)
        for tt in range(NT):
            par = tt % 2
            t0 = tt * 128
            if half + tt >= 2:
                c.wait("sync", "te", hcons[par])
            hld = c.dma("sync", hb[par][:].rearrange("p (k t) -> p k t", t=128),
                        b["hid1"][:, :, t0:t0 + 128].rearrange("k p t -> p k t"))
            c.wait("sync", "ve", max(evd[par * 2], evd[par * 2 + 1]))
            c.dma("sync", xb[par][:], b["x1"][t0:t0 + 128, d0:d0 + 1024])
            ml = c.dma("sync", mb[par][:], b["matt"][t0:t0 + 128, d0:d0 + 1024])
            c.wait("tensor", "sd", hld)
            for dc in range(2):
                pi = par * 2 + dc
                c.wait("tensor", "ve", evd[pi])
                ins = None
                for k in range(64):
                    ins = c.op("tensor", "matmul", ps_o[pi][:],
                               hb[par][:, k * 128:(k + 1) * 128],
                               wd[:, k * 1024 + dc * 512: k * 1024 + (dc + 1) * 512],
                               start=(k == 0), stop=(k == 63))
                mmo = c.inc(ins, "te")
                if dc == 1:
                    hcons[par] = mmo
                c.wait("vector", "te", mmo)
                c.wait("vector", "sd", ml)
                if half + tt >= 2 and dc == 0:
                    c.wait("vector", "sd", ostd[par])
                c.op("vector", "tensor_add", ob[par][:, dc * 512:(dc + 1) * 512],
                     xb[par][:, dc * 512:(dc + 1) * 512],
                     mb[par][:, dc * 512:(dc + 1) * 512])
                evd[pi] = c.opi("vector", "tensor_add",
                                ob[par][:, dc * 512:(dc + 1) * 512],
                                ob[par][:, dc * 512:(dc + 1) * 512], ps_o[pi][:])
            c.wait("sync", "ve", evd[par * 2 + 1])
            ostd[par] = c.dma("sync", b["out"][t0:t0 + 128, d0:d0 + 1024], ob[par][:])
    c.wait("sync", "sd", c.cnt["sd"])
    c.wait("gpsimd", "sd", c.cnt["sd"])


EMITTERS = {
    "s0": emit_s0, "s1": emit_s1, "s2": emit_s2, "s3a": emit_s3a, "s3b": emit_s3b,
    "s4": emit_s4, "s5": emit_s5, "s6": emit_s6, "s7": emit_s7, "s8": emit_s8,
    "s9": emit_s9, "s10": emit_s10,
}


def build_program(stages):
    nc = bass.Bass("TRN2", target_bir_lowering=False)
    need_in = set()
    produced = set()
    wneed = set()
    for s in stages:
        ins, outs = STAGE_IO[s]
        for n in ins:
            if n not in produced and n != "x0":
                need_in.add(n)
        produced.update(outs)
        wneed.update(STAGE_W[s])
        if s in STAGE_X:
            wneed.update(STAGE_X[s])
    fused = len(stages) == len(STAGE_IO)
    b = {}
    for name in sorted(wneed):
        if name == "x0":
            b[name] = nc.dram_tensor(name, XIN["x0"][0], XIN["x0"][2],
                                     kind="ExternalInput").ap()
        else:
            shape, _, bdt = WEIGHTS[name]
            b[name] = nc.dram_tensor(name, shape, bdt, kind="ExternalInput").ap()
    spill_out = set()
    for name in sorted(set(SPILLS)):
        shape, _, bdt = SPILLS[name]
        if name in need_in:
            b[name] = nc.dram_tensor(name, shape, bdt, kind="ExternalInput").ap()
        elif name in produced:
            if fused:
                b[name] = nc.dram_tensor(name, shape, bdt, kind="Internal").ap()
            else:
                b[name] = nc.dram_tensor(name, shape, bdt, kind="ExternalOutput").ap()
                spill_out.add(name)
    has_out = any("out" in STAGE_IO[s][1] for s in stages)
    if has_out:
        b["out"] = nc.dram_tensor("out", XOUT["out"][0], XOUT["out"][2],
                                  kind="ExternalOutput").ap()
    with ExitStack() as es:
        c = Ctx(nc, es)
        for si, s in enumerate(stages):
            with ExitStack() as stage_es:
                c.es = stage_es
                if si > 0:
                    c.barrier()
                EMITTERS[s](c, b)
                c.seal()
            c.es = es
    innames = sorted(need_in | (wneed & set(WEIGHTS)) | ({"x0"} if "x0" in b else set()))
    outnames = sorted(spill_out | ({"out"} if has_out else set()))
    return nc, innames, outnames


def hadamard(n):
    m = np.ones((1, 1), np.float32)
    while m.shape[0] < n:
        m = np.block([[m, m], [m, -m]])
    return m


def prep_host(inputs):
    g = {}
    g["ipT"] = np.ascontiguousarray(inputs["L0_attn_in_proj_w"].T)
    g["opT"] = np.ascontiguousarray(inputs["L0_attn_out_proj_w"].T)
    g["w1T"] = np.ascontiguousarray(inputs["L0_ffn_W1"].T)
    g["w3T"] = np.ascontiguousarray(inputs["L0_ffn_W3"].T)
    w2v = np.ascontiguousarray(inputs["L0_ffn_W2"].T).reshape(64, 128, D)
    w2p = np.empty((64, 128, D), np.float32)
    q = np.arange(128)
    for p in range(64):
        w2p[p, q[q < 64]] = w2v[q[q < 64], 2 * p]
        w2p[p, q[q >= 64]] = w2v[q[q >= 64] - 64, 2 * p + 1]
    g["w2Tp"] = w2p
    g["wkT"] = np.ascontiguousarray(np.transpose(inputs["L1_attn_W_K"], (2, 0, 1)).reshape(D, AH * ADH))
    g["wqT"] = np.ascontiguousarray(np.transpose(inputs["L1_attn_W_Q"], (2, 0, 1)).reshape(D, AH * ADH))
    g["wvT"] = np.ascontiguousarray(np.transpose(inputs["L1_attn_W_V"], (2, 0, 1)).reshape(D, AH * ADH))
    g["woT"] = np.ascontiguousarray(inputs["L1_attn_W_O"].T)
    g["wgT"] = np.ascontiguousarray(inputs["L1_ffn_w_gate"].T)
    g["wuT"] = np.ascontiguousarray(inputs["L1_ffn_w_up"].T)
    g["wdT"] = np.ascontiguousarray(inputs["L1_ffn_w_down"].T)
    g["convw"] = np.ascontiguousarray(inputs["L0_attn_conv_w"].reshape(33, 128, 4))
    g["convb"] = np.ascontiguousarray(inputs["L0_attn_conv_b"].reshape(33, 128).T)
    g["dtb"] = np.ascontiguousarray(inputs["L0_attn_dt_bias"].reshape(64, 1))
    g["alog"] = np.ascontiguousarray(inputs["L0_attn_A_log"].reshape(64, 1))
    g["dvec"] = np.ascontiguousarray(inputs["L0_attn_D"].reshape(1, 64))
    g["nw"] = np.ascontiguousarray(inputs["L0_attn_norm_weight"].reshape(32, 128).T)
    g["c_id_bf"] = np.eye(128, dtype=NPBF)
    g["c_id_f"] = np.eye(128, dtype=np.float32)
    g["c_h128"] = hadamard(128).astype(NPBF)
    h64 = hadamard(64)
    hp = np.zeros((128, 128), np.float32)
    hp[0:64, 0:64] = h64
    hp[64:128, 64:128] = h64
    g["c_hp"] = (hp / np.sqrt(np.float32(HFF))).astype(NPBF)
    tri = np.triu(np.ones((128, 128), np.float32))
    g["c_triu_f"] = tri
    g["c_triu_bf"] = tri.astype(NPBF)
    g["c_ones_f"] = np.ones((1, 128), np.float32)
    g["c_onescol_bf"] = np.ones((128, 1), NPBF)
    for k in g:
        want = WEIGHTS[k][1]
        if g[k].dtype != want:
            g[k] = np.ascontiguousarray(g[k].astype(want))
    return g


GROUPS = [["s0"], ["s1"], ["s2"], ["s3a"], ["s3b"], ["s4"], ["s5"], ["s6"],
          ["s7"], ["s8"], ["s9"], ["s10"]]


def kernel(**inputs):
    x0 = np.ascontiguousarray(inputs["L0_attn_hidden_states"].astype(np.float32))
    ncores = 8
    g = prep_host(inputs)
    state = [dict() for _ in range(ncores)]
    for ci in range(ncores):
        state[ci]["x0"] = np.ascontiguousarray(x0[ci])
    out = np.zeros((ncores, T, D), np.float32)
    for grp in GROUPS:
        nc, innames, outnames = build_program(grp)
        in_maps = []
        for ci in range(ncores):
            m = {}
            for n in innames:
                m[n] = g[n] if n in g else state[ci][n]
            in_maps.append(m)
        res = run_bass_kernel_spmd(nc, in_maps, core_ids=list(range(ncores)))
        for ci in range(ncores):
            for n in outnames:
                state[ci][n] = res.results[ci][n]
    for ci in range(ncores):
        out[ci] = np.asarray(state[ci]["out"], np.float32)
    return out


# version 02
# speedup vs baseline: 1.1700x; 1.1700x over previous
import numpy as np
import ml_dtypes
from contextlib import ExitStack

import concourse.bass as bass
import concourse.mybir as mybir
from concourse.bass_utils import run_bass_kernel_spmd

F32 = mybir.dt.float32
BF16 = mybir.dt.bfloat16
AF = mybir.ActivationFunctionType
ALU = mybir.AluOpType
AX = mybir.AxisListType
NPBF = ml_dtypes.bfloat16

T = 2048
D = 2048
NT = 16
INTER = 4096
HEADS = 64
PD = 64
ND = 64
Q = 128
NCH = 16
HFF = 8192
AH = 16
ADH = 128
EPS_LN = 1e-5
EPS_MN = 1e-6
ASCALE = float(1.0 / np.sqrt(np.float32(ADH)))

ENGS = ("tensor", "vector", "scalar", "gpsimd", "sync")
QSEM = {"sync": "sd", "scalar": "ad", "gpsimd": "gd"}
ESEM = {"tensor": "te", "vector": "ve", "scalar": "se", "gpsimd": "ge"}


class Ctx:
    def __init__(self, nc, es):
        self.nc = nc
        self.es = es
        self.sems = {}
        self.cnt = {}
        self.last = {}
        self.lastinc = {}
        self.prefix = ""
        for name in ("sd", "ad", "gd", "te", "ve", "se", "ge"):
            self.sems[name] = es.enter_context(nc.semaphore(name))
            self.cnt[name] = 0

    def sb(self, name, shape, dtype):
        return self.es.enter_context(self.nc.sbuf_tensor(self.prefix + name, shape, dtype))

    def ps(self, name, shape, dtype=F32):
        return self.es.enter_context(self.nc.psum_tensor(self.prefix + name, shape, dtype))

    def eng(self, name):
        return getattr(self.nc, name)

    def inc(self, instr, sem, amt=1):
        instr.then_inc(self.sems[sem], amt)
        self.cnt[sem] += amt
        for e, s in ESEM.items():
            if s == sem and instr is self.last.get(e):
                self.lastinc[e] = True
        return self.cnt[sem]

    def wait(self, engine, sem, val):
        if val > 0:
            self.eng(engine).wait_ge(self.sems[sem], val)

    def dma(self, q, out, in_, inc=True, **kw):
        instr = self.eng(q).dma_start(out=out, in_=in_, **kw)
        self.last[q] = instr
        if inc:
            return self.inc(instr, QSEM[q], 16)
        return self.cnt[QSEM[q]]

    def op(self, engine, fname, *args, **kw):
        instr = getattr(self.eng(engine), fname)(*args, **kw)
        self.last[engine] = instr
        self.lastinc[engine] = False
        return instr

    def opi(self, engine, fname, *args, **kw):
        instr = self.op(engine, fname, *args, **kw)
        return self.inc(instr, ESEM[engine])

    def barrier(self):
        targets = [(s, self.cnt[s]) for s in ("sd", "ad", "gd", "te", "ve", "se", "ge")
                   if self.cnt[s] > 0]
        for e in ENGS:
            own = ESEM.get(e)
            for s, v in targets:
                if s == own:
                    continue
                self.eng(e).wait_ge(self.sems[s], v)

    def seal(self):
        for e in ("tensor", "vector", "scalar", "gpsimd"):
            ins = self.last.get(e)
            if ins is not None and not self.lastinc.get(e, True):
                self.inc(ins, ESEM[e])
            self.last[e] = None


WEIGHTS = {
    "ipT": ([D, 8384], np.float32, F32),
    "opT": ([INTER, D], np.float32, F32),
    "w1T": ([D, HFF], np.float32, F32),
    "w3T": ([D, HFF], np.float32, F32),
    "w2Tp": ([64, 128, D], np.float32, F32),
    "wkT": ([D, AH * ADH], np.float32, F32),
    "wqT": ([D, AH * ADH], np.float32, F32),
    "wvT": ([D, AH * ADH], np.float32, F32),
    "woT": ([AH * ADH, D], np.float32, F32),
    "wgT": ([D, HFF], np.float32, F32),
    "wuT": ([D, HFF], np.float32, F32),
    "wdT": ([HFF, D], np.float32, F32),
    "convw": ([33, 128, 4], np.float32, F32),
    "convb": ([128, 33], np.float32, F32),
    "dtb": ([64, 1], np.float32, F32),
    "alog": ([64, 1], np.float32, F32),
    "dvec": ([1, 64], np.float32, F32),
    "nw": ([128, 32], np.float32, F32),
    "c_id_bf": ([128, 128], NPBF, BF16),
    "c_id_f": ([128, 128], np.float32, F32),
    "c_h128": ([128, 128], NPBF, BF16),
    "c_hp": ([128, 128], NPBF, BF16),
    "c_triu_f": ([128, 128], np.float32, F32),
    "c_triu_bf": ([128, 128], NPBF, BF16),
    "c_ones_f": ([1, 128], np.float32, F32),
    "c_onescol_bf": ([128, 1], NPBF, BF16),
}
SPILLS = {
    "xn0": ([T, D], NPBF, BF16),
    "gateT": ([INTER, T], NPBF, BF16),
    "xconvT": ([INTER, T], NPBF, BF16),
    "bcT": ([64, 2 * T], NPBF, BF16),
    "csT": ([64, T], np.float32, F32),
    "dtT": ([64, T], np.float32, F32),
    "yfT": ([INTER, T], NPBF, BF16),
    "mham": ([T, D], np.float32, F32),
    "vsp": ([64, 128, T], NPBF, BF16),
    "x1": ([T, D], np.float32, F32),
    "xn1": ([T, D], NPBF, BF16),
    "zT": ([AH, ADH, T], NPBF, BF16),
    "matt": ([T, D], np.float32, F32),
    "hid1": ([64, 128, T], NPBF, BF16),
}
XIN = {"x0": ([T, D], np.float32, F32)}
XOUT = {"out": ([T, D], np.float32, F32)}

STAGE_IO = {
    "s0": (["x0"], ["xn0"]),
    "s1": (["xn0"], ["gateT"]),
    "s2": (["xn0"], ["xconvT", "bcT", "csT", "dtT"]),
    "s3a": (["xconvT", "bcT", "csT", "dtT", "gateT"], ["yfT"]),
    "s3b": (["yfT"], ["mham"]),
    "s4": (["xn0"], ["vsp"]),
    "s5": (["vsp", "x0", "mham"], ["x1"]),
    "s6": (["x1"], ["xn1"]),
    "s7": (["xn1"], ["zT"]),
    "s8": (["zT"], ["matt"]),
    "s9": (["xn1"], ["hid1"]),
    "s10": (["hid1", "x1", "matt"], ["out"]),
}
STAGE_W = {
    "s0": [],
    "s1": ["ipT"],
    "s2": ["ipT", "convw", "convb", "dtb", "alog"],
    "s3a": ["dvec", "c_ones_f", "c_onescol_bf", "c_id_bf", "c_id_f", "c_triu_f"],
    "s3b": ["opT", "nw"],
    "s4": ["w1T", "w3T", "c_h128"],
    "s5": ["w2Tp", "c_hp"],
    "s6": [],
    "s7": ["wkT", "wqT", "wvT", "c_onescol_bf", "c_ones_f", "c_triu_bf"],
    "s8": ["woT"],
    "s9": ["wgT", "wuT"],
    "s10": ["wdT"],
}
STAGE_X = {"s0": ["x0"], "s5": ["x0"]}


def load_xnT(c, src_dram, name):
    xnT = c.sb(name, [128, NT * T], BF16)
    v = 0
    for dt_i in range(NT):
        v = c.dma("sync", xnT[:, dt_i * T:(dt_i + 1) * T],
                  src_dram[:, dt_i * 128:(dt_i + 1) * 128], transpose=True)

    def tile(dt_i, lo, hi):
        return xnT[:, dt_i * T + lo: dt_i * T + hi]
    return tile, v


def emit_ln(c, b, src, dst):
    nbuf = 3
    xts = [c.sb(f"ln_x{i}", [128, D], F32) for i in range(nbuf)]
    scr = c.sb("ln_scr", [128, D], F32)
    xnb = [c.sb(f"ln_o{i}", [128, D], BF16) for i in range(2)]
    st = c.sb("ln_st", [128, 8], F32)
    se_done = [0] * nbuf
    out_done = [0, 0]
    for tt in range(NT):
        bi = tt % nbuf
        oi = tt % 2
        if tt >= nbuf:
            c.wait("sync", "ve", se_done[bi])
        ld = c.dma("sync", xts[bi][:], src[tt * 128:(tt + 1) * 128, :])
        c.wait("scalar", "sd", ld)
        sq = c.opi("scalar", "activation", scr[:], xts[bi][:], AF.Square,
                   accum_out=st[:, 1:2])
        c.wait("vector", "sd", ld)
        c.op("vector", "reduce_sum", st[:, 0:1], xts[bi][:], axis=AX.X)
        c.wait("vector", "se", sq)
        c.op("vector", "tensor_scalar", st[:, 2:3], st[:, 0:1], 1.0 / D, None, ALU.mult)
        c.op("vector", "tensor_scalar", st[:, 3:4], st[:, 1:2], 1.0 / D, None, ALU.mult)
        c.op("vector", "tensor_mul", st[:, 6:7], st[:, 2:3], st[:, 2:3])
        c.op("vector", "tensor_sub", st[:, 3:4], st[:, 3:4], st[:, 6:7])
        vv = c.opi("vector", "tensor_scalar", st[:, 3:4], st[:, 3:4], EPS_LN, None, ALU.add)
        c.wait("scalar", "ve", vv)
        c.op("scalar", "activation", st[:, 4:5], st[:, 3:4], AF.Ln)
        ss = c.opi("scalar", "activation", st[:, 4:5], st[:, 4:5], AF.Exp, scale=-0.5)
        c.wait("vector", "se", ss)
        if tt >= 2:
            c.wait("vector", "sd", out_done[oi])
        se_done[bi] = c.opi("vector", "tensor_scalar", xnb[oi][:], xts[bi][:],
                            st[:, 2:3], st[:, 4:5], ALU.subtract, op1=ALU.mult)
        c.wait("sync", "ve", se_done[bi])
        out_done[oi] = c.dma("sync", dst[tt * 128:(tt + 1) * 128, :], xnb[oi][:])
    c.wait("sync", "sd", c.cnt["sd"])


def emit_colproj(c, b, wname, wcols, dst, act, nchtiles, xn_src="xn0"):
    xnt, xn_rdy = load_xnT(c, b[xn_src], "cp_xnT")
    wbufs = [c.sb(f"cp_w{i}", [128, NT * 128], BF16) for i in range(2)]
    obufs = [c.sb(f"cp_o{i}", [128, 512], BF16) for i in range(4)]
    pss = [c.ps(f"cp_p{i}", [128, 512]) for i in range(4)]
    wsrc = b[wname].rearrange("(dt p) ch -> p dt ch", p=128)
    wld = [0, 0]
    wdone = [0, 0]
    ev_done = [0] * 4
    ob_done = [0] * 4
    c.wait("tensor", "sd", xn_rdy)
    for ct in range(nchtiles):
        wb = ct % 2
        if ct >= 2:
            c.wait("gpsimd", "te", wdone[wb])
        wld[wb] = c.dma("gpsimd",
                        wbufs[wb][:].rearrange("p (dt ch) -> p dt ch", ch=128),
                        wsrc[:, :, wcols + ct * 128: wcols + (ct + 1) * 128])
        c.wait("tensor", "gd", wld[wb])
        for tc in range(4):
            pi = (ct * 4 + tc) % 4
            c.wait("tensor", "se", ev_done[pi])
            for dt_i in range(NT):
                ins = c.op("tensor", "matmul", pss[pi][:],
                           wbufs[wb][:, dt_i * 128:(dt_i + 1) * 128],
                           xnt(dt_i, tc * 512, (tc + 1) * 512),
                           start=(dt_i == 0), stop=(dt_i == NT - 1))
            mm = c.inc(ins, "te")
            if tc == 3:
                wdone[wb] = mm
            c.wait("scalar", "te", mm)
            c.wait("scalar", "sd", ob_done[pi])
            ev_done[pi] = c.opi("scalar", "activation", obufs[pi][:], pss[pi][:], act)
            c.wait("sync", "se", ev_done[pi])
            ob_done[pi] = c.dma("sync",
                                dst[ct * 128:(ct + 1) * 128, tc * 512:(tc + 1) * 512],
                                obufs[pi][:])
    c.wait("sync", "sd", c.cnt["sd"])


def emit_s0(c, b):
    emit_ln(c, b, b["x0"], b["xn0"])


def emit_s6(c, b):
    emit_ln(c, b, b["x1"], b["xn1"])


def emit_s1(c, b):
    emit_colproj(c, b, "ipT", 0, b["gateT"], AF.Silu, 32)


def emit_s2(c, b):
    xnt, xn_rdy = load_xnT(c, b["xn0"], "s2_xnT")
    wbufs = [c.sb(f"s2_w{i}", [128, NT * 128], BF16) for i in range(2)]
    cibufs = [c.sb(f"s2_ci{i}", [128, 4 + T], BF16) for i in range(2)]
    cvf = c.sb("s2_cvf", [128, T], F32)
    cvo = [c.sb(f"s2_cvo{i}", [128, T], BF16) for i in range(2)]
    cw = c.sb("s2_cw", [128, 33 * 4], F32)
    cb = c.sb("s2_cb", [128, 33], F32)
    pss = [c.ps(f"s2_p{i}", [128, 512]) for i in range(4)]
    wsrc = b["ipT"].rearrange("(dt p) ch -> p dt ch", p=128)

    c.dma("sync", cw[:].rearrange("p (k c) -> p k c", c=4),
          b["convw"].rearrange("k p c -> p k c"))
    c.dma("sync", cb[:], b["convb"])
    c.opi("vector", "memset", cibufs[0][:, 0:4], 0)
    z1 = c.opi("vector", "memset", cibufs[1][:, 0:4], 0)

    c.wait("tensor", "sd", xn_rdy)
    wld = [0, 0]
    wdone = [0, 0]
    ev_done = [0] * 4
    ci_done = [0, 0]
    cv_done = [0, 0]
    cvs_done = [0, 0]
    for ct in range(33):
        wb = ct % 2
        if ct >= 2:
            c.wait("gpsimd", "te", wdone[wb])
        wld[wb] = c.dma("gpsimd",
                        wbufs[wb][:].rearrange("p (dt ch) -> p dt ch", ch=128),
                        wsrc[:, :, 4096 + ct * 128: 4096 + (ct + 1) * 128])
        c.wait("tensor", "gd", wld[wb])
        cib = ct % 2
        for tc in range(4):
            pi = tc
            c.wait("tensor", "se", ev_done[pi])
            for dt_i in range(NT):
                ins = c.op("tensor", "matmul", pss[pi][:],
                           wbufs[wb][:, dt_i * 128:(dt_i + 1) * 128],
                           xnt(dt_i, tc * 512, (tc + 1) * 512),
                           start=(dt_i == 0), stop=(dt_i == NT - 1))
            mm = c.inc(ins, "te")
            if tc == 3:
                wdone[wb] = mm
            c.wait("scalar", "te", mm)
            if ct >= 2 and tc == 0:
                c.wait("scalar", "ve", ci_done[cib])
            ev_done[pi] = c.opi("scalar", "activation",
                                cibufs[cib][:, 4 + tc * 512: 4 + (tc + 1) * 512],
                                pss[pi][:], AF.Copy)
        c.wait("vector", "se", ev_done[3])
        if ct >= 2:
            c.wait("vector", "se", cv_done[cib])
        c.op("vector", "tensor_scalar", cvf[:], cibufs[cib][:, 1:1 + T],
             cw[:, 4 * ct: 4 * ct + 1], cb[:, ct:ct + 1], ALU.mult, op1=ALU.add)
        ins = None
        for k in range(1, 4):
            ins = c.op("vector", "scalar_tensor_tensor", cvf[:],
                       cibufs[cib][:, 1 + k:1 + k + T],
                       cw[:, 4 * ct + k: 4 * ct + k + 1], cvf[:], ALU.mult, ALU.add)
        ci_done[cib] = c.inc(ins, "ve")
        co = ct % 2
        c.wait("scalar", "ve", ci_done[cib])
        if ct >= 2:
            c.wait("scalar", "sd", cvs_done[co])
        cv_done[co] = c.opi("scalar", "activation", cvo[co][:], cvf[:], AF.Silu)
        c.wait("sync", "se", cv_done[co])
        if ct < 32:
            cvs_done[co] = c.dma("sync", b["xconvT"][ct * 128:(ct + 1) * 128, :],
                                 cvo[co][:])
        else:
            c.dma("sync", b["bcT"][:, 0:T], cvo[co][0:64, :])
            cvs_done[co] = c.dma("sync", b["bcT"][:, T:2 * T], cvo[co][64:128, :])

    dwb = c.sb("s2_dw", [128, NT * 64], BF16)
    dtraw = c.sb("s2_dtraw", [64, T], F32)
    sc1 = c.sb("s2_sc1", [64, T], F32)
    sc2 = c.sb("s2_sc2", [64, T], F32)
    dts = c.sb("s2_dts", [64, T], F32)
    csb = c.sb("s2_csb", [64, T], F32)
    smal = c.sb("s2_smal", [64, 2], F32)
    dld = c.dma("gpsimd", dwb[:].rearrange("p (dt ch) -> p dt ch", ch=64),
                wsrc[:, :, 8320:8384])
    c.dma("sync", smal[:, 0:1], b["dtb"])
    bld = c.dma("sync", smal[:, 1:2], b["alog"])
    c.wait("scalar", "sd", bld)
    na = c.opi("scalar", "activation", smal[:, 1:2], smal[:, 1:2], AF.Exp)
    c.wait("vector", "se", na)
    nv = c.opi("vector", "tensor_scalar", smal[:, 1:2], smal[:, 1:2], -1.0, None, ALU.mult)
    c.wait("tensor", "gd", dld)
    for tc in range(4):
        c.wait("tensor", "se", ev_done[tc])
        for dt_i in range(NT):
            ins = c.op("tensor", "matmul", pss[tc][0:64, :],
                       dwb[:, dt_i * 64:(dt_i + 1) * 64],
                       xnt(dt_i, tc * 512, (tc + 1) * 512),
                       start=(dt_i == 0), stop=(dt_i == NT - 1))
        mm = c.inc(ins, "te")
        c.wait("vector", "te", mm)
        c.op("vector", "tensor_copy", dtraw[:, tc * 512:(tc + 1) * 512], pss[tc][0:64, :])
    av = c.opi("vector", "tensor_scalar", sc1[:], dtraw[:], smal[:, 0:1], None, ALU.add)
    c.wait("scalar", "ve", av)
    c.op("scalar", "activation", sc2[:], sc1[:], AF.Abs)
    c.op("scalar", "activation", sc2[:], sc2[:], AF.Exp, scale=-1.0)
    ee = c.opi("scalar", "activation", sc2[:], sc2[:], AF.Ln, bias=1.0)
    c.wait("vector", "se", ee)
    c.op("vector", "tensor_scalar", dts[:], sc1[:], 0.0, None, ALU.max)
    dt_ok = c.opi("vector", "tensor_add", dts[:], dts[:], sc2[:])
    c.wait("sync", "ve", dt_ok)
    c.dma("sync", b["dtT"], dts[:])
    c.op("vector", "tensor_scalar", sc1[:], dts[:], smal[:, 1:2], None, ALU.mult)
    ins = None
    for ch in range(NCH):
        ins = c.op("vector", "tensor_tensor_scan", csb[:, ch * Q:(ch + 1) * Q],
                   sc1[:, ch * Q:(ch + 1) * Q], sc1[:, ch * Q:(ch + 1) * Q],
                   0.0, ALU.add, ALU.bypass)
    cs_ok = c.inc(ins, "ve")
    c.wait("sync", "ve", cs_ok)
    c.dma("sync", b["csT"], csb[:])
    c.wait("sync", "sd", c.cnt["sd"])


def emit_s3a(c, b):
    bc = c.sb("sa_bc", [64, 2 * T], BF16)
    csT = c.sb("sa_csT", [64, T], F32)
    dtT = c.sb("sa_dtT", [64, T], F32)
    dexp = c.sb("sa_dexp", [128, 64], F32)
    ones = c.sb("sa_ones", [1, 128], F32)
    onecb = c.sb("sa_onecb", [128, 1], BF16)
    idbf = c.sb("sa_idbf", [128, 128], BF16)
    idf = c.sb("sa_idf", [128, 128], F32)
    triu = c.sb("sa_triu", [128, 128], F32)
    drow = c.sb("sa_drow", [1, 64], F32)
    c.dma("sync", bc[:], b["bcT"])
    c.dma("sync", csT[:], b["csT"])
    c.dma("sync", dtT[:], b["dtT"])
    c.dma("sync", ones[:], b["c_ones_f"])
    c.dma("sync", onecb[:], b["c_onescol_bf"])
    c.dma("sync", idbf[:], b["c_id_bf"])
    c.dma("sync", idf[:], b["c_id_f"])
    c.dma("sync", triu[:], b["c_triu_f"])
    v = c.dma("sync", drow[:], b["dvec"])
    p_bc = c.ps("sa_pbc", [128, 1024])
    p_A = c.ps("sa_pA", [128, 512])
    p_B = c.ps("sa_pB", [128, 512])
    p_st = c.ps("sa_pst", [128, 512])
    p_m = c.ps("sa_pm", [128, 512])
    p_bf = c.ps("sa_pbf", [128, 128], BF16)
    xnat = [c.sb(f"sa_xn{i}", [128, INTER], BF16) for i in range(2)]
    xtil = c.sb("sa_xtil", [128, INTER], BF16)
    xdd = c.sb("sa_xdd", [128, INTER], BF16)
    gbuf = [c.sb(f"sa_g{i}", [128, INTER], BF16) for i in range(2)]
    csn = c.sb("sa_csn", [128, 64], F32)
    dtn = c.sb("sa_dtn", [128, 64], F32)
    csrow = c.sb("sa_csrow", [1, 1024], F32)
    cschk = c.sb("sa_cschk", [64, 128], F32)
    dtchk = c.sb("sa_dtchk", [64, 128], F32)
    csrT = c.sb("sa_csrT", [1, 64], F32)
    ebcT = c.sb("sa_ebcT", [64, 64], F32)
    dltT = c.sb("sa_dltT", [64, 128], F32)
    dln = c.sb("sa_dln", [128, 64], F32)
    bnat = c.sb("sa_bnat", [128, 64], BF16)
    bhat = c.sb("sa_bhat", [128, 512], BF16)
    mtp = c.sb("sa_mtp", [128, 128], BF16)
    rall = c.sb("sa_rall", [128, 1024], F32)
    lall = c.sb("sa_lall", [128, 1024], F32)
    mall = c.sb("sa_mall", [128, 8 * 128], BF16)
    epair = c.sb("sa_ep", [128, 512], F32)
    yt = c.sb("sa_yt", [128, INTER], F32)
    y2 = c.sb("sa_y2", [128, INTER], BF16)
    tei = c.sb("sa_tei", [128, 512], F32)
    s_sb = [c.sb(f"sa_s{i}", [64, HEADS * PD], F32) for i in range(2)]
    stmp = c.sb("sa_stmp", [64, 512], F32)
    sbfg = c.sb("sa_sbfg", [64, 512], BF16)
    rstd = c.sb("sa_rstd", [1, 128], F32)
    rbc = c.sb("sa_rbc", [128, 128], F32)
    yfb = [c.sb(f"sa_yf{i}", [128, INTER], BF16) for i in range(2)]

    c.opi("vector", "memset", s_sb[0][:], 0)
    c.wait("tensor", "sd", v)
    t0_ = c.inc(c.op("tensor", "matmul", p_m[:, 0:64], ones[0:1, :], drow[0:1, :],
                     start=True, stop=True), "te")
    c.wait("vector", "te", t0_)
    pm_free = c.opi("vector", "tensor_copy", dexp[:], p_m[:, 0:64])

    xl = [0, 0]; gl = [0, 0]
    ymm_prev = 0
    ymm_prev_g = 0
    mv_prev = 0
    st1_prev = 0
    yAB_prev = 0
    y2_prev = 0
    xcons = [0, 0]
    gcons = [0, 0]
    crow_cons = 0
    chk_cons = 0
    bn_free = 0
    bc_free = 0
    yAB_free = 0
    st_free = 0
    yst = [0, 0]
    for ch in range(NCH):
        par = ch % 2
        t0, t1 = ch * Q, (ch + 1) * Q
        if ch >= 2:
            c.wait("sync", "te", xcons[par])
            c.wait("sync", "ve", gcons[par])
        xl[par] = c.dma("sync", xnat[par][:], b["xconvT"][:, t0:t1], transpose=True)
        gl[par] = c.dma("sync", gbuf[par][:].rearrange("p (k t) -> p k t", t=Q),
                        b["gateT"].rearrange("(k p) t -> p k t", p=128)[:, :, t0:t1])
        c.wait("vector", "te", chk_cons)
        c.wait("vector", "se", bc_free)
        c.op("vector", "tensor_copy", cschk[:], csT[:, t0:t1])
        cd = c.opi("vector", "tensor_copy", dtchk[:], dtT[:, t0:t1])
        c.wait("tensor", "ve", cd)
        c.wait("tensor", "ve", pm_free)
        c.op("tensor", "transpose", p_m[:, 0:64], cschk[:], idf[0:64, 0:64])
        c.op("tensor", "transpose", p_m[:, 64:128], dtchk[:], idf[0:64, 0:64])
        tr1 = c.inc(c.op("tensor", "transpose", p_m[0:1, 128:192],
                         csT[:, t1 - 1:t1], idf[0:64, 0:64]), "te")
        chk_cons = tr1
        c.wait("vector", "te", tr1)
        c.op("vector", "tensor_copy", csn[:], p_m[:, 0:64])
        c.op("vector", "tensor_copy", dtn[:], p_m[:, 64:128])
        pm_free = c.opi("vector", "tensor_copy", csrT[:], p_m[0:1, 128:192])
        c.wait("tensor", "ve", pm_free)
        tb = c.inc(c.op("tensor", "matmul", p_m[0:64, 192:256], ones[0:1, 0:64],
                        csrT[0:1, :], start=True, stop=True), "te")
        c.wait("scalar", "te", tb)
        c.wait("scalar", "ve", st1_prev)
        eb = c.opi("scalar", "activation", ebcT[:], p_m[0:64, 192:256], AF.Exp)
        dl1 = c.opi("vector", "tensor_scalar", dltT[:], cschk[:],
                    csT[:, t1 - 1:t1], None, ALU.subtract)
        c.wait("scalar", "ve", dl1)
        dl2 = c.opi("scalar", "activation", dltT[:], dltT[:], AF.Exp, scale=-1.0)
        c.wait("vector", "se", dl2)
        dl3 = c.opi("vector", "tensor_mul", dltT[:], dltT[:], dtchk[:])
        c.wait("tensor", "ve", dl3)
        c.wait("tensor", "se", eb)
        tr2 = c.inc(c.op("tensor", "transpose", p_m[:, 256:320], dltT[:], idf[0:64, 0:64]), "te")
        c.wait("vector", "te", tr2)
        pm_free = c.opi("vector", "tensor_copy", dln[:], p_m[:, 256:320])
        c.wait("tensor", "ve", bn_free)
        trb = c.inc(c.op("tensor", "transpose", p_bf[:, 0:64], bc[:, t0:t1],
                         idbf[0:64, 0:64]), "te")
        c.wait("vector", "te", trb)
        bn_free = c.opi("vector", "tensor_copy", bnat[:], p_bf[:, 0:64])
        c.wait("tensor", "ve", pm_free)
        mt1 = c.inc(c.op("tensor", "matmul", p_m[:, 320:448], bc[:, t0:t1],
                         bc[:, T + t0:T + t1], start=True, stop=True), "te")
        c.wait("vector", "te", mt1)
        pm_free = c.opi("vector", "tensor_mul", mtp[:], p_m[:, 320:448], triu[:])
        mt2 = pm_free
        c.wait("vector", "sd", xl[par])
        c.wait("vector", "te", ymm_prev)
        xt1 = c.opi("vector", "tensor_tensor",
                    xtil[:].rearrange("p (h q) -> p h q", q=PD),
                    xnat[par][:].rearrange("p (h q) -> p h q", q=PD),
                    dtn[:].unsqueeze(2).broadcast_to([128, HEADS, PD]), ALU.mult)
        xd1 = c.opi("vector", "tensor_tensor",
                    xdd[:].rearrange("p (h q) -> p h q", q=PD),
                    xnat[par][:].rearrange("p (h q) -> p h q", q=PD),
                    dexp[:, 0:HEADS].unsqueeze(2).broadcast_to([128, HEADS, PD]), ALU.mult)
        sprev = s_sb[ch % 2]
        snew = s_sb[(ch + 1) % 2]
        yt_guard = y2_prev
        for g in range(8):
            h0 = g * 8
            c.wait("sync", "te", crow_cons)
            crl = c.dma("sync", csrow[:].rearrange("o (h t) -> o h t", t=Q),
                        b["csT"][h0:h0 + 8, t0:t1].unsqueeze(0))
            c.wait("tensor", "se", bc_free)
            c.wait("tensor", "sd", crl)
            c.op("tensor", "matmul", p_bc[:, 0:512], ones[0:1, :],
                 csrow[0:1, 0:512], start=True, stop=True)
            bcv = c.inc(c.op("tensor", "matmul", p_bc[:, 512:1024], ones[0:1, :],
                             csrow[0:1, 512:1024],
                             start=True, stop=True), "te")
            crow_cons = bcv
            c.wait("scalar", "te", bcv)
            c.wait("scalar", "ve", yAB_prev)
            ins = None
            for hh in range(8):
                ins = c.op("scalar", "activation", rall[:, hh * Q:(hh + 1) * Q],
                           p_bc[:, hh * Q:(hh + 1) * Q], AF.Relu, scale=-1.0,
                           bias=csn[:, h0 + hh: h0 + hh + 1])
            c.inc(ins, "se")
            c.op("scalar", "activation",
                 epair[0:64, :].rearrange("p (k t) -> p k t", t=Q),
                 p_bc[0:64, :].rearrange("p (hh t) -> p hh t", t=Q)[:, 0:8:2, :],
                 AF.Exp)
            epv = c.inc(c.op("scalar", "activation",
                             epair[64:128, :].rearrange("p (k t) -> p k t", t=Q),
                             p_bc[64:128, :].rearrange("p (hh t) -> p hh t", t=Q)[:, 1:8:2, :],
                             AF.Exp), "se")
            c.wait("scalar", "ve", mv_prev)
            lv = c.opi("scalar", "activation", lall[:], rall[:], AF.Exp, scale=-1.0)
            bc_free = lv
            c.wait("vector", "se", lv)
            c.wait("vector", "ve", mt2)
            c.wait("vector", "te", ymm_prev_g)
            mv = c.opi("vector", "tensor_tensor",
                       mall[:].rearrange("p (hh t) -> p hh t", t=Q),
                       lall[:].rearrange("p (hh t) -> p hh t", t=Q),
                       mtp[:].unsqueeze(1).broadcast_to([128, 8, Q]), ALU.mult)
            c.wait("vector", "te", trb)
            c.wait("vector", "te", ymm_prev_g)
            c.wait("vector", "se", eb)
            bh1 = c.opi("vector", "tensor_tensor",
                        bhat[:].rearrange("p (h n) -> p h n", n=ND),
                        bnat[:].unsqueeze(1).broadcast_to([128, 8, ND]),
                        dln[:, h0:h0 + 8].unsqueeze(2).broadcast_to([128, 8, ND]),
                        ALU.mult)
            st1 = c.opi("vector", "tensor_tensor",
                        stmp[:].rearrange("n (h q) -> n h q", q=PD),
                        sprev[:, h0 * PD:(h0 + 8) * PD].rearrange("n (h q) -> n h q", q=PD),
                        ebcT[:, h0:h0 + 8].unsqueeze(2).broadcast_to([64, 8, PD]),
                        ALU.mult)
            st1 = c.opi("vector", "tensor_copy", sbfg[:],
                        sprev[:, h0 * PD:(h0 + 8) * PD])
            st1_prev = st1
            c.wait("tensor", "ve", mv)
            c.wait("tensor", "ve", max(xt1, xd1, bh1, st1, yAB_free, st_free))
            insA = insB = None
            for kk in range(4):
                for e in range(2):
                    h = h0 + 2 * kk + e
                    tp = (0, 64 * e)
                    rows = slice(64 * e, 64 * e + 64)
                    c.op("tensor", "matmul", p_A[rows, kk * 128:(kk + 1) * 128],
                         xtil[:, h * PD:(h + 1) * PD],
                         mall[:, (2 * kk + e) * Q:(2 * kk + e + 1) * Q],
                         start=True, stop=False, tile_position=tp)
                    insA = c.op("tensor", "matmul", p_A[rows, kk * 128:(kk + 1) * 128],
                                xdd[:, h * PD:(h + 1) * PD], idbf[:],
                                start=False, stop=True, tile_position=tp)
                    insB = c.op("tensor", "matmul", p_B[rows, kk * 128:(kk + 1) * 128],
                                sbfg[:, (2 * kk + e) * PD:(2 * kk + e + 1) * PD],
                                bc[:, T + t0:T + t1],
                                start=True, stop=True, tile_position=tp)
            c.inc(insA, "te")
            yB = c.inc(insB, "te")
            mv_prev = mv
            inss = None
            for hh in range(8):
                h = h0 + hh
                inss = c.op("tensor", "matmul", p_st[0:64, hh * 64:(hh + 1) * 64],
                            bhat[:, hh * ND:(hh + 1) * ND],
                            xnat[par][:, h * PD:(h + 1) * PD],
                            start=True, stop=True)
            yS = c.inc(inss, "te")
            ymm_prev_g = yS
            if g == 7:
                xcons[par] = yS
                ymm_prev = yS
            c.wait("vector", "te", yS)
            c.wait("vector", "se", epv)
            c.wait("vector", "se", yt_guard)
            c.op("vector", "tensor_tensor", tei[:], p_B[:, 0:512],
                 epair[:], ALU.mult)
            yAB_free = c.opi("vector", "tensor_tensor", yt[:, h0 * PD:(h0 + 8) * PD],
                             p_A[:, 0:512], tei[:], ALU.add)
            yAB_prev = yAB_free
            st_free = c.opi("vector", "tensor_add", snew[:, h0 * PD:(h0 + 8) * PD],
                            stmp[:], p_st[0:64, :])
        c.wait("scalar", "ve", yAB_free)
        y2v = c.opi("scalar", "activation", y2[:], yt[:], AF.Square)
        y2_prev = y2v
        c.wait("tensor", "se", y2v)
        c.wait("tensor", "ve", pm_free)
        insv = None
        for kk in range(32):
            insv = c.op("tensor", "matmul", p_m[0:1, 0:128], onecb[:],
                        y2[:, kk * 128:(kk + 1) * 128], start=(kk == 0), stop=(kk == 31))
        vv = c.inc(insv, "te")
        c.wait("vector", "te", vv)
        r1 = c.opi("vector", "tensor_scalar", rstd[:], p_m[0:1, 0:128],
                   1.0 / INTER, EPS_MN, ALU.mult, op1=ALU.add)
        c.wait("scalar", "ve", r1)
        c.op("scalar", "activation", rstd[:], rstd[:], AF.Ln)
        r2 = c.opi("scalar", "activation", rstd[:], rstd[:], AF.Exp, scale=-0.5)
        c.wait("tensor", "se", r2)
        rb = c.inc(c.op("tensor", "matmul", p_m[:, 128:256], ones[0:1, :],
                        rstd[0:1, :], start=True, stop=True), "te")
        c.wait("vector", "te", rb)
        pm_free = c.opi("vector", "tensor_copy", rbc[:], p_m[:, 128:256])
        c.wait("vector", "sd", gl[par])
        if ch >= 2:
            c.wait("vector", "sd", yst[par])
        c.op("vector", "tensor_tensor", yt[:].rearrange("p (k t2) -> p k t2", t2=128),
             yt[:].rearrange("p (k t2) -> p k t2", t2=128),
             rbc[:].unsqueeze(1).broadcast_to([128, 32, 128]), ALU.mult)
        gcons[par] = c.opi("vector", "tensor_mul", yfb[par][:], yt[:], gbuf[par][:])
        c.wait("sync", "ve", gcons[par])
        yst[par] = c.dma("sync",
                         b["yfT"].rearrange("(k p) t -> p k t", p=128)[:, :, t0:t1],
                         yfb[par][:].rearrange("p (k t2) -> p k t2", t2=128))
    c.wait("sync", "sd", c.cnt["sd"])


def emit_s3b(c, b):
    nw = c.sb("sb_nw", [128, 32], F32)
    wsb = c.sb("sb_w", [128, 32 * D], BF16)
    ybufs = [c.sb(f"sb_y{i}", [128, 32 * 128], BF16) for i in range(2)]
    obufs = [c.sb(f"sb_o{i}", [128, D], F32) for i in range(2)]
    pss = [c.ps(f"sb_p{i}", [128, 512]) for i in range(8)]
    v = c.dma("sync", nw[:], b["nw"])
    wld = c.dma("gpsimd", wsb[:].rearrange("p (k d) -> p k d", d=D),
                b["opT"].rearrange("(k p) d -> p k d", p=128))
    c.wait("vector", "sd", v)
    c.wait("vector", "gd", wld)
    ins = None
    for k in range(32):
        ins = c.op("vector", "tensor_scalar", wsb[:, k * D:(k + 1) * D],
                   wsb[:, k * D:(k + 1) * D], nw[:, k:k + 1], None, ALU.mult)
    wok = c.inc(ins, "ve")
    c.wait("tensor", "ve", wok)
    yld = [0, 0]
    ycons = [0, 0]
    ev = [0] * 8
    ost = [0, 0]
    for tt in range(NT):
        par = tt % 2
        if tt >= 2:
            c.wait("sync", "te", ycons[par])
        yld[par] = c.dma("sync", ybufs[par][:].rearrange("p (k t) -> p k t", t=128),
                         b["yfT"].rearrange("(k p) t -> p k t", p=128)[:, :, tt * 128:(tt + 1) * 128])
        c.wait("tensor", "sd", yld[par])
        for dc in range(4):
            pi = par * 4 + dc
            c.wait("tensor", "ve", ev[pi])
            ins = None
            for k in range(32):
                ins = c.op("tensor", "matmul", pss[pi][:],
                           ybufs[par][:, k * 128:(k + 1) * 128],
                           wsb[:, k * D + dc * 512: k * D + (dc + 1) * 512],
                           start=(k == 0), stop=(k == 31))
            mm = c.inc(ins, "te")
            if dc == 3:
                ycons[par] = mm
            c.wait("vector", "te", mm)
            if dc == 0 and tt >= 2:
                c.wait("vector", "sd", ost[par])
            ev[pi] = c.opi("vector", "tensor_copy",
                           obufs[par][:, dc * 512:(dc + 1) * 512], pss[pi][:])
        c.wait("sync", "ve", ev[par * 4 + 3])
        ost[par] = c.dma("sync", b["mham"][tt * 128:(tt + 1) * 128, :], obufs[par][:])
    c.wait("sync", "sd", c.cnt["sd"])


def emit_dualproj(c, b, wa, wb_, dst, fwht, xn_src):
    xnt, xn_rdy = load_xnT(c, b[xn_src], "dp_xnT")
    h128 = c.sb("dp_h128", [128, 128], BF16)
    if fwht:
        c.dma("sync", h128[:], b["c_h128"])
    wbufs = [c.sb(f"dp_w{i}", [128, NT * 128], BF16) for i in range(4)]
    sbuf = c.sb("dp_s", [128, 512], F32)
    hbuf = c.sb("dp_h", [128, 512], BF16)
    obufs = [c.sb(f"dp_o{i}", [128, 512], BF16) for i in range(4)]
    psu = [c.ps(f"dp_pu{i}", [128, 512]) for i in range(2)]
    psv = [c.ps(f"dp_pv{i}", [128, 512]) for i in range(2)]
    psw = [c.ps(f"dp_pw{i}", [128, 512]) for i in range(2)]
    wsrca = b[wa].rearrange("(dt p) hh -> p dt hh", p=128)
    wsrcb = b[wb_].rearrange("(dt p) hh -> p dt hh", p=128)
    c.wait("tensor", "sd", xn_rdy)
    wld = [0, 0]
    wdone = [0, 0]
    sil_done = [0, 0]
    mul_done = [0, 0]
    st1_done = [0, 0]
    last_mul = 0
    last_st1 = 0
    ost = [0] * 4
    for a in range(64):
        par = a % 2
        if a >= 2:
            c.wait("gpsimd", "te", wdone[par])
        c.dma("gpsimd", wbufs[par][:].rearrange("p (dt hh) -> p dt hh", hh=128),
              wsrca[:, :, a * 128:(a + 1) * 128])
        wld[par] = c.dma("gpsimd", wbufs[2 + par][:].rearrange("p (dt hh) -> p dt hh", hh=128),
                         wsrcb[:, :, a * 128:(a + 1) * 128])
        c.wait("tensor", "gd", wld[par])
        for tc in range(4):
            oi = (a * 4 + tc) % 4
            c.wait("tensor", "se", sil_done[par])
            ins = None
            for dt_i in range(NT):
                ins = c.op("tensor", "matmul", psu[par][:],
                           wbufs[par][:, dt_i * 128:(dt_i + 1) * 128],
                           xnt(dt_i, tc * 512, (tc + 1) * 512),
                           start=(dt_i == 0), stop=(dt_i == NT - 1))
            mmu = c.inc(ins, "te")
            c.wait("tensor", "ve", mul_done[par])
            for dt_i in range(NT):
                ins = c.op("tensor", "matmul", psv[par][:],
                           wbufs[2 + par][:, dt_i * 128:(dt_i + 1) * 128],
                           xnt(dt_i, tc * 512, (tc + 1) * 512),
                           start=(dt_i == 0), stop=(dt_i == NT - 1))
            mmv = c.inc(ins, "te")
            if tc == 3:
                wdone[par] = mmv
            c.wait("scalar", "te", mmu)
            c.wait("scalar", "ve", last_mul)
            sil_done[par] = c.opi("scalar", "activation", sbuf[:], psu[par][:], AF.Silu)
            c.wait("vector", "se", sil_done[par])
            c.wait("vector", "te", mmv)
            if fwht:
                c.wait("vector", "te", last_st1)
                mul_done[par] = c.opi("vector", "tensor_mul", hbuf[:], sbuf[:], psv[par][:])
                last_mul = mul_done[par]
                c.wait("tensor", "ve", mul_done[par])
                c.wait("tensor", "ve", st1_done[par])
                mmw = c.inc(c.op("tensor", "matmul", psw[par][:], h128[:], hbuf[:],
                                 start=True, stop=True), "te")
                last_st1 = mmw
                c.wait("vector", "te", mmw)
                c.wait("vector", "sd", ost[oi])
                st1_done[par] = c.opi("vector", "tensor_copy", obufs[oi][:], psw[par][:])
                dv = st1_done[par]
            else:
                c.wait("vector", "sd", ost[oi])
                mul_done[par] = c.opi("vector", "tensor_mul", obufs[oi][:], sbuf[:], psv[par][:])
                last_mul = mul_done[par]
                dv = mul_done[par]
            c.wait("sync", "ve", dv)
            ost[oi] = c.dma("sync", dst[a, :, tc * 512:(tc + 1) * 512], obufs[oi][:])
    c.wait("sync", "sd", c.cnt["sd"])


def emit_s4(c, b):
    emit_dualproj(c, b, "w1T", "w3T", b["vsp"], True, "xn0")


def emit_s9(c, b):
    emit_dualproj(c, b, "wgT", "wuT", b["hid1"], False, "xn1")


def emit_s5(c, b):
    hp = c.sb("s5_hp", [128, 128], BF16)
    v0 = c.dma("sync", hp[:], b["c_hp"])
    w2 = c.sb("s5_w2", [128, 64 * 1024], BF16)
    vt = [c.sb(f"s5_vt{i}", [128, 64 * 128], BF16) for i in range(2)]
    s2 = c.sb("s5_s2", [128, 64 * 128], BF16)
    xb = [c.sb(f"s5_x{i}", [128, 1024], F32) for i in range(2)]
    mb = [c.sb(f"s5_m{i}", [128, 1024], F32) for i in range(2)]
    ob = [c.sb(f"s5_o{i}", [128, 1024], F32) for i in range(2)]
    ps_s2 = [c.ps(f"s5_ps{i}", [128, 512]) for i in range(2)]
    ps_o = [c.ps(f"s5_po{i}", [128, 512]) for i in range(4)]
    vcons = [0, 0]
    s2done = [0, 0]
    evd = [0] * 4
    ostd = [0, 0]
    s2cons = 0
    for half in range(2):
        d0 = half * 1024
        if half > 0:
            c.wait("gpsimd", "te", c.cnt["te"])
        wld = c.dma("gpsimd", w2[:].rearrange("p (k d) -> p k d", d=1024),
                    b["w2Tp"][:, :, d0:d0 + 1024].rearrange("k p d -> p k d"))
        c.wait("tensor", "gd", wld)
        c.wait("tensor", "sd", v0)
        for tt in range(NT):
            par = tt % 2
            t0 = tt * 128
            if half + tt >= 2:
                c.wait("sync", "te", vcons[par])
            c.dma("sync", vt[par][0:64, :].rearrange("p (k t) -> p k t", t=128),
                  b["vsp"][:, 0:128:2, t0:t0 + 128])
            vld = c.dma("sync", vt[par][64:128, :].rearrange("p (k t) -> p k t", t=128),
                        b["vsp"][:, 1:128:2, t0:t0 + 128])
            c.wait("tensor", "sd", vld)
            c.wait("vector", "te", s2cons)
            for bp in range(64):
                pi = bp % 2
                c.wait("tensor", "ve", s2done[pi])
                mm = c.inc(c.op("tensor", "matmul", ps_s2[pi][:, 0:128], hp[:],
                                vt[par][:, bp * 128:(bp + 1) * 128],
                                start=True, stop=True), "te")
                if bp == 63:
                    vcons[par] = mm
                c.wait("vector", "te", mm)
                s2done[pi] = c.opi("vector", "tensor_copy",
                                   s2[:, bp * 128:(bp + 1) * 128], ps_s2[pi][:, 0:128])
            s2v = max(s2done)
            c.wait("tensor", "ve", s2v)
            for dc in range(2):
                pi = par * 2 + dc
                c.wait("tensor", "ve", evd[pi])
                ins = None
                for bp in range(64):
                    ins = c.op("tensor", "matmul", ps_o[pi][:],
                               s2[:, bp * 128:(bp + 1) * 128],
                               w2[:, bp * 1024 + dc * 512: bp * 1024 + (dc + 1) * 512],
                               start=(bp == 0), stop=(bp == 63))
                mmo = c.inc(ins, "te")
                if dc == 1:
                    s2cons = mmo
                c.wait("sync", "ve", evd[pi])
                xl = c.dma("sync", xb[par][:, dc * 512:(dc + 1) * 512],
                           b["x0"][t0:t0 + 128, d0 + dc * 512: d0 + (dc + 1) * 512])
                ml = c.dma("sync", mb[par][:, dc * 512:(dc + 1) * 512],
                           b["mham"][t0:t0 + 128, d0 + dc * 512: d0 + (dc + 1) * 512])
                c.wait("vector", "te", mmo)
                c.wait("vector", "sd", ml)
                if half + tt >= 2 and dc == 0:
                    c.wait("vector", "sd", ostd[par])
                c.op("vector", "tensor_add", ob[par][:, dc * 512:(dc + 1) * 512],
                     xb[par][:, dc * 512:(dc + 1) * 512],
                     mb[par][:, dc * 512:(dc + 1) * 512])
                evd[pi] = c.opi("vector", "tensor_add",
                                ob[par][:, dc * 512:(dc + 1) * 512],
                                ob[par][:, dc * 512:(dc + 1) * 512], ps_o[pi][:])
            c.wait("sync", "ve", evd[par * 2 + 1])
            ostd[par] = c.dma("sync", b["x1"][t0:t0 + 128, d0:d0 + 1024], ob[par][:])
    c.wait("sync", "sd", c.cnt["sd"])


def emit_s7(c, b):
    xnt, xn_rdy = load_xnT(c, b["xn1"], "s7_xnT")
    onescol = c.sb("s7_oc", [128, 1], BF16)
    onesf = c.sb("s7_on", [1, 128], F32)
    triub = c.sb("s7_tr", [128, 128], BF16)
    c.dma("sync", onescol[:], b["c_onescol_bf"])
    c.dma("sync", onesf[:], b["c_ones_f"])
    v0 = c.dma("sync", triub[:], b["c_triu_bf"])
    wk = [c.sb(f"s7_wk{i}", [128, NT * 128], BF16) for i in range(2)]
    wq = [c.sb(f"s7_wq{i}", [128, NT * 128], BF16) for i in range(2)]
    wv = [c.sb(f"s7_wv{i}", [128, NT * 128], BF16) for i in range(2)]
    kT = c.sb("s7_kT", [128, T], BF16)
    qT = c.sb("s7_qT", [128, T], BF16)
    vn = c.sb("s7_vn", [128, NT * 128], BF16)
    pb = c.sb("s7_pb", [128, 8 * 128], BF16)
    rn = c.sb("s7_rn", [1, 128], F32)
    rnb = c.sb("s7_rnb", [128, 128], F32)
    zb = [c.sb(f"s7_z{i}", [128, T], BF16) for i in range(2)]
    ps_s = [c.ps(f"s7_ps{i}", [128, 1024]) for i in range(2)]
    ps_z = c.ps("s7_pz", [128, 512])
    ps_d = c.ps("s7_pd", [128, 512])
    ps_q = [c.ps(f"s7_pq{i}", [128, 512]) for i in range(2)]
    wks = b["wkT"].rearrange("(dt p) f -> p dt f", p=128)
    wqs = b["wqT"].rearrange("(dt p) f -> p dt f", p=128)
    wvs = b["wvT"].rearrange("(dt p) f -> p dt f", p=128)
    c.wait("tensor", "sd", max(xn_rdy, v0))
    wld = [0, 0]
    wcons = [0, 0]
    kq_ev = 0
    st_free = [0, 0]
    pb_free = 0
    z_free = 0
    d_free = 0
    zst = [0, 0]
    z_ev = [0, 0]
    for h in range(AH):
        par = h % 2
        if h >= 2:
            c.wait("gpsimd", "te", wcons[par])
        c.dma("gpsimd", wk[par][:].rearrange("p (dt f) -> p dt f", f=128),
              wks[:, :, h * 128:(h + 1) * 128])
        c.dma("gpsimd", wq[par][:].rearrange("p (dt f) -> p dt f", f=128),
              wqs[:, :, h * 128:(h + 1) * 128])
        wld[par] = c.dma("gpsimd", wv[par][:].rearrange("p (dt f) -> p dt f", f=128),
                         wvs[:, :, h * 128:(h + 1) * 128])
        c.wait("tensor", "gd", wld[par])
        for wbuf, dstb in ((wk, kT), (wq, qT)):
            for tc in range(4):
                pi = tc % 2
                c.wait("tensor", "ve", kq_ev)
                ins = None
                for dt_i in range(NT):
                    ins = c.op("tensor", "matmul", ps_q[pi][:],
                               wbuf[par][:, dt_i * 128:(dt_i + 1) * 128],
                               xnt(dt_i, tc * 512, (tc + 1) * 512),
                               start=(dt_i == 0), stop=(dt_i == NT - 1))
                mm = c.inc(ins, "te")
                c.wait("vector", "te", mm)
                kq_ev = c.opi("vector", "tensor_copy",
                              dstb[:, tc * 512:(tc + 1) * 512], ps_q[pi][:])
        for tt2 in range(NT):
            pi = tt2 % 2
            c.wait("tensor", "ve", kq_ev)
            ins = None
            for dt_i in range(NT):
                ins = c.op("tensor", "matmul", ps_q[pi][0:128, 0:128],
                           xnt(dt_i, tt2 * 128, (tt2 + 1) * 128),
                           wv[par][:, dt_i * 128:(dt_i + 1) * 128],
                           start=(dt_i == 0), stop=(dt_i == NT - 1))
            mm = c.inc(ins, "te")
            c.wait("vector", "te", mm)
            kq_ev = c.opi("vector", "tensor_copy",
                          vn[:, tt2 * 128:(tt2 + 1) * 128], ps_q[pi][0:128, 0:128])
        wcons[par] = c.cnt["te"]
        c.wait("tensor", "ve", kq_ev)
        zo = h % 2
        if h >= 2:
            c.wait("vector", "sd", zst[zo])
        for qt in range(NT):
            nblk = (qt + 8) // 8
            for blk in range(nblk):
                j0 = blk * 8
                nj = min(8, qt + 1 - j0)
                sb_i = blk % 2
                c.wait("tensor", "se", st_free[sb_i])
                ins = None
                for jj in range(nj):
                    ins = c.op("tensor", "matmul", ps_s[sb_i][:, jj * 128:(jj + 1) * 128],
                               kT[:, (j0 + jj) * 128:(j0 + jj + 1) * 128],
                               qT[:, qt * 128:(qt + 1) * 128], start=True, stop=True)
                sv = c.inc(ins, "te")
                c.wait("scalar", "te", max(sv, pb_free))
                ex = c.opi("scalar", "activation", pb[:, 0:nj * 128],
                           ps_s[sb_i][:, 0:nj * 128], AF.Exp, scale=ASCALE)
                st_free[sb_i] = ex
                if j0 + nj - 1 == qt:
                    c.wait("vector", "se", ex)
                    exv = c.opi("vector", "tensor_mul", pb[:, (nj - 1) * 128:nj * 128],
                                pb[:, (nj - 1) * 128:nj * 128], triub[:])
                    c.wait("tensor", "ve", exv)
                else:
                    c.wait("tensor", "se", ex)
                if blk == 0:
                    c.wait("tensor", "ve", max(z_free, d_free))
                insd = None
                insz = None
                for jj in range(nj):
                    last = (blk == nblk - 1 and jj == nj - 1)
                    insz = c.op("tensor", "matmul", ps_z[:, 0:128],
                                vn[:, (j0 + jj) * 128:(j0 + jj + 1) * 128],
                                pb[:, jj * 128:(jj + 1) * 128],
                                start=(blk == 0 and jj == 0), stop=last)
                    insd = c.op("tensor", "matmul", ps_d[0:1, 0:128], onescol[:],
                                pb[:, jj * 128:(jj + 1) * 128],
                                start=(blk == 0 and jj == 0), stop=last)
                pb_free = c.inc(insd, "te")
                zdone = c.inc(insz, "te")
            c.wait("vector", "te", pb_free)
            rv = c.opi("vector", "reciprocal", rn[:], ps_d[0:1, 0:128])
            d_free = rv
            c.wait("tensor", "ve", rv)
            rb = c.inc(c.op("tensor", "matmul", ps_d[:, 128:256], onesf[0:1, :],
                            rn[0:1, :], start=True, stop=True), "te")
            c.wait("vector", "te", max(rb, zdone))
            c.op("vector", "tensor_copy", rnb[:], ps_d[:, 128:256])
            zev = c.opi("vector", "tensor_tensor", zb[zo][:, qt * 128:(qt + 1) * 128],
                        ps_z[:, 0:128], rnb[:], ALU.mult)
            z_free = zev
            d_free = zev
        z_ev[zo] = z_free
        c.wait("sync", "ve", z_ev[zo])
        zst[zo] = c.dma("sync", b["zT"][h], zb[zo][:])
    c.wait("sync", "sd", c.cnt["sd"])


def emit_s8(c, b):
    wsb = c.sb("s8_w", [128, AH * D], BF16)
    zbufs = [c.sb(f"s8_z{i}", [128, AH * 128], BF16) for i in range(2)]
    obufs = [c.sb(f"s8_o{i}", [128, D], F32) for i in range(2)]
    pss = [c.ps(f"s8_p{i}", [128, 512]) for i in range(8)]
    wld = c.dma("gpsimd", wsb[:].rearrange("p (k d) -> p k d", d=D),
                b["woT"].rearrange("(k p) d -> p k d", p=128))
    c.wait("tensor", "gd", wld)
    zld = [0, 0]
    zcons = [0, 0]
    ev = [0] * 8
    ost = [0, 0]
    for tt in range(NT):
        par = tt % 2
        if tt >= 2:
            c.wait("sync", "te", zcons[par])
        zld[par] = c.dma("sync", zbufs[par][:].rearrange("p (k t) -> p k t", t=128),
                         b["zT"][:, :, tt * 128:(tt + 1) * 128].rearrange("k p t -> p k t"))
        c.wait("tensor", "sd", zld[par])
        for dc in range(4):
            pi = par * 4 + dc
            c.wait("tensor", "ve", ev[pi])
            ins = None
            for k in range(AH):
                ins = c.op("tensor", "matmul", pss[pi][:],
                           zbufs[par][:, k * 128:(k + 1) * 128],
                           wsb[:, k * D + dc * 512: k * D + (dc + 1) * 512],
                           start=(k == 0), stop=(k == AH - 1))
            mm = c.inc(ins, "te")
            if dc == 3:
                zcons[par] = mm
            c.wait("vector", "te", mm)
            if dc == 0 and tt >= 2:
                c.wait("vector", "sd", ost[par])
            ev[pi] = c.opi("vector", "tensor_copy",
                           obufs[par][:, dc * 512:(dc + 1) * 512], pss[pi][:])
        c.wait("sync", "ve", ev[par * 4 + 3])
        ost[par] = c.dma("sync", b["matt"][tt * 128:(tt + 1) * 128, :], obufs[par][:])
    c.wait("sync", "sd", c.cnt["sd"])


def emit_s10(c, b):
    wd = c.sb("s10_wd", [128, 64 * 1024], BF16)
    hb = [c.sb(f"s10_h{i}", [128, 64 * 128], BF16) for i in range(2)]
    xb = [c.sb(f"s10_x{i}", [128, 1024], F32) for i in range(2)]
    mb = [c.sb(f"s10_m{i}", [128, 1024], F32) for i in range(2)]
    ob = [c.sb(f"s10_o{i}", [128, 1024], F32) for i in range(2)]
    ps_o = [c.ps(f"s10_p{i}", [128, 512]) for i in range(4)]
    hcons = [0, 0]
    evd = [0] * 4
    ostd = [0, 0]
    for half in range(2):
        d0 = half * 1024
        if half > 0:
            c.wait("gpsimd", "te", c.cnt["te"])
        wld = c.dma("gpsimd", wd[:].rearrange("p (k d) -> p k d", d=1024),
                    b["wdT"][:, d0:d0 + 1024].rearrange("(k p) d -> p k d", p=128))
        c.wait("tensor", "gd", wld)
        for tt in range(NT):
            par = tt % 2
            t0 = tt * 128
            if half + tt >= 2:
                c.wait("sync", "te", hcons[par])
            hld = c.dma("sync", hb[par][:].rearrange("p (k t) -> p k t", t=128),
                        b["hid1"][:, :, t0:t0 + 128].rearrange("k p t -> p k t"))
            c.wait("sync", "ve", max(evd[par * 2], evd[par * 2 + 1]))
            c.dma("sync", xb[par][:], b["x1"][t0:t0 + 128, d0:d0 + 1024])
            ml = c.dma("sync", mb[par][:], b["matt"][t0:t0 + 128, d0:d0 + 1024])
            c.wait("tensor", "sd", hld)
            for dc in range(2):
                pi = par * 2 + dc
                c.wait("tensor", "ve", evd[pi])
                ins = None
                for k in range(64):
                    ins = c.op("tensor", "matmul", ps_o[pi][:],
                               hb[par][:, k * 128:(k + 1) * 128],
                               wd[:, k * 1024 + dc * 512: k * 1024 + (dc + 1) * 512],
                               start=(k == 0), stop=(k == 63))
                mmo = c.inc(ins, "te")
                if dc == 1:
                    hcons[par] = mmo
                c.wait("vector", "te", mmo)
                c.wait("vector", "sd", ml)
                if half + tt >= 2 and dc == 0:
                    c.wait("vector", "sd", ostd[par])
                c.op("vector", "tensor_add", ob[par][:, dc * 512:(dc + 1) * 512],
                     xb[par][:, dc * 512:(dc + 1) * 512],
                     mb[par][:, dc * 512:(dc + 1) * 512])
                evd[pi] = c.opi("vector", "tensor_add",
                                ob[par][:, dc * 512:(dc + 1) * 512],
                                ob[par][:, dc * 512:(dc + 1) * 512], ps_o[pi][:])
            c.wait("sync", "ve", evd[par * 2 + 1])
            ostd[par] = c.dma("sync", b["out"][t0:t0 + 128, d0:d0 + 1024], ob[par][:])
    c.wait("sync", "sd", c.cnt["sd"])
    c.wait("gpsimd", "sd", c.cnt["sd"])


EMITTERS = {
    "s0": emit_s0, "s1": emit_s1, "s2": emit_s2, "s3a": emit_s3a, "s3b": emit_s3b,
    "s4": emit_s4, "s5": emit_s5, "s6": emit_s6, "s7": emit_s7, "s8": emit_s8,
    "s9": emit_s9, "s10": emit_s10,
}


def build_program(stages):
    nc = bass.Bass("TRN2", target_bir_lowering=False)
    need_in = set()
    produced = set()
    wneed = set()
    for s in stages:
        ins, outs = STAGE_IO[s]
        for n in ins:
            if n not in produced and n != "x0":
                need_in.add(n)
        produced.update(outs)
        wneed.update(STAGE_W[s])
        if s in STAGE_X:
            wneed.update(STAGE_X[s])
    fused = len(stages) == len(STAGE_IO)
    b = {}
    for name in sorted(wneed):
        if name == "x0":
            b[name] = nc.dram_tensor(name, XIN["x0"][0], XIN["x0"][2],
                                     kind="ExternalInput").ap()
        else:
            shape, _, bdt = WEIGHTS[name]
            b[name] = nc.dram_tensor(name, shape, bdt, kind="ExternalInput").ap()
    spill_out = set()
    for name in sorted(set(SPILLS)):
        shape, _, bdt = SPILLS[name]
        if name in need_in:
            b[name] = nc.dram_tensor(name, shape, bdt, kind="ExternalInput").ap()
        elif name in produced:
            if fused:
                b[name] = nc.dram_tensor(name, shape, bdt, kind="Internal").ap()
            else:
                b[name] = nc.dram_tensor(name, shape, bdt, kind="ExternalOutput").ap()
                spill_out.add(name)
    has_out = any("out" in STAGE_IO[s][1] for s in stages)
    if has_out:
        b["out"] = nc.dram_tensor("out", XOUT["out"][0], XOUT["out"][2],
                                  kind="ExternalOutput").ap()
    with ExitStack() as es:
        c = Ctx(nc, es)
        for si, s in enumerate(stages):
            with ExitStack() as stage_es:
                c.es = stage_es
                c.prefix = s + "_"
                if si > 0:
                    c.barrier()
                EMITTERS[s](c, b)
                c.seal()
            c.es = es
    innames = sorted(need_in | (wneed & set(WEIGHTS)) | ({"x0"} if "x0" in b else set()))
    outnames = sorted(spill_out | ({"out"} if has_out else set()))
    return nc, innames, outnames


def hadamard(n):
    m = np.ones((1, 1), np.float32)
    while m.shape[0] < n:
        m = np.block([[m, m], [m, -m]])
    return m


def prep_host(inputs):
    g = {}
    g["ipT"] = np.ascontiguousarray(inputs["L0_attn_in_proj_w"].T)
    g["opT"] = np.ascontiguousarray(inputs["L0_attn_out_proj_w"].T)
    g["w1T"] = np.ascontiguousarray(inputs["L0_ffn_W1"].T)
    g["w3T"] = np.ascontiguousarray(inputs["L0_ffn_W3"].T)
    w2v = np.ascontiguousarray(inputs["L0_ffn_W2"].T).reshape(64, 128, D)
    w2p = np.empty((64, 128, D), np.float32)
    q = np.arange(128)
    for p in range(64):
        w2p[p, q[q < 64]] = w2v[q[q < 64], 2 * p]
        w2p[p, q[q >= 64]] = w2v[q[q >= 64] - 64, 2 * p + 1]
    g["w2Tp"] = w2p
    g["wkT"] = np.ascontiguousarray(np.transpose(inputs["L1_attn_W_K"], (2, 0, 1)).reshape(D, AH * ADH))
    g["wqT"] = np.ascontiguousarray(np.transpose(inputs["L1_attn_W_Q"], (2, 0, 1)).reshape(D, AH * ADH))
    g["wvT"] = np.ascontiguousarray(np.transpose(inputs["L1_attn_W_V"], (2, 0, 1)).reshape(D, AH * ADH))
    g["woT"] = np.ascontiguousarray(inputs["L1_attn_W_O"].T)
    g["wgT"] = np.ascontiguousarray(inputs["L1_ffn_w_gate"].T)
    g["wuT"] = np.ascontiguousarray(inputs["L1_ffn_w_up"].T)
    g["wdT"] = np.ascontiguousarray(inputs["L1_ffn_w_down"].T)
    g["convw"] = np.ascontiguousarray(inputs["L0_attn_conv_w"].reshape(33, 128, 4))
    g["convb"] = np.ascontiguousarray(inputs["L0_attn_conv_b"].reshape(33, 128).T)
    g["dtb"] = np.ascontiguousarray(inputs["L0_attn_dt_bias"].reshape(64, 1))
    g["alog"] = np.ascontiguousarray(inputs["L0_attn_A_log"].reshape(64, 1))
    g["dvec"] = np.ascontiguousarray(inputs["L0_attn_D"].reshape(1, 64))
    g["nw"] = np.ascontiguousarray(inputs["L0_attn_norm_weight"].reshape(32, 128).T)
    g["c_id_bf"] = np.eye(128, dtype=NPBF)
    g["c_id_f"] = np.eye(128, dtype=np.float32)
    g["c_h128"] = hadamard(128).astype(NPBF)
    h64 = hadamard(64)
    hp = np.zeros((128, 128), np.float32)
    hp[0:64, 0:64] = h64
    hp[64:128, 64:128] = h64
    g["c_hp"] = (hp / np.sqrt(np.float32(HFF))).astype(NPBF)
    tri = np.triu(np.ones((128, 128), np.float32))
    g["c_triu_f"] = tri
    g["c_triu_bf"] = tri.astype(NPBF)
    g["c_ones_f"] = np.ones((1, 128), np.float32)
    g["c_onescol_bf"] = np.ones((128, 1), NPBF)
    for k in g:
        want = WEIGHTS[k][1]
        if g[k].dtype != want:
            g[k] = np.ascontiguousarray(g[k].astype(want))
    return g


GROUPS = [["s0"], ["s1"], ["s2"], ["s3a"], ["s3b"], ["s4"], ["s5"], ["s6"],
          ["s7"], ["s8"], ["s9"], ["s10"]]
FUSED_GROUPS = [["s0", "s1", "s2", "s3a", "s3b", "s4", "s5", "s6",
                 "s7", "s8", "s9", "s10"]]


def kernel(**inputs):
    x0 = np.ascontiguousarray(inputs["L0_attn_hidden_states"].astype(np.float32))
    ncores = 8
    g = prep_host(inputs)
    state = [dict() for _ in range(ncores)]
    for ci in range(ncores):
        state[ci]["x0"] = np.ascontiguousarray(x0[ci])
    out = np.zeros((ncores, T, D), np.float32)
    for grp in FUSED_GROUPS:
        nc, innames, outnames = build_program(grp)
        in_maps = []
        for ci in range(ncores):
            m = {}
            for n in innames:
                m[n] = g[n] if n in g else state[ci][n]
            in_maps.append(m)
        res = run_bass_kernel_spmd(nc, in_maps, core_ids=list(range(ncores)))
        for ci in range(ncores):
            for n in outnames:
                state[ci][n] = res.results[ci][n]
    for ci in range(ncores):
        out[ci] = np.asarray(state[ci]["out"], np.float32)
    return out


# version 06
# speedup vs baseline: 1.1905x; 1.0176x over previous
import numpy as np
import ml_dtypes
from contextlib import ExitStack

import concourse.bass as bass
import concourse.mybir as mybir
from concourse.bass_utils import run_bass_kernel_spmd

F32 = mybir.dt.float32
BF16 = mybir.dt.bfloat16
AF = mybir.ActivationFunctionType
ALU = mybir.AluOpType
AX = mybir.AxisListType
NPBF = ml_dtypes.bfloat16

T = 2048
D = 2048
NT = 16
INTER = 4096
HEADS = 64
PD = 64
ND = 64
Q = 128
NCH = 16
HFF = 8192
AH = 16
ADH = 128
EPS_LN = 1e-5
EPS_MN = 1e-6
ASCALE = float(1.0 / np.sqrt(np.float32(ADH)))

ENGS = ("tensor", "vector", "scalar", "gpsimd", "sync")
QSEM = {"sync": "sd", "scalar": "ad", "gpsimd": "gd"}
ESEM = {"tensor": "te", "vector": "ve", "scalar": "se", "gpsimd": "ge"}


class Ctx:
    def __init__(self, nc, es):
        self.nc = nc
        self.es = es
        self.sems = {}
        self.cnt = {}
        self.last = {}
        self.lastinc = {}
        self.prefix = ""
        for name in ("sd", "ad", "gd", "te", "ve", "se", "ge"):
            self.sems[name] = es.enter_context(nc.semaphore(name))
            self.cnt[name] = 0

    def sb(self, name, shape, dtype):
        return self.es.enter_context(self.nc.sbuf_tensor(self.prefix + name, shape, dtype))

    def ps(self, name, shape, dtype=F32):
        return self.es.enter_context(self.nc.psum_tensor(self.prefix + name, shape, dtype))

    def eng(self, name):
        return getattr(self.nc, name)

    def inc(self, instr, sem, amt=1):
        instr.then_inc(self.sems[sem], amt)
        self.cnt[sem] += amt
        for e, s in ESEM.items():
            if s == sem and instr is self.last.get(e):
                self.lastinc[e] = True
        return self.cnt[sem]

    def wait(self, engine, sem, val):
        if val > 0:
            self.eng(engine).wait_ge(self.sems[sem], val)

    def dma(self, q, out, in_, inc=True, **kw):
        instr = self.eng(q).dma_start(out=out, in_=in_, **kw)
        self.last[q] = instr
        if inc:
            return self.inc(instr, QSEM[q], 16)
        return self.cnt[QSEM[q]]

    def op(self, engine, fname, *args, **kw):
        instr = getattr(self.eng(engine), fname)(*args, **kw)
        self.last[engine] = instr
        self.lastinc[engine] = False
        return instr

    def opi(self, engine, fname, *args, **kw):
        instr = self.op(engine, fname, *args, **kw)
        return self.inc(instr, ESEM[engine])

    def barrier(self):
        targets = [(s, self.cnt[s]) for s in ("sd", "ad", "gd", "te", "ve", "se", "ge")
                   if self.cnt[s] > 0]
        for e in ENGS:
            own = ESEM.get(e)
            for s, v in targets:
                if s == own:
                    continue
                self.eng(e).wait_ge(self.sems[s], v)

    def seal(self):
        for e in ("tensor", "vector", "scalar", "gpsimd"):
            ins = self.last.get(e)
            if ins is not None and not self.lastinc.get(e, True):
                self.inc(ins, ESEM[e])
            self.last[e] = None


WEIGHTS = {
    "ipT": ([D, 8384], np.float32, F32),
    "opT": ([INTER, D], np.float32, F32),
    "w1T": ([D, HFF], np.float32, F32),
    "w3T": ([D, HFF], np.float32, F32),
    "w2Tp": ([64, 128, D], np.float32, F32),
    "wkT": ([D, AH * ADH], np.float32, F32),
    "wqT": ([D, AH * ADH], np.float32, F32),
    "wvT": ([D, AH * ADH], np.float32, F32),
    "woT": ([AH * ADH, D], np.float32, F32),
    "wgT": ([D, HFF], np.float32, F32),
    "wuT": ([D, HFF], np.float32, F32),
    "wdT": ([HFF, D], np.float32, F32),
    "convw": ([33, 128, 4], np.float32, F32),
    "convb": ([128, 33], np.float32, F32),
    "dtb": ([64, 1], np.float32, F32),
    "alog": ([64, 1], np.float32, F32),
    "dvec": ([1, 64], np.float32, F32),
    "nw": ([128, 32], np.float32, F32),
    "c_id_bf": ([128, 128], NPBF, BF16),
    "c_id_f": ([128, 128], np.float32, F32),
    "c_h128": ([128, 128], NPBF, BF16),
    "c_hp": ([128, 128], NPBF, BF16),
    "c_triu_f": ([128, 128], np.float32, F32),
    "c_triu_bf": ([128, 128], NPBF, BF16),
    "c_ones_f": ([1, 128], np.float32, F32),
    "c_onescol_bf": ([128, 1], NPBF, BF16),
}
SPILLS = {
    "xn0": ([T, D], NPBF, BF16),
    "gateT": ([INTER, T], NPBF, BF16),
    "xconvT": ([INTER, T], NPBF, BF16),
    "bcT": ([64, 2 * T], NPBF, BF16),
    "csT": ([64, T], np.float32, F32),
    "dtT": ([64, T], np.float32, F32),
    "yfT": ([INTER, T], NPBF, BF16),
    "mham": ([T, D], np.float32, F32),
    "vsp": ([64, 128, T], NPBF, BF16),
    "x1": ([T, D], np.float32, F32),
    "xn1": ([T, D], NPBF, BF16),
    "zT": ([AH, ADH, T], NPBF, BF16),
    "matt": ([T, D], NPBF, BF16),
    "hid1": ([64, 128, T], NPBF, BF16),
}
XIN = {"x0": ([T, D], np.float32, F32)}
XOUT = {"out": ([T, D], np.float32, F32)}

STAGE_IO = {
    "s0": (["x0"], ["xn0"]),
    "s1": (["xn0"], ["gateT"]),
    "s2": (["xn0"], ["xconvT", "bcT", "csT", "dtT"]),
    "s3a": (["xconvT", "bcT", "csT", "dtT", "gateT"], ["yfT", "mham"]),
    "s4": (["xn0"], ["vsp"]),
    "s5": (["vsp", "x0", "mham"], ["x1"]),
    "s6": (["x1"], ["xn1"]),
    "s7": (["xn1"], ["zT"]),
    "s8": (["zT"], ["matt"]),
    "s9": (["xn1"], ["hid1"]),
    "s10": (["hid1", "x1", "matt"], ["out"]),
}
STAGE_W = {
    "s0": [],
    "s1": ["ipT"],
    "s2": ["ipT", "convw", "convb", "dtb", "alog"],
    "s3a": ["dvec", "c_ones_f", "c_onescol_bf", "c_id_bf", "c_id_f", "c_triu_f",
            "opT", "nw"],
    "s4": ["w1T", "w3T", "c_h128"],
    "s5": ["w2Tp", "c_hp"],
    "s6": [],
    "s7": ["wkT", "wqT", "wvT", "c_onescol_bf", "c_ones_f", "c_triu_bf"],
    "s8": ["woT"],
    "s9": ["wgT", "wuT"],
    "s10": ["wdT"],
}
STAGE_X = {"s0": ["x0"], "s5": ["x0"]}


def load_xnT(c, src_dram, name):
    xnT = c.sb(name, [128, NT * T], BF16)
    v = 0
    for dt_i in range(NT):
        v = c.dma("sync", xnT[:, dt_i * T:(dt_i + 1) * T],
                  src_dram[:, dt_i * 128:(dt_i + 1) * 128], transpose=True)

    def tile(dt_i, lo, hi):
        return xnT[:, dt_i * T + lo: dt_i * T + hi]
    return tile, v


def emit_ln(c, b, src, dst):
    nbuf = 3
    xts = [c.sb(f"ln_x{i}", [128, D], F32) for i in range(nbuf)]
    scr = c.sb("ln_scr", [128, D], F32)
    xnb = [c.sb(f"ln_o{i}", [128, D], BF16) for i in range(2)]
    st = c.sb("ln_st", [128, 8], F32)
    se_done = [0] * nbuf
    out_done = [0, 0]
    for tt in range(NT):
        bi = tt % nbuf
        oi = tt % 2
        if tt >= nbuf:
            c.wait("sync", "ve", se_done[bi])
        ld = c.dma("sync", xts[bi][:], src[tt * 128:(tt + 1) * 128, :])
        c.wait("scalar", "sd", ld)
        c.op("scalar", "activation", scr[:], xts[bi][:], AF.Square,
             accum_out=st[:, 1:2])
        sq = c.opi("scalar", "activation", scr[:], xts[bi][:], AF.Copy,
                   accum_out=st[:, 0:1])
        c.wait("vector", "se", sq)
        c.op("vector", "tensor_scalar", st[:, 2:3], st[:, 0:1], 1.0 / D, None, ALU.mult)
        c.op("vector", "tensor_scalar", st[:, 3:4], st[:, 1:2], 1.0 / D, None, ALU.mult)
        c.op("vector", "tensor_mul", st[:, 6:7], st[:, 2:3], st[:, 2:3])
        c.op("vector", "tensor_sub", st[:, 3:4], st[:, 3:4], st[:, 6:7])
        vv = c.opi("vector", "tensor_scalar", st[:, 3:4], st[:, 3:4], EPS_LN, None, ALU.add)
        c.wait("scalar", "ve", vv)
        c.op("scalar", "activation", st[:, 4:5], st[:, 3:4], AF.Ln)
        ss = c.opi("scalar", "activation", st[:, 4:5], st[:, 4:5], AF.Exp, scale=-0.5)
        c.wait("vector", "se", ss)
        if tt >= 2:
            c.wait("vector", "sd", out_done[oi])
        se_done[bi] = c.opi("vector", "tensor_scalar", xnb[oi][:], xts[bi][:],
                            st[:, 2:3], st[:, 4:5], ALU.subtract, op1=ALU.mult)
        c.wait("sync", "ve", se_done[bi])
        out_done[oi] = c.dma("sync", dst[tt * 128:(tt + 1) * 128, :], xnb[oi][:])
    c.wait("sync", "sd", c.cnt["sd"])


def emit_colproj(c, b, wname, wcols, dst, act, nchtiles, xn_src="xn0"):
    xnt, xn_rdy = load_xnT(c, b[xn_src], "cp_xnT")
    wbufs = [c.sb(f"cp_w{i}", [128, NT * 128], BF16) for i in range(2)]
    obufs = [c.sb(f"cp_o{i}", [128, 512], BF16) for i in range(4)]
    pss = [c.ps(f"cp_p{i}", [128, 512]) for i in range(4)]
    wsrc = b[wname].rearrange("(dt p) ch -> p dt ch", p=128)
    wld = [0, 0]
    wdone = [0, 0]
    ev_done = [0] * 4
    ob_done = [0] * 4
    c.wait("tensor", "sd", xn_rdy)
    for ct in range(nchtiles):
        wb = ct % 2
        if ct >= 2:
            c.wait("gpsimd", "te", wdone[wb])
        wld[wb] = c.dma("gpsimd",
                        wbufs[wb][:].rearrange("p (dt ch) -> p dt ch", ch=128),
                        wsrc[:, :, wcols + ct * 128: wcols + (ct + 1) * 128])
        c.wait("tensor", "gd", wld[wb])
        for tc in range(4):
            pi = (ct * 4 + tc) % 4
            c.wait("tensor", "se", ev_done[pi])
            for dt_i in range(NT):
                ins = c.op("tensor", "matmul", pss[pi][:],
                           wbufs[wb][:, dt_i * 128:(dt_i + 1) * 128],
                           xnt(dt_i, tc * 512, (tc + 1) * 512),
                           start=(dt_i == 0), stop=(dt_i == NT - 1))
            mm = c.inc(ins, "te")
            if tc == 3:
                wdone[wb] = mm
            c.wait("scalar", "te", mm)
            c.wait("scalar", "sd", ob_done[pi])
            ev_done[pi] = c.opi("scalar", "activation", obufs[pi][:], pss[pi][:], act)
            c.wait("sync", "se", ev_done[pi])
            ob_done[pi] = c.dma("sync",
                                dst[ct * 128:(ct + 1) * 128, tc * 512:(tc + 1) * 512],
                                obufs[pi][:])
    c.wait("sync", "sd", c.cnt["sd"])


def emit_s0(c, b):
    emit_ln(c, b, b["x0"], b["xn0"])


def emit_s6(c, b):
    emit_ln(c, b, b["x1"], b["xn1"])


def emit_s1(c, b):
    emit_colproj(c, b, "ipT", 0, b["gateT"], AF.Silu, 32)


def emit_s2(c, b):
    xnt, xn_rdy = load_xnT(c, b["xn0"], "s2_xnT")
    wbufs = [c.sb(f"s2_w{i}", [128, NT * 128], BF16) for i in range(2)]
    cibufs = [c.sb(f"s2_ci{i}", [128, 4 + T], BF16) for i in range(2)]
    cvf = c.sb("s2_cvf", [128, T], F32)
    cvo = [c.sb(f"s2_cvo{i}", [128, T], BF16) for i in range(2)]
    cw = c.sb("s2_cw", [128, 33 * 4], F32)
    cb = c.sb("s2_cb", [128, 33], F32)
    pss = [c.ps(f"s2_p{i}", [128, 512]) for i in range(4)]
    wsrc = b["ipT"].rearrange("(dt p) ch -> p dt ch", p=128)

    c.dma("sync", cw[:].rearrange("p (k c) -> p k c", c=4),
          b["convw"].rearrange("k p c -> p k c"))
    c.dma("sync", cb[:], b["convb"])
    c.opi("vector", "memset", cibufs[0][:, 0:4], 0)
    z1 = c.opi("vector", "memset", cibufs[1][:, 0:4], 0)

    c.wait("tensor", "sd", xn_rdy)
    wld = [0, 0]
    wdone = [0, 0]
    ev_done = [0] * 4
    ci_done = [0, 0]
    cv_done = [0, 0]
    cvs_done = [0, 0]
    for ct in range(33):
        wb = ct % 2
        if ct >= 2:
            c.wait("gpsimd", "te", wdone[wb])
        wld[wb] = c.dma("gpsimd",
                        wbufs[wb][:].rearrange("p (dt ch) -> p dt ch", ch=128),
                        wsrc[:, :, 4096 + ct * 128: 4096 + (ct + 1) * 128])
        c.wait("tensor", "gd", wld[wb])
        cib = ct % 2
        for tc in range(4):
            pi = tc
            c.wait("tensor", "se", ev_done[pi])
            for dt_i in range(NT):
                ins = c.op("tensor", "matmul", pss[pi][:],
                           wbufs[wb][:, dt_i * 128:(dt_i + 1) * 128],
                           xnt(dt_i, tc * 512, (tc + 1) * 512),
                           start=(dt_i == 0), stop=(dt_i == NT - 1))
            mm = c.inc(ins, "te")
            if tc == 3:
                wdone[wb] = mm
            c.wait("scalar", "te", mm)
            if ct >= 2 and tc == 0:
                c.wait("scalar", "ve", ci_done[cib])
            ev_done[pi] = c.opi("scalar", "activation",
                                cibufs[cib][:, 4 + tc * 512: 4 + (tc + 1) * 512],
                                pss[pi][:], AF.Copy)
        c.wait("vector", "se", ev_done[3])
        if ct >= 2:
            c.wait("vector", "se", cv_done[cib])
        c.op("vector", "tensor_scalar", cvf[:], cibufs[cib][:, 1:1 + T],
             cw[:, 4 * ct: 4 * ct + 1], cb[:, ct:ct + 1], ALU.mult, op1=ALU.add)
        ins = None
        for k in range(1, 4):
            ins = c.op("vector", "scalar_tensor_tensor", cvf[:],
                       cibufs[cib][:, 1 + k:1 + k + T],
                       cw[:, 4 * ct + k: 4 * ct + k + 1], cvf[:], ALU.mult, ALU.add)
        ci_done[cib] = c.inc(ins, "ve")
        co = ct % 2
        c.wait("scalar", "ve", ci_done[cib])
        if ct >= 2:
            c.wait("scalar", "sd", cvs_done[co])
        cv_done[co] = c.opi("scalar", "activation", cvo[co][:], cvf[:], AF.Silu)
        c.wait("sync", "se", cv_done[co])
        if ct < 32:
            cvs_done[co] = c.dma("sync", b["xconvT"][ct * 128:(ct + 1) * 128, :],
                                 cvo[co][:])
        else:
            c.dma("sync", b["bcT"][:, 0:T], cvo[co][0:64, :])
            cvs_done[co] = c.dma("sync", b["bcT"][:, T:2 * T], cvo[co][64:128, :])

    dwb = c.sb("s2_dw", [128, NT * 64], BF16)
    dtraw = c.sb("s2_dtraw", [64, T], F32)
    sc1 = c.sb("s2_sc1", [64, T], F32)
    sc2 = c.sb("s2_sc2", [64, T], F32)
    dts = c.sb("s2_dts", [64, T], F32)
    csb = c.sb("s2_csb", [64, T], F32)
    smal = c.sb("s2_smal", [64, 2], F32)
    dld = c.dma("gpsimd", dwb[:].rearrange("p (dt ch) -> p dt ch", ch=64),
                wsrc[:, :, 8320:8384])
    c.dma("sync", smal[:, 0:1], b["dtb"])
    bld = c.dma("sync", smal[:, 1:2], b["alog"])
    c.wait("scalar", "sd", bld)
    na = c.opi("scalar", "activation", smal[:, 1:2], smal[:, 1:2], AF.Exp)
    c.wait("vector", "se", na)
    nv = c.opi("vector", "tensor_scalar", smal[:, 1:2], smal[:, 1:2], -1.0, None, ALU.mult)
    c.wait("tensor", "gd", dld)
    for tc in range(4):
        c.wait("tensor", "se", ev_done[tc])
        for dt_i in range(NT):
            ins = c.op("tensor", "matmul", pss[tc][0:64, :],
                       dwb[:, dt_i * 64:(dt_i + 1) * 64],
                       xnt(dt_i, tc * 512, (tc + 1) * 512),
                       start=(dt_i == 0), stop=(dt_i == NT - 1))
        mm = c.inc(ins, "te")
        c.wait("vector", "te", mm)
        c.op("vector", "tensor_copy", dtraw[:, tc * 512:(tc + 1) * 512], pss[tc][0:64, :])
    av = c.opi("vector", "tensor_scalar", sc1[:], dtraw[:], smal[:, 0:1], None, ALU.add)
    c.wait("scalar", "ve", av)
    c.op("scalar", "activation", sc2[:], sc1[:], AF.Abs)
    c.op("scalar", "activation", sc2[:], sc2[:], AF.Exp, scale=-1.0)
    ee = c.opi("scalar", "activation", sc2[:], sc2[:], AF.Ln, bias=1.0)
    c.wait("vector", "se", ee)
    c.op("vector", "tensor_scalar", dts[:], sc1[:], 0.0, None, ALU.max)
    dt_ok = c.opi("vector", "tensor_add", dts[:], dts[:], sc2[:])
    c.wait("sync", "ve", dt_ok)
    c.dma("sync", b["dtT"], dts[:])
    c.op("vector", "tensor_scalar", sc1[:], dts[:], smal[:, 1:2], None, ALU.mult)
    ins = None
    for ch in range(NCH):
        ins = c.op("vector", "tensor_tensor_scan", csb[:, ch * Q:(ch + 1) * Q],
                   sc1[:, ch * Q:(ch + 1) * Q], sc1[:, ch * Q:(ch + 1) * Q],
                   0.0, ALU.add, ALU.bypass)
    cs_ok = c.inc(ins, "ve")
    c.wait("sync", "ve", cs_ok)
    c.dma("sync", b["csT"], csb[:])
    c.wait("sync", "sd", c.cnt["sd"])


def emit_s3a(c, b):
    bc = c.sb("bc", [64, 2 * T], BF16)
    dexp = c.sb("dexp", [128, 64], F32)
    ones = c.sb("ones", [1, 128], F32)
    onecb = c.sb("onecb", [128, 1], BF16)
    idbf = c.sb("idbf", [128, 128], BF16)
    idf = c.sb("idf", [128, 128], F32)
    triu = c.sb("triu", [128, 128], F32)
    drow = c.sb("drow", [1, 64], F32)
    nw = c.sb("nw", [128, 32], F32)
    wsb = c.sb("wsb", [128, 32 * 1024], BF16)
    ob = c.sb("ob", [128, 1024], F32)
    c.dma("sync", bc[:], b["bcT"])
    c.dma("sync", ones[:], b["c_ones_f"])
    c.dma("sync", onecb[:], b["c_onescol_bf"])
    c.dma("sync", idbf[:], b["c_id_bf"])
    c.dma("sync", idf[:], b["c_id_f"])
    c.dma("sync", triu[:], b["c_triu_f"])
    c.dma("sync", nw[:], b["nw"])
    v = c.dma("sync", drow[:], b["dvec"])
    wsrc = b["opT"].rearrange("(k p) d -> p k d", p=128)
    wld = c.dma("gpsimd", wsb[:].rearrange("p (k d) -> p k d", d=1024),
                wsrc[:, :, 0:1024])
    p_bc = c.ps("pbc", [128, 1024])
    p_A = c.ps("pA", [128, 512])
    p_B = c.ps("pB", [128, 512])
    p_st = c.ps("pst", [128, 512])
    p_m = c.ps("pm", [128, 512])
    p_bf = c.ps("pbf", [128, 512])
    xnat = [c.sb(f"xn{i}", [128, INTER], BF16) for i in range(2)]
    xtil = c.sb("xtil", [128, INTER], BF16)
    xdd = c.sb("xdd", [128, INTER], BF16)
    gbuf = c.sb("g0", [128, INTER], BF16)
    csn = c.sb("csn", [128, 64], F32)
    dtn = c.sb("dtn", [128, 64], F32)
    csrow = c.sb("csrow", [1, 1024], F32)
    cschk = c.sb("cschk", [64, 128], F32)
    dtchk = c.sb("dtchk", [64, 128], F32)
    csrT = c.sb("csrT", [1, 64], F32)
    ebcT = c.sb("ebcT", [64, 64], F32)
    dltT = c.sb("dltT", [64, 128], F32)
    dln = c.sb("dln", [128, 64], F32)
    bnat = c.sb("bnat", [128, 64], BF16)
    bhat = c.sb("bhat", [128, 512], BF16)
    mtp = c.sb("mtp", [128, 128], BF16)
    rall = c.sb("rall", [128, 1024], F32)
    lall = c.sb("lall", [128, 1024], F32)
    mall = c.sb("mall", [128, 8 * 128], BF16)
    epair = c.sb("ep", [128, 512], F32)
    yt = c.sb("yt", [128, INTER], F32)
    tei = c.sb("tei", [128, 512], BF16)
    s_sb = [c.sb(f"s{i}", [64, HEADS * PD], F32) for i in range(2)]
    stmp = c.sb("stmp", [64, 512], F32)
    sbfg = c.sb("sbfg", [64, 512], BF16)
    rstd = c.sb("rstd", [1, 128], F32)
    rbc = c.sb("rbc", [128, 128], F32)
    yfb = [c.sb(f"yf{i}", [128, INTER], BF16) for i in range(2)]

    c.opi("vector", "memset", s_sb[0][:], 0)
    c.wait("vector", "gd", wld)
    ins = None
    for k in range(32):
        ins = c.op("vector", "tensor_scalar", wsb[:, k * 1024:(k + 1) * 1024],
                   wsb[:, k * 1024:(k + 1) * 1024], nw[:, k:k + 1], None, ALU.mult)
    wok = c.inc(ins, "ve")
    c.wait("tensor", "sd", v)
    t0_ = c.inc(c.op("tensor", "matmul", p_m[:, 0:64], ones[0:1, :], drow[0:1, :],
                     start=True, stop=True), "te")
    c.wait("vector", "te", t0_)
    pm_free = c.opi("vector", "tensor_copy", dexp[:], p_m[:, 0:64])

    xl = [0, 0]
    gl = 0
    ymm_prev = 0
    ymm_prev_g = 0
    var_te = 0
    mv_prev = 0
    st1_prev = 0
    yAB_prev = 0
    y2_prev = 0
    xcons = [0, 0]
    gcons = 0
    crow_cons = 0
    chk_cons = 0
    chk_scons = 0
    bn_free = 0
    bc_free = 0
    yAB_free = 0
    st_free = 0
    yst = [0, 0]
    yf_ve = [0, 0]
    s3b_te = [0, 0]
    s3b_ev = [0, 0]
    ob_ev = 0
    ob_st = 0
    for ch in range(NCH):
        par = ch % 2
        t0, t1 = ch * Q, (ch + 1) * Q
        if ch >= 2:
            c.wait("sync", "te", xcons[par])
        xl[par] = c.dma("sync", xnat[par][:], b["xconvT"][:, t0:t1], transpose=True)
        c.wait("sync", "ve", gcons)
        gl = c.dma("sync", gbuf[:].rearrange("p (k t) -> p k t", t=Q),
                   b["gateT"].rearrange("(k p) t -> p k t", p=128)[:, :, t0:t1])
        c.wait("sync", "te", chk_cons)
        c.wait("sync", "ve", chk_scons)
        c.dma("sync", cschk[:], b["csT"][:, t0:t1])
        ckl = c.dma("sync", dtchk[:], b["dtT"][:, t0:t1])
        c.wait("tensor", "sd", ckl)
        c.wait("tensor", "ve", pm_free)
        c.op("tensor", "transpose", p_m[:, 0:64], cschk[:], idf[0:64, 0:64])
        c.op("tensor", "transpose", p_m[:, 64:128], dtchk[:], idf[0:64, 0:64])
        tr1 = c.inc(c.op("tensor", "transpose", p_m[0:1, 128:192],
                         cschk[:, 127:128], idf[0:64, 0:64]), "te")
        c.wait("vector", "te", tr1)
        c.op("vector", "tensor_copy", csn[:], p_m[:, 0:64])
        c.op("vector", "tensor_copy", dtn[:], p_m[:, 64:128])
        pm_free = c.opi("vector", "tensor_copy", csrT[:], p_m[0:1, 128:192])
        c.wait("tensor", "ve", pm_free)
        tb = c.inc(c.op("tensor", "matmul", p_m[0:64, 192:256], ones[0:1, 0:64],
                        csrT[0:1, :], start=True, stop=True), "te")
        c.wait("scalar", "te", tb)
        c.wait("scalar", "ve", st1_prev)
        eb = c.opi("scalar", "activation", ebcT[:], p_m[0:64, 192:256], AF.Exp)
        dl1 = c.opi("vector", "tensor_scalar", dltT[:], cschk[:],
                    cschk[:, 127:128], None, ALU.subtract)
        c.wait("scalar", "ve", dl1)
        dl2 = c.opi("scalar", "activation", dltT[:], dltT[:], AF.Exp, scale=-1.0)
        c.wait("vector", "se", dl2)
        dl3 = c.opi("vector", "tensor_mul", dltT[:], dltT[:], dtchk[:])
        chk_scons = dl3
        c.wait("tensor", "ve", dl3)
        c.wait("tensor", "se", eb)
        tr2 = c.inc(c.op("tensor", "transpose", p_m[:, 256:320], dltT[:],
                         idf[0:64, 0:64]), "te")
        chk_cons = tr2
        c.wait("vector", "te", tr2)
        pm_free = c.opi("vector", "tensor_copy", dln[:], p_m[:, 256:320])
        c.wait("tensor", "ve", max(bn_free, s3b_ev[0], s3b_ev[1]))
        trb = c.inc(c.op("tensor", "transpose", p_bf[:, 0:32].bitcast(BF16),
                         bc[:, t0:t1], idbf[0:64, 0:64]), "te")
        c.wait("vector", "te", trb)
        bn_free = c.opi("vector", "tensor_copy", bnat[:], p_bf[:, 0:32].bitcast(BF16))
        c.wait("tensor", "ve", pm_free)
        mt1 = c.inc(c.op("tensor", "matmul", p_m[:, 320:448], bc[:, t0:t1],
                         bc[:, T + t0:T + t1], start=True, stop=True), "te")
        c.wait("vector", "te", mt1)
        pm_free = c.opi("vector", "tensor_mul", mtp[:], p_m[:, 320:448], triu[:])
        mt2 = pm_free
        c.wait("vector", "sd", xl[par])
        c.wait("vector", "te", ymm_prev)
        xt1 = c.opi("vector", "tensor_tensor",
                    xtil[:].rearrange("p (h q) -> p h q", q=PD),
                    xnat[par][:].rearrange("p (h q) -> p h q", q=PD),
                    dtn[:].unsqueeze(2).broadcast_to([128, HEADS, PD]), ALU.mult)
        c.wait("vector", "te", var_te)
        xd1 = c.opi("vector", "tensor_tensor",
                    xdd[:].rearrange("p (h q) -> p h q", q=PD),
                    xnat[par][:].rearrange("p (h q) -> p h q", q=PD),
                    dexp[:, 0:HEADS].unsqueeze(2).broadcast_to([128, HEADS, PD]), ALU.mult)
        sprev = s_sb[ch % 2]
        snew = s_sb[(ch + 1) % 2]
        yt_guard = y2_prev
        if ch >= 2:
            tt = ch - 2
            tpar = tt % 2
            c.wait("tensor", "ve", max(yf_ve[tpar], wok, bn_free))
            for dc in range(2):
                c.wait("tensor", "ve", max(s3b_ev[0], s3b_ev[1]))
                ins = None
                for k in range(32):
                    ins = c.op("tensor", "matmul", p_bf[:, 0:512],
                               yfb[tpar][:, k * 128:(k + 1) * 128],
                               wsb[:, k * 1024 + dc * 512: k * 1024 + (dc + 1) * 512],
                               start=(k == 0), stop=(k == 31))
                mm = c.inc(ins, "te")
                if dc == 1:
                    s3b_te[tpar] = mm
                c.wait("vector", "te", mm)
                if dc == 0:
                    c.wait("vector", "sd", ob_st)
                s3b_ev[dc] = c.opi("vector", "tensor_copy",
                                   ob[:, dc * 512:(dc + 1) * 512], p_bf[:, 0:512])
            c.wait("sync", "ve", s3b_ev[1])
            ob_st = c.dma("sync", b["mham"][tt * 128:(tt + 1) * 128, 0:1024], ob[:])
        for g in range(8):
            h0 = g * 8
            c.wait("sync", "te", crow_cons)
            crl = c.dma("sync", csrow[:].rearrange("o (h t) -> o h t", t=Q),
                        b["csT"][h0:h0 + 8, t0:t1].unsqueeze(0))
            c.wait("tensor", "se", bc_free)
            c.wait("tensor", "sd", crl)
            c.op("tensor", "matmul", p_bc[:, 0:512], ones[0:1, :],
                 csrow[0:1, 0:512], start=True, stop=True)
            bcv = c.inc(c.op("tensor", "matmul", p_bc[:, 512:1024], ones[0:1, :],
                             csrow[0:1, 512:1024], start=True, stop=True), "te")
            crow_cons = bcv
            c.wait("scalar", "te", bcv)
            c.wait("scalar", "ve", yAB_prev)
            ins = None
            for hh in range(8):
                ins = c.op("scalar", "activation", rall[:, hh * Q:(hh + 1) * Q],
                           p_bc[:, hh * Q:(hh + 1) * Q], AF.Relu, scale=-1.0,
                           bias=csn[:, h0 + hh: h0 + hh + 1])
            c.inc(ins, "se")
            c.op("scalar", "activation",
                 epair[0:64, :].rearrange("p (k t) -> p k t", t=Q),
                 p_bc[0:64, :].rearrange("p (hh t) -> p hh t", t=Q)[:, 0:8:2, :],
                 AF.Exp)
            epv = c.inc(c.op("scalar", "activation",
                             epair[64:128, :].rearrange("p (k t) -> p k t", t=Q),
                             p_bc[64:128, :].rearrange("p (hh t) -> p hh t", t=Q)[:, 1:8:2, :],
                             AF.Exp), "se")
            c.wait("scalar", "ve", mv_prev)
            lv = c.opi("scalar", "activation", lall[:], rall[:], AF.Exp, scale=-1.0)
            bc_free = lv
            c.wait("vector", "se", lv)
            c.wait("vector", "ve", mt2)
            c.wait("vector", "te", ymm_prev_g)
            mv = c.opi("vector", "tensor_tensor",
                       mall[:].rearrange("p (hh t) -> p hh t", t=Q),
                       lall[:].rearrange("p (hh t) -> p hh t", t=Q),
                       mtp[:].unsqueeze(1).broadcast_to([128, 8, Q]), ALU.mult)
            c.wait("vector", "te", trb)
            c.wait("vector", "se", eb)
            bh1 = c.opi("vector", "tensor_tensor",
                        bhat[:].rearrange("p (h n) -> p h n", n=ND),
                        bnat[:].unsqueeze(1).broadcast_to([128, 8, ND]),
                        dln[:, h0:h0 + 8].unsqueeze(2).broadcast_to([128, 8, ND]),
                        ALU.mult)
            st1 = c.opi("vector", "tensor_tensor",
                        stmp[:].rearrange("n (h q) -> n h q", q=PD),
                        sprev[:, h0 * PD:(h0 + 8) * PD].rearrange("n (h q) -> n h q", q=PD),
                        ebcT[:, h0:h0 + 8].unsqueeze(2).broadcast_to([64, 8, PD]),
                        ALU.mult)
            st1 = c.opi("vector", "tensor_copy", sbfg[:],
                        sprev[:, h0 * PD:(h0 + 8) * PD])
            st1_prev = st1
            c.wait("tensor", "ve", mv)
            c.wait("tensor", "ve", max(xt1, xd1, bh1, st1, yAB_free, st_free))
            insA = insB = None
            for kk in range(4):
                for e in range(2):
                    h = h0 + 2 * kk + e
                    tp = (0, 64 * e)
                    rows = slice(64 * e, 64 * e + 64)
                    c.op("tensor", "matmul", p_A[rows, kk * 128:(kk + 1) * 128],
                         xtil[:, h * PD:(h + 1) * PD],
                         mall[:, (2 * kk + e) * Q:(2 * kk + e + 1) * Q],
                         start=True, stop=False, tile_position=tp)
                    insA = c.op("tensor", "matmul", p_A[rows, kk * 128:(kk + 1) * 128],
                                xdd[:, h * PD:(h + 1) * PD], idbf[:],
                                start=False, stop=True, tile_position=tp)
                    insB = c.op("tensor", "matmul", p_B[rows, kk * 128:(kk + 1) * 128],
                                sbfg[:, (2 * kk + e) * PD:(2 * kk + e + 1) * PD],
                                bc[:, T + t0:T + t1],
                                start=True, stop=True, tile_position=tp)
            c.inc(insA, "te")
            yB = c.inc(insB, "te")
            mv_prev = mv
            inss = None
            for hh in range(8):
                h = h0 + hh
                inss = c.op("tensor", "matmul", p_st[0:64, hh * 64:(hh + 1) * 64],
                            bhat[:, hh * ND:(hh + 1) * ND],
                            xnat[par][:, h * PD:(h + 1) * PD],
                            start=True, stop=True)
            yS = c.inc(inss, "te")
            ymm_prev_g = yS
            if g == 7:
                xcons[par] = yS
                ymm_prev = yS
            c.wait("vector", "te", yS)
            c.wait("vector", "se", epv)
            c.wait("vector", "se", yt_guard)
            c.op("vector", "tensor_tensor", tei[:], p_B[:, 0:512],
                 epair[:], ALU.mult)
            yAB_free = c.opi("vector", "tensor_tensor", yt[:, h0 * PD:(h0 + 8) * PD],
                             p_A[:, 0:512], tei[:], ALU.add)
            yAB_prev = yAB_free
            st_free = c.opi("vector", "tensor_add", snew[:, h0 * PD:(h0 + 8) * PD],
                            stmp[:], p_st[0:64, :])
        c.wait("scalar", "ve", yAB_free)
        y2v = c.opi("scalar", "activation", xdd[:], yt[:], AF.Square)
        y2_prev = y2v
        c.wait("tensor", "se", y2v)
        c.wait("tensor", "ve", pm_free)
        insv = None
        for kk in range(32):
            insv = c.op("tensor", "matmul", p_m[0:1, 0:128], onecb[:],
                        xdd[:, kk * 128:(kk + 1) * 128], start=(kk == 0), stop=(kk == 31))
        vv = c.inc(insv, "te")
        var_te = vv
        c.wait("vector", "te", vv)
        r1 = c.opi("vector", "tensor_scalar", rstd[:], p_m[0:1, 0:128],
                   1.0 / INTER, EPS_MN, ALU.mult, op1=ALU.add)
        c.wait("scalar", "ve", r1)
        c.op("scalar", "activation", rstd[:], rstd[:], AF.Ln)
        r2 = c.opi("scalar", "activation", rstd[:], rstd[:], AF.Exp, scale=-0.5)
        c.wait("tensor", "se", r2)
        rb = c.inc(c.op("tensor", "matmul", p_m[:, 128:256], ones[0:1, :],
                        rstd[0:1, :], start=True, stop=True), "te")
        c.wait("vector", "te", rb)
        pm_free = c.opi("vector", "tensor_copy", rbc[:], p_m[:, 128:256])
        c.wait("vector", "sd", gl)
        if ch >= 2:
            c.wait("vector", "sd", yst[par])
        c.wait("vector", "te", s3b_te[par])
        c.op("vector", "tensor_tensor", yt[:].rearrange("p (k t2) -> p k t2", t2=128),
             yt[:].rearrange("p (k t2) -> p k t2", t2=128),
             rbc[:].unsqueeze(1).broadcast_to([128, 32, 128]), ALU.mult)
        gcons = c.opi("vector", "tensor_mul", yfb[par][:], yt[:], gbuf[:])
        yf_ve[par] = gcons
        c.wait("sync", "ve", gcons)
        yst[par] = c.dma("sync",
                         b["yfT"].rearrange("(k p) t -> p k t", p=128)[:, :, t0:t1],
                         yfb[par][:].rearrange("p (k t2) -> p k t2", t2=128))
    for tt in (NCH - 2, NCH - 1):
        tpar = tt % 2
        c.wait("tensor", "ve", yf_ve[tpar])
        for dc in range(2):
            c.wait("tensor", "ve", max(s3b_ev[0], s3b_ev[1]))
            ins = None
            for k in range(32):
                ins = c.op("tensor", "matmul", p_bf[:, 0:512],
                           yfb[tpar][:, k * 128:(k + 1) * 128],
                           wsb[:, k * 1024 + dc * 512: k * 1024 + (dc + 1) * 512],
                           start=(k == 0), stop=(k == 31))
            mm = c.inc(ins, "te")
            c.wait("vector", "te", mm)
            if dc == 0:
                c.wait("vector", "sd", ob_st)
            s3b_ev[dc] = c.opi("vector", "tensor_copy", ob[:, dc * 512:(dc + 1) * 512],
                               p_bf[:, 0:512])
        c.wait("sync", "ve", s3b_ev[1])
        ob_st = c.dma("sync", b["mham"][tt * 128:(tt + 1) * 128, 0:1024], ob[:])
    c.wait("gpsimd", "te", c.cnt["te"])
    wld = c.dma("gpsimd", wsb[:].rearrange("p (k d) -> p k d", d=1024),
                wsrc[:, :, 1024:2048])
    c.wait("vector", "gd", wld)
    ins = None
    for k in range(32):
        ins = c.op("vector", "tensor_scalar", wsb[:, k * 1024:(k + 1) * 1024],
                   wsb[:, k * 1024:(k + 1) * 1024], nw[:, k:k + 1], None, ALU.mult)
    wok = c.inc(ins, "ve")
    c.wait("tensor", "ve", wok)
    ycons = [0, 0]
    for tt in range(NT):
        tpar = tt % 2
        if tt >= 2:
            c.wait("sync", "te", ycons[tpar])
        yld = c.dma("sync", xnat[tpar][:].rearrange("p (k t) -> p k t", t=128),
                    b["yfT"].rearrange("(k p) t -> p k t", p=128)[:, :, tt * 128:(tt + 1) * 128])
        c.wait("tensor", "sd", yld)
        for dc in range(2):
            c.wait("tensor", "ve", max(s3b_ev[0], s3b_ev[1]))
            ins = None
            for k in range(32):
                ins = c.op("tensor", "matmul", p_bf[:, 0:512],
                           xnat[tpar][:, k * 128:(k + 1) * 128],
                           wsb[:, k * 1024 + dc * 512: k * 1024 + (dc + 1) * 512],
                           start=(k == 0), stop=(k == 31))
            mm = c.inc(ins, "te")
            if dc == 1:
                ycons[tpar] = mm
            c.wait("vector", "te", mm)
            if dc == 0:
                c.wait("vector", "sd", ob_st)
            s3b_ev[dc] = c.opi("vector", "tensor_copy", ob[:, dc * 512:(dc + 1) * 512],
                               p_bf[:, 0:512])
        c.wait("sync", "ve", s3b_ev[1])
        ob_st = c.dma("sync", b["mham"][tt * 128:(tt + 1) * 128, 1024:2048], ob[:])
    c.wait("sync", "sd", c.cnt["sd"])


def emit_dualproj(c, b, wa, wb_, dst, fwht, xn_src):
    xnt, xn_rdy = load_xnT(c, b[xn_src], "dp_xnT")
    h128 = c.sb("dp_h128", [128, 128], BF16)
    if fwht:
        c.dma("sync", h128[:], b["c_h128"])
    wbufs = [c.sb(f"dp_w{i}", [128, NT * 128], BF16) for i in range(4)]
    sbufs = [c.sb(f"dp_s{i}", [128, 512], F32) for i in range(2)]
    hbufs = [c.sb(f"dp_h{i}", [128, 512], BF16) for i in range(2)]
    obufs = [c.sb(f"dp_o{i}", [128, 512], BF16) for i in range(4)]
    psu = [c.ps(f"dp_pu{i}", [128, 512]) for i in range(2)]
    psv = [c.ps(f"dp_pv{i}", [128, 512]) for i in range(2)]
    psw = [c.ps(f"dp_pw{i}", [128, 512]) for i in range(2)]
    wsrca = b[wa].rearrange("(dt p) hh -> p dt hh", p=128)
    wsrcb = b[wb_].rearrange("(dt p) hh -> p dt hh", p=128)
    c.wait("tensor", "sd", xn_rdy)
    wld = [0, 0]
    wdone = [0, 0]
    sil_done = [0, 0]
    mul_done = [0, 0]
    st1_done = [0, 0]
    mul_prev = [0, 0]
    st1_prev = [0, 0]
    ost = [0] * 4
    for a in range(64):
        par = a % 2
        if a >= 2:
            c.wait("gpsimd", "te", wdone[par])
        c.dma("gpsimd", wbufs[par][:].rearrange("p (dt hh) -> p dt hh", hh=128),
              wsrca[:, :, a * 128:(a + 1) * 128])
        wld[par] = c.dma("gpsimd", wbufs[2 + par][:].rearrange("p (dt hh) -> p dt hh", hh=128),
                         wsrcb[:, :, a * 128:(a + 1) * 128])
        c.wait("tensor", "gd", wld[par])
        for tc in range(4):
            oi = (a * 4 + tc) % 4
            bi = (a * 4 + tc) % 2
            sbuf = sbufs[bi]
            hbuf = hbufs[bi]
            c.wait("tensor", "se", sil_done[par])
            ins = None
            for dt_i in range(NT):
                ins = c.op("tensor", "matmul", psu[par][:],
                           wbufs[par][:, dt_i * 128:(dt_i + 1) * 128],
                           xnt(dt_i, tc * 512, (tc + 1) * 512),
                           start=(dt_i == 0), stop=(dt_i == NT - 1))
            mmu = c.inc(ins, "te")
            c.wait("tensor", "ve", mul_done[par])
            for dt_i in range(NT):
                ins = c.op("tensor", "matmul", psv[par][:],
                           wbufs[2 + par][:, dt_i * 128:(dt_i + 1) * 128],
                           xnt(dt_i, tc * 512, (tc + 1) * 512),
                           start=(dt_i == 0), stop=(dt_i == NT - 1))
            mmv = c.inc(ins, "te")
            if tc == 3:
                wdone[par] = mmv
            c.wait("scalar", "te", mmu)
            c.wait("scalar", "ve", mul_prev[bi])
            sil_done[par] = c.opi("scalar", "activation", sbuf[:], psu[par][:], AF.Silu)
            c.wait("vector", "se", sil_done[par])
            c.wait("vector", "te", mmv)
            if fwht:
                c.wait("vector", "te", st1_prev[bi])
                mul_done[par] = c.opi("vector", "tensor_mul", hbuf[:], sbuf[:], psv[par][:])
                mul_prev[bi] = mul_done[par]
                c.wait("tensor", "ve", mul_done[par])
                c.wait("tensor", "ve", st1_done[par])
                mmw = c.inc(c.op("tensor", "matmul", psw[par][:], h128[:], hbuf[:],
                                 start=True, stop=True), "te")
                st1_prev[bi] = mmw
                c.wait("vector", "te", mmw)
                c.wait("vector", "sd", ost[oi])
                st1_done[par] = c.opi("vector", "tensor_copy", obufs[oi][:], psw[par][:])
                dv = st1_done[par]
            else:
                c.wait("vector", "sd", ost[oi])
                mul_done[par] = c.opi("vector", "tensor_mul", obufs[oi][:], sbuf[:], psv[par][:])
                mul_prev[bi] = mul_done[par]
                dv = mul_done[par]
            c.wait("sync", "ve", dv)
            ost[oi] = c.dma("sync", dst[a, :, tc * 512:(tc + 1) * 512], obufs[oi][:])
    c.wait("sync", "sd", c.cnt["sd"])


def emit_s4(c, b):
    emit_dualproj(c, b, "w1T", "w3T", b["vsp"], True, "xn0")


def emit_s9(c, b):
    emit_dualproj(c, b, "wgT", "wuT", b["hid1"], False, "xn1")


def emit_s5(c, b):
    hp = c.sb("hp", [128, 128], BF16)
    v0 = c.dma("sync", hp[:], b["c_hp"])
    w2 = c.sb("w2", [128, 64 * 1024], BF16)
    vt = [c.sb(f"vt{i}", [128, 64 * 128], BF16) for i in range(2)]
    s2 = c.sb("s2", [128, 64 * 128], BF16)
    xb = [c.sb(f"x{i}", [128, 1024], F32) for i in range(2)]
    mb = [c.sb(f"mm{i}", [128, 1024], F32) for i in range(2)]
    ps_s2 = [c.ps(f"ps{i}", [128, 512]) for i in range(2)]
    ps_o = [c.ps(f"po{i}", [128, 512]) for i in range(4)]
    vcons = [0, 0]
    s2done = [0, 0]
    s2cons = [0, 0]
    evd = [0] * 4
    ostd = [0, 0]
    for half in range(2):
        d0 = half * 1024
        if half > 0:
            c.wait("gpsimd", "te", c.cnt["te"])
        wld = c.dma("gpsimd", w2[:].rearrange("p (k d) -> p k d", d=1024),
                    b["w2Tp"][:, :, d0:d0 + 1024].rearrange("k p d -> p k d"))
        c.wait("tensor", "gd", wld)
        c.wait("tensor", "sd", v0)
        for tt in range(NT):
            par = tt % 2
            t0 = tt * 128
            if half + tt >= 2:
                c.wait("sync", "te", vcons[par])
            c.dma("sync", vt[par][0:64, :].rearrange("p (k t) -> p k t", t=128),
                  b["vsp"][:, 0:128:2, t0:t0 + 128])
            vld = c.dma("sync", vt[par][64:128, :].rearrange("p (k t) -> p k t", t=128),
                        b["vsp"][:, 1:128:2, t0:t0 + 128])
            c.wait("sync", "ve", max(evd[par * 2], evd[par * 2 + 1]))
            c.wait("sync", "sd", ostd[par])
            c.dma("sync", xb[par][:], b["x0"][t0:t0 + 128, d0:d0 + 1024])
            ml = c.dma("sync", mb[par][:], b["mham"][t0:t0 + 128, d0:d0 + 1024])
            c.wait("tensor", "sd", vld)
            for g in range(16):
                pi = g % 2
                c.wait("tensor", "ve", s2done[pi])
                ins = None
                for j in range(4):
                    bp = g * 4 + j
                    ins = c.op("tensor", "matmul", ps_s2[pi][:, j * 128:(j + 1) * 128],
                               hp[:], vt[par][:, bp * 128:(bp + 1) * 128],
                               start=True, stop=True)
                mm = c.inc(ins, "te")
                if g == 15:
                    vcons[par] = mm
                c.wait("vector", "te", mm)
                c.wait("vector", "te", s2cons[pi])
                s2done[pi] = c.opi("vector", "tensor_copy",
                                   s2[:, g * 512:(g + 1) * 512], ps_s2[pi][:])
                if g > 0:
                    gp = g - 1
                    c.wait("tensor", "ve", s2done[1 - pi])
                    for dc in range(2):
                        po = par * 2 + dc
                        if gp == 0:
                            c.wait("tensor", "ve", evd[po])
                        ins = None
                        for j in range(4):
                            bp = gp * 4 + j
                            ins = c.op("tensor", "matmul", ps_o[po][:],
                                       s2[:, bp * 128:(bp + 1) * 128],
                                       w2[:, bp * 1024 + dc * 512: bp * 1024 + (dc + 1) * 512],
                                       start=(bp == 0), stop=False)
                    s2cons[gp % 2] = c.inc(ins, "te")
            c.wait("tensor", "ve", s2done[1])
            for dc in range(2):
                po = par * 2 + dc
                ins = None
                for j in range(4):
                    bp = 60 + j
                    ins = c.op("tensor", "matmul", ps_o[po][:],
                               s2[:, bp * 128:(bp + 1) * 128],
                               w2[:, bp * 1024 + dc * 512: bp * 1024 + (dc + 1) * 512],
                               start=False, stop=True)
                mmo = c.inc(ins, "te")
                s2cons[1] = mmo
                c.wait("vector", "te", mmo)
                c.wait("vector", "sd", ml)
                c.op("vector", "tensor_add", xb[par][:, dc * 512:(dc + 1) * 512],
                     xb[par][:, dc * 512:(dc + 1) * 512],
                     mb[par][:, dc * 512:(dc + 1) * 512])
                evd[po] = c.opi("vector", "tensor_add",
                                xb[par][:, dc * 512:(dc + 1) * 512],
                                xb[par][:, dc * 512:(dc + 1) * 512], ps_o[po][:])
            c.wait("sync", "ve", evd[par * 2 + 1])
            ostd[par] = c.dma("sync", b["x1"][t0:t0 + 128, d0:d0 + 1024], xb[par][:])
    c.wait("sync", "sd", c.cnt["sd"])


def emit_s7(c, b):
    xnt, xn_rdy = load_xnT(c, b["xn1"], "xnT")
    onescol = c.sb("oc", [128, 1], BF16)
    onesf = c.sb("on", [1, 128], F32)
    triub = c.sb("tr", [128, 128], BF16)
    c.dma("sync", onescol[:], b["c_onescol_bf"])
    c.dma("sync", onesf[:], b["c_ones_f"])
    v0 = c.dma("sync", triub[:], b["c_triu_bf"])
    wk = [c.sb(f"wk{i}", [128, NT * 128], BF16) for i in range(2)]
    wq = [c.sb(f"wq{i}", [128, NT * 128], BF16) for i in range(2)]
    wv = [c.sb(f"wv{i}", [128, NT * 128], BF16) for i in range(2)]
    kT = c.sb("kT", [128, T], BF16)
    qT = c.sb("qT", [128, T], BF16)
    vn = c.sb("vn", [128, NT * 128], BF16)
    pb = [c.sb(f"pb{i}", [128, 512], BF16) for i in range(2)]
    den = c.sb("den", [1, T], F32)
    rn = c.sb("rn", [1, T], F32)
    rnb = c.sb("rnb", [128, 512], F32)
    zraw = c.sb("zraw", [128, T], BF16)
    zb = [c.sb(f"z{i}", [128, T], BF16) for i in range(2)]
    ps_s = [c.ps(f"ps{i}", [128, 512]) for i in range(2)]
    ps_z = [c.ps(f"pz{i}", [128, 256]) for i in range(2)]
    ps_d = c.ps("pd", [1, 512])
    ps_q = [c.ps(f"pq{i}", [128, 512]) for i in range(2)]
    ps_r = c.ps("pr", [128, 512])
    wks = b["wkT"].rearrange("(dt p) f -> p dt f", p=128)
    wqs = b["wqT"].rearrange("(dt p) f -> p dt f", p=128)
    wvs = b["wvT"].rearrange("(dt p) f -> p dt f", p=128)
    c.wait("tensor", "sd", max(xn_rdy, v0))
    wld = [0, 0]
    wcons = [0, 0]
    kq_ev = 0
    st_free = [0, 0]
    pb_free = [0, 0]
    z_free = [0, 0]
    d_free = [0, 0]
    r_free = 0
    zst = [0, 0]
    for h in range(AH):
        par = h % 2
        if h >= 2:
            c.wait("gpsimd", "te", wcons[par])
        c.dma("gpsimd", wk[par][:].rearrange("p (dt f) -> p dt f", f=128),
              wks[:, :, h * 128:(h + 1) * 128])
        c.dma("gpsimd", wq[par][:].rearrange("p (dt f) -> p dt f", f=128),
              wqs[:, :, h * 128:(h + 1) * 128])
        wld[par] = c.dma("gpsimd", wv[par][:].rearrange("p (dt f) -> p dt f", f=128),
                         wvs[:, :, h * 128:(h + 1) * 128])
        c.wait("tensor", "gd", wld[par])
        for wbuf, dstb in ((wk, kT), (wq, qT)):
            for tc in range(4):
                pi = tc % 2
                c.wait("tensor", "ve", kq_ev)
                ins = None
                for dt_i in range(NT):
                    ins = c.op("tensor", "matmul", ps_q[pi][:],
                               wbuf[par][:, dt_i * 128:(dt_i + 1) * 128],
                               xnt(dt_i, tc * 512, (tc + 1) * 512),
                               start=(dt_i == 0), stop=(dt_i == NT - 1))
                mm = c.inc(ins, "te")
                c.wait("vector", "te", mm)
                kq_ev = c.opi("vector", "tensor_copy",
                              dstb[:, tc * 512:(tc + 1) * 512], ps_q[pi][:])
        for tt2 in range(NT):
            pi = tt2 % 2
            c.wait("tensor", "ve", kq_ev)
            ins = None
            for dt_i in range(NT):
                ins = c.op("tensor", "matmul", ps_q[pi][0:128, 0:128],
                           xnt(dt_i, tt2 * 128, (tt2 + 1) * 128),
                           wv[par][:, dt_i * 128:(dt_i + 1) * 128],
                           start=(dt_i == 0), stop=(dt_i == NT - 1))
            mm = c.inc(ins, "te")
            c.wait("vector", "te", mm)
            kq_ev = c.opi("vector", "tensor_copy",
                          vn[:, tt2 * 128:(tt2 + 1) * 128], ps_q[pi][0:128, 0:128])
        wcons[par] = c.cnt["te"]
        c.wait("tensor", "ve", kq_ev)
        for qp in range(8):
            q0 = qp * 2
            njt = 2 * qp + 2
            nblk = (njt + 1) // 2
            zpi = qp % 2
            for blk in range(nblk):
                j0 = blk * 2
                nj = min(2, njt - j0)
                sb_i = blk % 2
                pbi = blk % 2
                c.wait("tensor", "se", st_free[sb_i])
                ins = None
                for jj in range(nj):
                    ins = c.op("tensor", "matmul", ps_s[sb_i][:, jj * 256:(jj + 1) * 256],
                               kT[:, (j0 + jj) * 128:(j0 + jj + 1) * 128],
                               qT[:, q0 * 128:(q0 + 2) * 128], start=True, stop=True)
                sv = c.inc(ins, "te")
                c.wait("scalar", "te", max(sv, pb_free[pbi]))
                ex = c.opi("scalar", "activation", pb[pbi][:, 0:nj * 256],
                           ps_s[sb_i][:, 0:nj * 256], AF.Exp, scale=ASCALE)
                st_free[sb_i] = ex
                mk = ex
                for jj in range(nj):
                    jt = j0 + jj
                    if jt == q0:
                        c.wait("vector", "se", ex)
                        mk = c.opi("vector", "tensor_mul",
                                   pb[pbi][:, jj * 256:jj * 256 + 128],
                                   pb[pbi][:, jj * 256:jj * 256 + 128], triub[:])
                    elif jt == q0 + 1:
                        c.wait("vector", "se", ex)
                        c.op("vector", "memset", pb[pbi][:, jj * 256:jj * 256 + 128], 0)
                        mk = c.opi("vector", "tensor_mul",
                                   pb[pbi][:, jj * 256 + 128:(jj + 1) * 256],
                                   pb[pbi][:, jj * 256 + 128:(jj + 1) * 256], triub[:])
                if mk != ex:
                    c.wait("tensor", "ve", mk)
                else:
                    c.wait("tensor", "se", ex)
                if blk == 0:
                    c.wait("tensor", "ve", max(z_free[zpi], d_free[zpi]))
                insd = None
                insz = None
                for jj in range(nj):
                    last = (blk == nblk - 1 and jj == nj - 1)
                    insz = c.op("tensor", "matmul", ps_z[zpi][:],
                                vn[:, (j0 + jj) * 128:(j0 + jj + 1) * 128],
                                pb[pbi][:, jj * 256:(jj + 1) * 256],
                                start=(blk == 0 and jj == 0), stop=last)
                    insd = c.op("tensor", "matmul", ps_d[0:1, zpi * 256:zpi * 256 + 256],
                                onescol[:], pb[pbi][:, jj * 256:(jj + 1) * 256],
                                start=(blk == 0 and jj == 0), stop=last)
                pb_free[pbi] = c.inc(insd, "te")
                zdone = c.inc(insz, "te")
            c.wait("vector", "te", zdone)
            z_free[zpi] = c.opi("vector", "tensor_copy",
                                zraw[:, q0 * 128:(q0 + 2) * 128], ps_z[zpi][:])
            d_free[zpi] = c.opi("vector", "tensor_copy",
                                den[0:1, q0 * 128:(q0 + 2) * 128],
                                ps_d[0:1, zpi * 256:zpi * 256 + 256])
        zo = h % 2
        if h >= 2:
            c.wait("vector", "sd", zst[zo])
        rv = c.opi("vector", "reciprocal", rn[:], den[:])
        c.wait("tensor", "ve", rv)
        for c4 in range(4):
            c.wait("tensor", "ve", r_free)
            rb = c.inc(c.op("tensor", "matmul", ps_r[:], onesf[0:1, :].broadcast_to([1, 512])
                            if False else onesf[0:1, 0:128],
                            rn[0:1, c4 * 512:(c4 + 1) * 512], start=True, stop=True), "te")
            c.wait("vector", "te", rb)
            r_free = c.opi("vector", "tensor_copy", rnb[:], ps_r[:])
            zst[zo] = c.opi("vector", "tensor_tensor", zb[zo][:, c4 * 512:(c4 + 1) * 512],
                            zraw[:, c4 * 512:(c4 + 1) * 512], rnb[:], ALU.mult)
        zev = zst[zo]
        c.wait("sync", "ve", zev)
        zst[zo] = c.dma("sync", b["zT"][h], zb[zo][:])
    c.wait("sync", "sd", c.cnt["sd"])


def emit_s8(c, b):
    wsb = c.sb("s8_w", [128, AH * D], BF16)
    zbufs = [c.sb(f"s8_z{i}", [128, AH * 128], BF16) for i in range(2)]
    obufs = [c.sb(f"s8_o{i}", [128, D], BF16) for i in range(2)]
    pss = [c.ps(f"s8_p{i}", [128, 512]) for i in range(8)]
    wld = c.dma("gpsimd", wsb[:].rearrange("p (k d) -> p k d", d=D),
                b["woT"].rearrange("(k p) d -> p k d", p=128))
    c.wait("tensor", "gd", wld)
    zld = [0, 0]
    zcons = [0, 0]
    ev = [0] * 8
    ost = [0, 0]
    for tt in range(NT):
        par = tt % 2
        if tt >= 2:
            c.wait("sync", "te", zcons[par])
        zld[par] = c.dma("sync", zbufs[par][:].rearrange("p (k t) -> p k t", t=128),
                         b["zT"][:, :, tt * 128:(tt + 1) * 128].rearrange("k p t -> p k t"))
        c.wait("tensor", "sd", zld[par])
        for dc in range(4):
            pi = par * 4 + dc
            c.wait("tensor", "ve", ev[pi])
            ins = None
            for k in range(AH):
                ins = c.op("tensor", "matmul", pss[pi][:],
                           zbufs[par][:, k * 128:(k + 1) * 128],
                           wsb[:, k * D + dc * 512: k * D + (dc + 1) * 512],
                           start=(k == 0), stop=(k == AH - 1))
            mm = c.inc(ins, "te")
            if dc == 3:
                zcons[par] = mm
            c.wait("vector", "te", mm)
            if dc == 0 and tt >= 2:
                c.wait("vector", "sd", ost[par])
            ev[pi] = c.opi("vector", "tensor_copy",
                           obufs[par][:, dc * 512:(dc + 1) * 512], pss[pi][:])
        c.wait("sync", "ve", ev[par * 4 + 3])
        ost[par] = c.dma("sync", b["matt"][tt * 128:(tt + 1) * 128, :], obufs[par][:])
    c.wait("sync", "sd", c.cnt["sd"])


def emit_s10(c, b):
    wd = c.sb("wd", [128, 64 * 1024], BF16)
    hb = [c.sb(f"h{i}", [128, 64 * 256], BF16) for i in range(2)]
    xb = [c.sb(f"x{i}", [128, 1024], F32) for i in range(2)]
    mb = [c.sb(f"mm{i}", [128, 1024], BF16) for i in range(2)]
    ps_o = [c.ps(f"p{i}", [128, 512]) for i in range(4)]
    hcons = [0, 0]
    evd = [0] * 4
    ostd = [0, 0]
    for half in range(2):
        d0 = half * 1024
        if half > 0:
            c.wait("gpsimd", "te", c.cnt["te"])
        wld = c.dma("gpsimd", wd[:].rearrange("p (k d) -> p k d", d=1024),
                    b["wdT"][:, d0:d0 + 1024].rearrange("(k p) d -> p k d", p=128))
        c.wait("tensor", "gd", wld)
        for tc in range(8):
            par = tc % 2
            if half + tc >= 2:
                c.wait("sync", "te", hcons[par])
            hld = c.dma("sync", hb[par][:].rearrange("p (k t) -> p k t", t=256),
                        b["hid1"][:, :, tc * 256:(tc + 1) * 256].rearrange("k p t -> p k t"))
            for ti in range(2):
                tt = tc * 2 + ti
                t0 = tt * 128
                tpar = tt % 2
                c.wait("sync", "ve", max(evd[tpar * 2], evd[tpar * 2 + 1]))
                c.wait("sync", "sd", ostd[tpar])
                c.dma("sync", xb[tpar][:], b["x1"][t0:t0 + 128, d0:d0 + 1024])
                ml = c.dma("sync", mb[tpar][:], b["matt"][t0:t0 + 128, d0:d0 + 1024])
                c.wait("tensor", "sd", hld)
                for dc in range(2):
                    pi = tpar * 2 + dc
                    c.wait("tensor", "ve", evd[pi])
                    ins = None
                    for k in range(64):
                        ins = c.op("tensor", "matmul", ps_o[pi][:],
                                   hb[par][:, k * 256 + ti * 128: k * 256 + ti * 128 + 128],
                                   wd[:, k * 1024 + dc * 512: k * 1024 + (dc + 1) * 512],
                                   start=(k == 0), stop=(k == 63))
                    mmo = c.inc(ins, "te")
                    if dc == 1 and ti == 1:
                        hcons[par] = mmo
                    c.wait("vector", "te", mmo)
                    c.wait("vector", "sd", ml)
                    c.op("vector", "tensor_add", xb[tpar][:, dc * 512:(dc + 1) * 512],
                         xb[tpar][:, dc * 512:(dc + 1) * 512],
                         mb[tpar][:, dc * 512:(dc + 1) * 512])
                    evd[pi] = c.opi("vector", "tensor_add",
                                    xb[tpar][:, dc * 512:(dc + 1) * 512],
                                    xb[tpar][:, dc * 512:(dc + 1) * 512], ps_o[pi][:])
                c.wait("sync", "ve", evd[tpar * 2 + 1])
                ostd[tpar] = c.dma("sync", b["out"][t0:t0 + 128, d0:d0 + 1024], xb[tpar][:])
    c.wait("sync", "sd", c.cnt["sd"])
    c.wait("gpsimd", "sd", c.cnt["sd"])


EMITTERS = {
    "s0": emit_s0, "s1": emit_s1, "s2": emit_s2, "s3a": emit_s3a,
    "s4": emit_s4, "s5": emit_s5, "s6": emit_s6, "s7": emit_s7, "s8": emit_s8,
    "s9": emit_s9, "s10": emit_s10,
}


def build_program(stages):
    nc = bass.Bass("TRN2", target_bir_lowering=False)
    need_in = set()
    produced = set()
    wneed = set()
    for s in stages:
        ins, outs = STAGE_IO[s]
        for n in ins:
            if n not in produced and n != "x0":
                need_in.add(n)
        produced.update(outs)
        wneed.update(STAGE_W[s])
        if s in STAGE_X:
            wneed.update(STAGE_X[s])
    fused = len(stages) == len(STAGE_IO)
    b = {}
    for name in sorted(wneed):
        if name == "x0":
            b[name] = nc.dram_tensor(name, XIN["x0"][0], XIN["x0"][2],
                                     kind="ExternalInput").ap()
        else:
            shape, _, bdt = WEIGHTS[name]
            b[name] = nc.dram_tensor(name, shape, bdt, kind="ExternalInput").ap()
    spill_out = set()
    for name in sorted(set(SPILLS)):
        shape, _, bdt = SPILLS[name]
        if name in need_in:
            b[name] = nc.dram_tensor(name, shape, bdt, kind="ExternalInput").ap()
        elif name in produced:
            if fused and not globals().get("DEBUG_SPILLS", False):
                b[name] = nc.dram_tensor(name, shape, bdt, kind="Internal").ap()
            else:
                b[name] = nc.dram_tensor(name, shape, bdt, kind="ExternalOutput").ap()
                spill_out.add(name)
    has_out = any("out" in STAGE_IO[s][1] for s in stages)
    if has_out:
        b["out"] = nc.dram_tensor("out", XOUT["out"][0], XOUT["out"][2],
                                  kind="ExternalOutput").ap()
    stage_marks = []
    with ExitStack() as es:
        c = Ctx(nc, es)
        for si, s in enumerate(stages):
            with ExitStack() as stage_es:
                c.es = stage_es
                c.prefix = s + "_"
                if si > 0:
                    c.barrier()
                EMITTERS[s](c, b)
                c.seal()
                stage_marks.append((s, nc.n_instructions()))
            c.es = es
    innames = sorted(need_in | (wneed & set(WEIGHTS)) | ({"x0"} if "x0" in b else set()))
    outnames = sorted(spill_out | ({"out"} if has_out else set()))
    nc._stage_marks = stage_marks
    return nc, innames, outnames


def hadamard(n):
    m = np.ones((1, 1), np.float32)
    while m.shape[0] < n:
        m = np.block([[m, m], [m, -m]])
    return m


def prep_host(inputs):
    g = {}
    g["ipT"] = np.ascontiguousarray(inputs["L0_attn_in_proj_w"].T)
    g["opT"] = np.ascontiguousarray(inputs["L0_attn_out_proj_w"].T)
    g["w1T"] = np.ascontiguousarray(inputs["L0_ffn_W1"].T)
    g["w3T"] = np.ascontiguousarray(inputs["L0_ffn_W3"].T)
    w2v = np.ascontiguousarray(inputs["L0_ffn_W2"].T).reshape(64, 128, D)
    w2p = np.empty((64, 128, D), np.float32)
    q = np.arange(128)
    for p in range(64):
        w2p[p, q[q < 64]] = w2v[q[q < 64], 2 * p]
        w2p[p, q[q >= 64]] = w2v[q[q >= 64] - 64, 2 * p + 1]
    g["w2Tp"] = w2p
    g["wkT"] = np.ascontiguousarray(np.transpose(inputs["L1_attn_W_K"], (2, 0, 1)).reshape(D, AH * ADH))
    g["wqT"] = np.ascontiguousarray(np.transpose(inputs["L1_attn_W_Q"], (2, 0, 1)).reshape(D, AH * ADH))
    g["wvT"] = np.ascontiguousarray(np.transpose(inputs["L1_attn_W_V"], (2, 0, 1)).reshape(D, AH * ADH))
    g["woT"] = np.ascontiguousarray(inputs["L1_attn_W_O"].T)
    g["wgT"] = np.ascontiguousarray(inputs["L1_ffn_w_gate"].T)
    g["wuT"] = np.ascontiguousarray(inputs["L1_ffn_w_up"].T)
    g["wdT"] = np.ascontiguousarray(inputs["L1_ffn_w_down"].T)
    g["convw"] = np.ascontiguousarray(inputs["L0_attn_conv_w"].reshape(33, 128, 4))
    g["convb"] = np.ascontiguousarray(inputs["L0_attn_conv_b"].reshape(33, 128).T)
    g["dtb"] = np.ascontiguousarray(inputs["L0_attn_dt_bias"].reshape(64, 1))
    g["alog"] = np.ascontiguousarray(inputs["L0_attn_A_log"].reshape(64, 1))
    g["dvec"] = np.ascontiguousarray(inputs["L0_attn_D"].reshape(1, 64))
    g["nw"] = np.ascontiguousarray(inputs["L0_attn_norm_weight"].reshape(32, 128).T)
    g["c_id_bf"] = np.eye(128, dtype=NPBF)
    g["c_id_f"] = np.eye(128, dtype=np.float32)
    g["c_h128"] = hadamard(128).astype(NPBF)
    h64 = hadamard(64)
    hp = np.zeros((128, 128), np.float32)
    hp[0:64, 0:64] = h64
    hp[64:128, 64:128] = h64
    g["c_hp"] = (hp / np.sqrt(np.float32(HFF))).astype(NPBF)
    tri = np.triu(np.ones((128, 128), np.float32))
    g["c_triu_f"] = tri
    g["c_triu_bf"] = tri.astype(NPBF)
    g["c_ones_f"] = np.ones((1, 128), np.float32)
    g["c_onescol_bf"] = np.ones((128, 1), NPBF)
    for k in g:
        want = WEIGHTS[k][1]
        if g[k].dtype != want:
            g[k] = np.ascontiguousarray(g[k].astype(want))
    return g


GROUPS = [["s0"], ["s1"], ["s2"], ["s3a"], ["s4"], ["s5"], ["s6"],
          ["s7"], ["s8"], ["s9"], ["s10"]]
FUSED_GROUPS = [["s0", "s1", "s2", "s3a", "s4", "s5", "s6",
                 "s7", "s8", "s9", "s10"]]


def kernel(**inputs):
    x0 = np.ascontiguousarray(inputs["L0_attn_hidden_states"].astype(np.float32))
    ncores = 8
    g = prep_host(inputs)
    state = [dict() for _ in range(ncores)]
    for ci in range(ncores):
        state[ci]["x0"] = np.ascontiguousarray(x0[ci])
    out = np.zeros((ncores, T, D), np.float32)
    for grp in FUSED_GROUPS:
        nc, innames, outnames = build_program(grp)
        in_maps = []
        for ci in range(ncores):
            m = {}
            for n in innames:
                m[n] = g[n] if n in g else state[ci][n]
            in_maps.append(m)
        res = run_bass_kernel_spmd(nc, in_maps, core_ids=list(range(ncores)))
        for ci in range(ncores):
            for n in outnames:
                state[ci][n] = res.results[ci][n]
    for ci in range(ncores):
        out[ci] = np.asarray(state[ci]["out"], np.float32)
    return out


# version 11 (final)
# speedup vs baseline: 1.2275x; 1.0311x over previous
import numpy as np
import ml_dtypes
from contextlib import ExitStack

import concourse.bass as bass
import concourse.mybir as mybir
from concourse.bass_utils import run_bass_kernel_spmd

F32 = mybir.dt.float32
BF16 = mybir.dt.bfloat16
AF = mybir.ActivationFunctionType
ALU = mybir.AluOpType
AX = mybir.AxisListType
NPBF = ml_dtypes.bfloat16

T = 2048
D = 2048
NT = 16
INTER = 4096
HEADS = 64
PD = 64
ND = 64
Q = 128
NCH = 16
HFF = 8192
AH = 16
ADH = 128
EPS_LN = 1e-5
EPS_MN = 1e-6
ASCALE = float(1.0 / np.sqrt(np.float32(ADH)))

ENGS = ("tensor", "vector", "scalar", "gpsimd", "sync")
QSEM = {"sync": "sd", "scalar": "ad", "gpsimd": "gd"}
ESEM = {"tensor": "te", "vector": "ve", "scalar": "se", "gpsimd": "ge"}


class Ctx:
    def __init__(self, nc, es):
        self.nc = nc
        self.es = es
        self.sems = {}
        self.cnt = {}
        self.last = {}
        self.lastinc = {}
        self.prefix = ""
        for name in ("sd", "ad", "gd", "te", "ve", "se", "ge"):
            self.sems[name] = es.enter_context(nc.semaphore(name))
            self.cnt[name] = 0

    def sb(self, name, shape, dtype):
        return self.es.enter_context(self.nc.sbuf_tensor(self.prefix + name, shape, dtype))

    def ps(self, name, shape, dtype=F32):
        return self.es.enter_context(self.nc.psum_tensor(self.prefix + name, shape, dtype))

    def eng(self, name):
        return getattr(self.nc, name)

    def inc(self, instr, sem, amt=1):
        instr.then_inc(self.sems[sem], amt)
        self.cnt[sem] += amt
        for e, s in ESEM.items():
            if s == sem and instr is self.last.get(e):
                self.lastinc[e] = True
        return self.cnt[sem]

    def wait(self, engine, sem, val):
        if val > 0:
            self.eng(engine).wait_ge(self.sems[sem], val)

    def dma(self, q, out, in_, inc=True, **kw):
        instr = self.eng(q).dma_start(out=out, in_=in_, **kw)
        self.last[q] = instr
        if inc:
            return self.inc(instr, QSEM[q], 16)
        return self.cnt[QSEM[q]]

    def op(self, engine, fname, *args, **kw):
        instr = getattr(self.eng(engine), fname)(*args, **kw)
        self.last[engine] = instr
        self.lastinc[engine] = False
        return instr

    def opi(self, engine, fname, *args, **kw):
        instr = self.op(engine, fname, *args, **kw)
        return self.inc(instr, ESEM[engine])

    def barrier(self):
        targets = [(s, self.cnt[s]) for s in ("sd", "ad", "gd", "te", "ve", "se", "ge")
                   if self.cnt[s] > 0]
        for e in ENGS:
            own = ESEM.get(e)
            for s, v in targets:
                if s == own:
                    continue
                self.eng(e).wait_ge(self.sems[s], v)

    def seal(self):
        for e in ("tensor", "vector", "scalar", "gpsimd"):
            ins = self.last.get(e)
            if ins is not None and not self.lastinc.get(e, True):
                self.inc(ins, ESEM[e])
            self.last[e] = None


WEIGHTS = {
    "ipT": ([D, 8384], np.float32, F32),
    "opT": ([INTER, D], np.float32, F32),
    "w1T": ([D, HFF], np.float32, F32),
    "w3T": ([D, HFF], np.float32, F32),
    "w2Tp": ([64, 128, D], np.float32, F32),
    "wkT": ([D, AH * ADH], np.float32, F32),
    "wqT": ([D, AH * ADH], np.float32, F32),
    "wvT": ([D, AH * ADH], np.float32, F32),
    "woT": ([AH * ADH, D], np.float32, F32),
    "wgT": ([D, HFF], np.float32, F32),
    "wuT": ([D, HFF], np.float32, F32),
    "wdT": ([HFF, D], np.float32, F32),
    "convw": ([33, 128, 4], np.float32, F32),
    "convb": ([128, 33], np.float32, F32),
    "dtb": ([64, 1], np.float32, F32),
    "alog": ([64, 1], np.float32, F32),
    "dvec": ([1, 64], np.float32, F32),
    "nw": ([128, 32], np.float32, F32),
    "c_id_bf": ([128, 128], NPBF, BF16),
    "c_id_f": ([128, 128], np.float32, F32),
    "c_h128": ([128, 128], NPBF, BF16),
    "c_hp": ([128, 128], NPBF, BF16),
    "c_triu_f": ([128, 128], np.float32, F32),
    "c_triu_bf": ([128, 128], NPBF, BF16),
    "c_ones_f": ([1, 128], np.float32, F32),
    "c_onescol_bf": ([128, 1], NPBF, BF16),
}
SPILLS = {
    "xn0": ([T, D], NPBF, BF16),
    "gateT": ([INTER, T], NPBF, BF16),
    "xconvT": ([INTER, T], NPBF, BF16),
    "bcT": ([64, 2 * T], NPBF, BF16),
    "csT": ([64, T], np.float32, F32),
    "dtT": ([64, T], np.float32, F32),
    "yfT": ([INTER, T], NPBF, BF16),
    "mham": ([T, D], np.float32, F32),
    "vsp": ([64, 128, T], NPBF, BF16),
    "x1": ([T, D], np.float32, F32),
    "xn1": ([T, D], NPBF, BF16),
    "zT": ([AH, ADH, T], NPBF, BF16),
    "matt": ([T, D], NPBF, BF16),
    "hid1": ([64, 128, T], NPBF, BF16),
}
XIN = {"x0": ([T, D], np.float32, F32)}
XOUT = {"out": ([T, D], np.float32, F32)}

STAGE_IO = {
    "s0": (["x0"], ["xn0"]),
    "s1": (["xn0"], ["gateT"]),
    "s2": (["xn0"], ["xconvT", "bcT", "csT", "dtT"]),
    "s3a": (["xconvT", "bcT", "csT", "dtT", "gateT"], ["yfT", "mham"]),
    "s4": (["xn0"], ["vsp"]),
    "s5": (["vsp", "x0", "mham"], ["x1"]),
    "s6": (["x1"], ["xn1"]),
    "s7": (["xn1"], ["zT"]),
    "s8": (["zT"], ["matt"]),
    "s9": (["xn1"], ["hid1"]),
    "s10": (["hid1", "x1", "matt"], ["out"]),
}
STAGE_W = {
    "s0": [],
    "s1": ["ipT"],
    "s2": ["ipT", "convw", "convb", "dtb", "alog"],
    "s3a": ["dvec", "c_ones_f", "c_onescol_bf", "c_id_bf", "c_id_f", "c_triu_f",
            "opT", "nw"],
    "s4": ["w1T", "w3T", "c_h128"],
    "s5": ["w2Tp", "c_hp"],
    "s6": [],
    "s7": ["wkT", "wqT", "wvT", "c_onescol_bf", "c_ones_f", "c_triu_bf"],
    "s8": ["woT"],
    "s9": ["wgT", "wuT"],
    "s10": ["wdT"],
}
STAGE_X = {"s0": ["x0"], "s5": ["x0"]}


def load_xnT(c, src_dram, name):
    xnT = c.sb(name, [128, NT * T], BF16)
    v = 0
    for dt_i in range(NT):
        v = c.dma("sync", xnT[:, dt_i * T:(dt_i + 1) * T],
                  src_dram[:, dt_i * 128:(dt_i + 1) * 128], transpose=True)

    def tile(dt_i, lo, hi):
        return xnT[:, dt_i * T + lo: dt_i * T + hi]
    return tile, v


def emit_ln(c, b, src, dst):
    nbuf = 3
    xts = [c.sb(f"ln_x{i}", [128, D], F32) for i in range(nbuf)]
    scr = c.sb("ln_scr", [128, D], F32)
    xnb = [c.sb(f"ln_o{i}", [128, D], BF16) for i in range(2)]
    st = c.sb("ln_st", [128, 8], F32)
    se_done = [0] * nbuf
    out_done = [0, 0]
    for tt in range(NT):
        bi = tt % nbuf
        oi = tt % 2
        if tt >= nbuf:
            c.wait("sync", "ve", se_done[bi])
        ld = c.dma("sync", xts[bi][:], src[tt * 128:(tt + 1) * 128, :])
        c.wait("scalar", "sd", ld)
        c.op("scalar", "activation", scr[:], xts[bi][:], AF.Square,
             accum_out=st[:, 1:2])
        sq = c.opi("scalar", "activation", scr[:], xts[bi][:], AF.Copy,
                   accum_out=st[:, 0:1])
        c.wait("vector", "se", sq)
        c.op("vector", "tensor_scalar", st[:, 2:3], st[:, 0:1], 1.0 / D, None, ALU.mult)
        c.op("vector", "tensor_scalar", st[:, 3:4], st[:, 1:2], 1.0 / D, None, ALU.mult)
        c.op("vector", "tensor_mul", st[:, 6:7], st[:, 2:3], st[:, 2:3])
        c.op("vector", "tensor_sub", st[:, 3:4], st[:, 3:4], st[:, 6:7])
        vv = c.opi("vector", "tensor_scalar", st[:, 3:4], st[:, 3:4], EPS_LN, None, ALU.add)
        c.wait("scalar", "ve", vv)
        c.op("scalar", "activation", st[:, 4:5], st[:, 3:4], AF.Ln)
        ss = c.opi("scalar", "activation", st[:, 4:5], st[:, 4:5], AF.Exp, scale=-0.5)
        c.wait("vector", "se", ss)
        if tt >= 2:
            c.wait("vector", "sd", out_done[oi])
        se_done[bi] = c.opi("vector", "tensor_scalar", xnb[oi][:], xts[bi][:],
                            st[:, 2:3], st[:, 4:5], ALU.subtract, op1=ALU.mult)
        c.wait("sync", "ve", se_done[bi])
        out_done[oi] = c.dma("sync", dst[tt * 128:(tt + 1) * 128, :], xnb[oi][:])
    c.wait("sync", "sd", c.cnt["sd"])


def emit_colproj(c, b, wname, wcols, dst, act, nchtiles, xn_src="xn0"):
    xnt, xn_rdy = load_xnT(c, b[xn_src], "cp_xnT")
    wbufs = [c.sb(f"cp_w{i}", [128, NT * 128], BF16) for i in range(2)]
    obufs = [c.sb(f"cp_o{i}", [128, 512], BF16) for i in range(4)]
    pss = [c.ps(f"cp_p{i}", [128, 512]) for i in range(4)]
    wsrc = b[wname].rearrange("(dt p) ch -> p dt ch", p=128)
    wld = [0, 0]
    wdone = [0, 0]
    ev_done = [0] * 4
    ob_done = [0] * 4
    c.wait("tensor", "sd", xn_rdy)
    for ct in range(nchtiles):
        wb = ct % 2
        if ct >= 2:
            c.wait("gpsimd", "te", wdone[wb])
        wld[wb] = c.dma("gpsimd",
                        wbufs[wb][:].rearrange("p (dt ch) -> p dt ch", ch=128),
                        wsrc[:, :, wcols + ct * 128: wcols + (ct + 1) * 128])
        c.wait("tensor", "gd", wld[wb])
        for tc in range(4):
            pi = (ct * 4 + tc) % 4
            c.wait("tensor", "se", ev_done[pi])
            for dt_i in range(NT):
                ins = c.op("tensor", "matmul", pss[pi][:],
                           wbufs[wb][:, dt_i * 128:(dt_i + 1) * 128],
                           xnt(dt_i, tc * 512, (tc + 1) * 512),
                           start=(dt_i == 0), stop=(dt_i == NT - 1))
            mm = c.inc(ins, "te")
            if tc == 3:
                wdone[wb] = mm
            c.wait("scalar", "te", mm)
            c.wait("scalar", "sd", ob_done[pi])
            ev_done[pi] = c.opi("scalar", "activation", obufs[pi][:], pss[pi][:], act)
            c.wait("sync", "se", ev_done[pi])
            ob_done[pi] = c.dma("sync",
                                dst[ct * 128:(ct + 1) * 128, tc * 512:(tc + 1) * 512],
                                obufs[pi][:])
    c.wait("sync", "sd", c.cnt["sd"])


def emit_s0(c, b):
    emit_ln(c, b, b["x0"], b["xn0"])


def emit_s6(c, b):
    emit_ln(c, b, b["x1"], b["xn1"])


def emit_s1(c, b):
    emit_colproj(c, b, "ipT", 0, b["gateT"], AF.Silu, 32)


def emit_s2(c, b):
    xnt, xn_rdy = load_xnT(c, b["xn0"], "s2_xnT")
    wbufs = [c.sb(f"s2_w{i}", [128, NT * 128], BF16) for i in range(2)]
    cibufs = [c.sb(f"s2_ci{i}", [128, 4 + T], BF16) for i in range(2)]
    cvf = c.sb("s2_cvf", [128, T], F32)
    cvo = [c.sb(f"s2_cvo{i}", [128, T], BF16) for i in range(2)]
    cw = c.sb("s2_cw", [128, 33 * 4], F32)
    cb = c.sb("s2_cb", [128, 33], F32)
    pss = [c.ps(f"s2_p{i}", [128, 512]) for i in range(4)]
    wsrc = b["ipT"].rearrange("(dt p) ch -> p dt ch", p=128)

    c.dma("sync", cw[:].rearrange("p (k c) -> p k c", c=4),
          b["convw"].rearrange("k p c -> p k c"))
    c.dma("sync", cb[:], b["convb"])
    c.opi("vector", "memset", cibufs[0][:, 0:4], 0)
    z1 = c.opi("vector", "memset", cibufs[1][:, 0:4], 0)

    c.wait("tensor", "sd", xn_rdy)
    wld = [0, 0]
    wdone = [0, 0]
    ev_done = [0] * 4
    ci_done = [0, 0]
    cv_done = [0, 0]
    cvs_done = [0, 0]
    for ct in range(33):
        wb = ct % 2
        if ct >= 2:
            c.wait("gpsimd", "te", wdone[wb])
        wld[wb] = c.dma("gpsimd",
                        wbufs[wb][:].rearrange("p (dt ch) -> p dt ch", ch=128),
                        wsrc[:, :, 4096 + ct * 128: 4096 + (ct + 1) * 128])
        c.wait("tensor", "gd", wld[wb])
        cib = ct % 2
        for tc in range(4):
            pi = tc
            c.wait("tensor", "se", ev_done[pi])
            for dt_i in range(NT):
                ins = c.op("tensor", "matmul", pss[pi][:],
                           wbufs[wb][:, dt_i * 128:(dt_i + 1) * 128],
                           xnt(dt_i, tc * 512, (tc + 1) * 512),
                           start=(dt_i == 0), stop=(dt_i == NT - 1))
            mm = c.inc(ins, "te")
            if tc == 3:
                wdone[wb] = mm
            c.wait("scalar", "te", mm)
            if ct >= 2 and tc == 0:
                c.wait("scalar", "ve", ci_done[cib])
            ev_done[pi] = c.opi("scalar", "activation",
                                cibufs[cib][:, 4 + tc * 512: 4 + (tc + 1) * 512],
                                pss[pi][:], AF.Copy)
        c.wait("vector", "se", ev_done[3])
        if ct >= 2:
            c.wait("vector", "se", cv_done[cib])
        c.op("vector", "tensor_scalar", cvf[:], cibufs[cib][:, 1:1 + T],
             cw[:, 4 * ct: 4 * ct + 1], cb[:, ct:ct + 1], ALU.mult, op1=ALU.add)
        ins = None
        for k in range(1, 4):
            ins = c.op("vector", "scalar_tensor_tensor", cvf[:],
                       cibufs[cib][:, 1 + k:1 + k + T],
                       cw[:, 4 * ct + k: 4 * ct + k + 1], cvf[:], ALU.mult, ALU.add)
        ci_done[cib] = c.inc(ins, "ve")
        co = ct % 2
        c.wait("scalar", "ve", ci_done[cib])
        if ct >= 2:
            c.wait("scalar", "sd", cvs_done[co])
        cv_done[co] = c.opi("scalar", "activation", cvo[co][:], cvf[:], AF.Silu)
        c.wait("sync", "se", cv_done[co])
        if ct < 32:
            cvs_done[co] = c.dma("sync", b["xconvT"][ct * 128:(ct + 1) * 128, :],
                                 cvo[co][:])
        else:
            c.dma("sync", b["bcT"][:, 0:T], cvo[co][0:64, :])
            cvs_done[co] = c.dma("sync", b["bcT"][:, T:2 * T], cvo[co][64:128, :])

    dwb = c.sb("s2_dw", [128, NT * 64], BF16)
    dtraw = c.sb("s2_dtraw", [64, T], F32)
    sc1 = c.sb("s2_sc1", [64, T], F32)
    sc2 = c.sb("s2_sc2", [64, T], F32)
    dts = c.sb("s2_dts", [64, T], F32)
    csb = c.sb("s2_csb", [64, T], F32)
    smal = c.sb("s2_smal", [64, 2], F32)
    dld = c.dma("gpsimd", dwb[:].rearrange("p (dt ch) -> p dt ch", ch=64),
                wsrc[:, :, 8320:8384])
    c.dma("sync", smal[:, 0:1], b["dtb"])
    bld = c.dma("sync", smal[:, 1:2], b["alog"])
    c.wait("scalar", "sd", bld)
    na = c.opi("scalar", "activation", smal[:, 1:2], smal[:, 1:2], AF.Exp)
    c.wait("vector", "se", na)
    nv = c.opi("vector", "tensor_scalar", smal[:, 1:2], smal[:, 1:2], -1.0, None, ALU.mult)
    c.wait("tensor", "gd", dld)
    for tc in range(4):
        c.wait("tensor", "se", ev_done[tc])
        for dt_i in range(NT):
            ins = c.op("tensor", "matmul", pss[tc][0:64, :],
                       dwb[:, dt_i * 64:(dt_i + 1) * 64],
                       xnt(dt_i, tc * 512, (tc + 1) * 512),
                       start=(dt_i == 0), stop=(dt_i == NT - 1))
        mm = c.inc(ins, "te")
        c.wait("vector", "te", mm)
        c.op("vector", "tensor_copy", dtraw[:, tc * 512:(tc + 1) * 512], pss[tc][0:64, :])
    av = c.opi("vector", "tensor_scalar", sc1[:], dtraw[:], smal[:, 0:1], None, ALU.add)
    c.wait("scalar", "ve", av)
    c.op("scalar", "activation", sc2[:], sc1[:], AF.Abs)
    c.op("scalar", "activation", sc2[:], sc2[:], AF.Exp, scale=-1.0)
    ee = c.opi("scalar", "activation", sc2[:], sc2[:], AF.Ln, bias=1.0)
    c.wait("vector", "se", ee)
    c.op("vector", "tensor_scalar", dts[:], sc1[:], 0.0, None, ALU.max)
    dt_ok = c.opi("vector", "tensor_add", dts[:], dts[:], sc2[:])
    c.wait("sync", "ve", dt_ok)
    c.dma("sync", b["dtT"], dts[:])
    c.op("vector", "tensor_scalar", sc1[:], dts[:], smal[:, 1:2], None, ALU.mult)
    ins = None
    for ch in range(NCH):
        ins = c.op("vector", "tensor_tensor_scan", csb[:, ch * Q:(ch + 1) * Q],
                   sc1[:, ch * Q:(ch + 1) * Q], sc1[:, ch * Q:(ch + 1) * Q],
                   0.0, ALU.add, ALU.bypass)
    cs_ok = c.inc(ins, "ve")
    c.wait("sync", "ve", cs_ok)
    c.dma("sync", b["csT"], csb[:])
    c.wait("sync", "sd", c.cnt["sd"])


def emit_s3a(c, b):
    bc = c.sb("bc", [64, 2 * T], BF16)
    dexp = c.sb("dexp", [128, 64], F32)
    ones = c.sb("ones", [1, 128], F32)
    onecb = c.sb("onecb", [128, 1], BF16)
    idbf = c.sb("idbf", [128, 128], BF16)
    idf = c.sb("idf", [128, 128], F32)
    triu = c.sb("triu", [128, 128], F32)
    drow = c.sb("drow", [1, 64], F32)
    nw = c.sb("nw", [128, 32], F32)
    wsb = c.sb("wsb", [128, 32 * 1024], BF16)
    ob = c.sb("ob", [128, 1024], F32)
    c.dma("sync", bc[:], b["bcT"])
    c.dma("sync", ones[:], b["c_ones_f"])
    c.dma("sync", onecb[:], b["c_onescol_bf"])
    c.dma("sync", idbf[:], b["c_id_bf"])
    c.dma("sync", idf[:], b["c_id_f"])
    c.dma("sync", triu[:], b["c_triu_f"])
    c.dma("sync", nw[:], b["nw"])
    v = c.dma("sync", drow[:], b["dvec"])
    wsrc = b["opT"].rearrange("(k p) d -> p k d", p=128)
    wld = c.dma("gpsimd", wsb[:].rearrange("p (k d) -> p k d", d=1024),
                wsrc[:, :, 0:1024])
    p_bc = c.ps("pbc", [128, 1024])
    p_A = c.ps("pA", [128, 512])
    p_B = c.ps("pB", [128, 512])
    p_st = c.ps("pst", [128, 512])
    p_m = c.ps("pm", [128, 512])
    p_bf = c.ps("pbf", [128, 512])
    xnat = [c.sb(f"xn{i}", [128, INTER], BF16) for i in range(2)]
    xtil = c.sb("xtil", [128, INTER], BF16)
    xdd = c.sb("xdd", [128, INTER], BF16)
    gbuf = c.sb("g0", [128, INTER], BF16)
    csn = c.sb("csn", [128, 64], F32)
    dtn = c.sb("dtn", [128, 64], F32)
    csrow = c.sb("csrow", [1, 1024], F32)
    cschk = c.sb("cschk", [64, 128], F32)
    dtchk = c.sb("dtchk", [64, 128], F32)
    csrT = c.sb("csrT", [1, 64], F32)
    ebcT = c.sb("ebcT", [64, 64], F32)
    dltT = c.sb("dltT", [64, 128], F32)
    dln = c.sb("dln", [128, 64], F32)
    bnat = c.sb("bnat", [128, 64], BF16)
    bhat = c.sb("bhat", [128, 512], BF16)
    mtp = c.sb("mtp", [128, 128], BF16)
    rall = c.sb("rall", [128, 1024], F32)
    lall = c.sb("lall", [128, 1024], F32)
    mall = c.sb("mall", [128, 8 * 128], BF16)
    epair = c.sb("ep", [128, 512], F32)
    yt = c.sb("yt", [128, INTER], F32)
    tei = c.sb("tei", [128, 512], BF16)
    s_sb = [c.sb(f"s{i}", [64, HEADS * PD], F32) for i in range(2)]
    stmp = c.sb("stmp", [64, 512], F32)
    sbfg = c.sb("sbfg", [64, 512], BF16)
    rstd = c.sb("rstd", [1, 128], F32)
    rbc = c.sb("rbc", [128, 128], F32)
    yfb = [c.sb(f"yf{i}", [128, INTER], BF16) for i in range(2)]

    c.opi("vector", "memset", s_sb[0][:], 0)
    c.wait("vector", "gd", wld)
    ins = None
    for k in range(32):
        ins = c.op("vector", "tensor_scalar", wsb[:, k * 1024:(k + 1) * 1024],
                   wsb[:, k * 1024:(k + 1) * 1024], nw[:, k:k + 1], None, ALU.mult)
    wok = c.inc(ins, "ve")
    c.wait("tensor", "sd", v)
    t0_ = c.inc(c.op("tensor", "matmul", p_m[:, 0:64], ones[0:1, :], drow[0:1, :],
                     start=True, stop=True), "te")
    c.wait("vector", "te", t0_)
    pm_free = c.opi("vector", "tensor_copy", dexp[:], p_m[:, 0:64])

    xl = [0, 0]
    gl = 0
    ymm_prev = 0
    ymm_prev_g = 0
    var_te = 0
    mv_prev = 0
    st1_prev = 0
    yAB_prev = 0
    y2_prev = 0
    xcons = [0, 0]
    gcons = 0
    crow_cons = 0
    chk_cons = 0
    chk_scons = 0
    bn_free = 0
    bc_free = 0
    yAB_free = 0
    st_free = 0
    yst = [0, 0]
    yf_ve = [0, 0]
    s3b_te = [0, 0]
    s3b_ev = [0, 0]
    ob_ev = 0
    ob_st = 0
    for ch in range(NCH):
        par = ch % 2
        t0, t1 = ch * Q, (ch + 1) * Q
        if ch >= 2:
            c.wait("sync", "te", xcons[par])
        xl[par] = c.dma("sync", xnat[par][:], b["xconvT"][:, t0:t1], transpose=True)
        c.wait("sync", "ve", gcons)
        gl = c.dma("sync", gbuf[:].rearrange("p (k t) -> p k t", t=Q),
                   b["gateT"].rearrange("(k p) t -> p k t", p=128)[:, :, t0:t1])
        c.wait("sync", "te", chk_cons)
        c.wait("sync", "ve", chk_scons)
        c.dma("sync", cschk[:], b["csT"][:, t0:t1])
        ckl = c.dma("sync", dtchk[:], b["dtT"][:, t0:t1])
        c.wait("tensor", "sd", ckl)
        c.wait("tensor", "ve", pm_free)
        c.op("tensor", "transpose", p_m[:, 0:64], cschk[:], idf[0:64, 0:64])
        c.op("tensor", "transpose", p_m[:, 64:128], dtchk[:], idf[0:64, 0:64])
        tr1 = c.inc(c.op("tensor", "transpose", p_m[0:1, 128:192],
                         cschk[:, 127:128], idf[0:64, 0:64]), "te")
        c.wait("vector", "te", tr1)
        c.op("vector", "tensor_copy", csn[:], p_m[:, 0:64])
        c.op("vector", "tensor_copy", dtn[:], p_m[:, 64:128])
        pm_free = c.opi("vector", "tensor_copy", csrT[:], p_m[0:1, 128:192])
        c.wait("tensor", "ve", pm_free)
        tb = c.inc(c.op("tensor", "matmul", p_m[0:64, 192:256], ones[0:1, 0:64],
                        csrT[0:1, :], start=True, stop=True), "te")
        c.wait("scalar", "te", tb)
        c.wait("scalar", "ve", st1_prev)
        eb = c.opi("scalar", "activation", ebcT[:], p_m[0:64, 192:256], AF.Exp)
        dl1 = c.opi("vector", "tensor_scalar", dltT[:], cschk[:],
                    cschk[:, 127:128], None, ALU.subtract)
        c.wait("scalar", "ve", dl1)
        dl2 = c.opi("scalar", "activation", dltT[:], dltT[:], AF.Exp, scale=-1.0)
        c.wait("vector", "se", dl2)
        dl3 = c.opi("vector", "tensor_mul", dltT[:], dltT[:], dtchk[:])
        chk_scons = dl3
        c.wait("tensor", "ve", dl3)
        c.wait("tensor", "se", eb)
        tr2 = c.inc(c.op("tensor", "transpose", p_m[:, 256:320], dltT[:],
                         idf[0:64, 0:64]), "te")
        chk_cons = tr2
        c.wait("vector", "te", tr2)
        pm_free = c.opi("vector", "tensor_copy", dln[:], p_m[:, 256:320])
        c.wait("tensor", "ve", max(bn_free, s3b_ev[0], s3b_ev[1]))
        trb = c.inc(c.op("tensor", "transpose", p_bf[:, 0:32].bitcast(BF16),
                         bc[:, t0:t1], idbf[0:64, 0:64]), "te")
        c.wait("vector", "te", trb)
        bn_free = c.opi("vector", "tensor_copy", bnat[:], p_bf[:, 0:32].bitcast(BF16))
        c.wait("tensor", "ve", pm_free)
        mt1 = c.inc(c.op("tensor", "matmul", p_m[:, 320:448], bc[:, t0:t1],
                         bc[:, T + t0:T + t1], start=True, stop=True), "te")
        c.wait("vector", "te", mt1)
        pm_free = c.opi("vector", "tensor_mul", mtp[:], p_m[:, 320:448], triu[:])
        mt2 = pm_free
        c.wait("vector", "sd", xl[par])
        c.wait("vector", "te", ymm_prev)
        xt1 = c.opi("vector", "tensor_tensor",
                    xtil[:].rearrange("p (h q) -> p h q", q=PD),
                    xnat[par][:].rearrange("p (h q) -> p h q", q=PD),
                    dtn[:].unsqueeze(2).broadcast_to([128, HEADS, PD]), ALU.mult)
        c.wait("vector", "te", var_te)
        xd1 = c.opi("vector", "tensor_tensor",
                    xdd[:].rearrange("p (h q) -> p h q", q=PD),
                    xnat[par][:].rearrange("p (h q) -> p h q", q=PD),
                    dexp[:, 0:HEADS].unsqueeze(2).broadcast_to([128, HEADS, PD]), ALU.mult)
        sprev = s_sb[ch % 2]
        snew = s_sb[(ch + 1) % 2]
        yt_guard = y2_prev
        if ch >= 2:
            tt = ch - 2
            tpar = tt % 2
            c.wait("tensor", "ve", max(yf_ve[tpar], wok, bn_free))
            for dc in range(2):
                c.wait("tensor", "ve", max(s3b_ev[0], s3b_ev[1]))
                ins = None
                for k in range(32):
                    ins = c.op("tensor", "matmul", p_bf[:, 0:512],
                               yfb[tpar][:, k * 128:(k + 1) * 128],
                               wsb[:, k * 1024 + dc * 512: k * 1024 + (dc + 1) * 512],
                               start=(k == 0), stop=(k == 31))
                mm = c.inc(ins, "te")
                if dc == 1:
                    s3b_te[tpar] = mm
                c.wait("vector", "te", mm)
                if dc == 0:
                    c.wait("vector", "sd", ob_st)
                s3b_ev[dc] = c.opi("vector", "tensor_copy",
                                   ob[:, dc * 512:(dc + 1) * 512], p_bf[:, 0:512])
            c.wait("sync", "ve", s3b_ev[1])
            ob_st = c.dma("sync", b["mham"][tt * 128:(tt + 1) * 128, 0:1024], ob[:])
        for g in range(8):
            h0 = g * 8
            c.wait("sync", "te", crow_cons)
            crl = c.dma("sync", csrow[:].rearrange("o (h t) -> o h t", t=Q),
                        b["csT"][h0:h0 + 8, t0:t1].unsqueeze(0))
            c.wait("tensor", "se", bc_free)
            c.wait("tensor", "sd", crl)
            c.op("tensor", "matmul", p_bc[:, 0:512], ones[0:1, :],
                 csrow[0:1, 0:512], start=True, stop=True)
            bcv = c.inc(c.op("tensor", "matmul", p_bc[:, 512:1024], ones[0:1, :],
                             csrow[0:1, 512:1024], start=True, stop=True), "te")
            crow_cons = bcv
            c.wait("scalar", "te", bcv)
            c.wait("scalar", "ve", yAB_prev)
            ins = None
            for hh in range(8):
                ins = c.op("scalar", "activation", rall[:, hh * Q:(hh + 1) * Q],
                           p_bc[:, hh * Q:(hh + 1) * Q], AF.Relu, scale=-1.0,
                           bias=csn[:, h0 + hh: h0 + hh + 1])
            c.inc(ins, "se")
            c.op("scalar", "activation",
                 epair[0:64, :].rearrange("p (k t) -> p k t", t=Q),
                 p_bc[0:64, :].rearrange("p (hh t) -> p hh t", t=Q)[:, 0:8:2, :],
                 AF.Exp)
            epv = c.inc(c.op("scalar", "activation",
                             epair[64:128, :].rearrange("p (k t) -> p k t", t=Q),
                             p_bc[64:128, :].rearrange("p (hh t) -> p hh t", t=Q)[:, 1:8:2, :],
                             AF.Exp), "se")
            c.wait("scalar", "ve", mv_prev)
            lv = c.opi("scalar", "activation", lall[:], rall[:], AF.Exp, scale=-1.0)
            bc_free = lv
            c.wait("vector", "se", lv)
            c.wait("vector", "ve", mt2)
            c.wait("vector", "te", ymm_prev_g)
            mv = c.opi("vector", "tensor_tensor",
                       mall[:].rearrange("p (hh t) -> p hh t", t=Q),
                       lall[:].rearrange("p (hh t) -> p hh t", t=Q),
                       mtp[:].unsqueeze(1).broadcast_to([128, 8, Q]), ALU.mult)
            c.wait("vector", "te", trb)
            c.wait("vector", "se", eb)
            bh1 = c.opi("vector", "tensor_tensor",
                        bhat[:].rearrange("p (h n) -> p h n", n=ND),
                        bnat[:].unsqueeze(1).broadcast_to([128, 8, ND]),
                        dln[:, h0:h0 + 8].unsqueeze(2).broadcast_to([128, 8, ND]),
                        ALU.mult)
            st1 = c.opi("vector", "tensor_tensor",
                        stmp[:].rearrange("n (h q) -> n h q", q=PD),
                        sprev[:, h0 * PD:(h0 + 8) * PD].rearrange("n (h q) -> n h q", q=PD),
                        ebcT[:, h0:h0 + 8].unsqueeze(2).broadcast_to([64, 8, PD]),
                        ALU.mult)
            st1 = c.opi("vector", "tensor_copy", sbfg[:],
                        sprev[:, h0 * PD:(h0 + 8) * PD])
            st1_prev = st1
            c.wait("tensor", "ve", mv)
            c.wait("tensor", "ve", max(xt1, xd1, bh1, st1, yAB_free, st_free))
            insA = insB = None
            for kk in range(4):
                for e in range(2):
                    h = h0 + 2 * kk + e
                    tp = (0, 64 * e)
                    rows = slice(64 * e, 64 * e + 64)
                    c.op("tensor", "matmul", p_A[rows, kk * 128:(kk + 1) * 128],
                         xtil[:, h * PD:(h + 1) * PD],
                         mall[:, (2 * kk + e) * Q:(2 * kk + e + 1) * Q],
                         start=True, stop=False, tile_position=tp)
                    insA = c.op("tensor", "matmul", p_A[rows, kk * 128:(kk + 1) * 128],
                                xdd[:, h * PD:(h + 1) * PD], idbf[:],
                                start=False, stop=True, tile_position=tp)
                    insB = c.op("tensor", "matmul", p_B[rows, kk * 128:(kk + 1) * 128],
                                sbfg[:, (2 * kk + e) * PD:(2 * kk + e + 1) * PD],
                                bc[:, T + t0:T + t1],
                                start=True, stop=True, tile_position=tp)
            c.inc(insA, "te")
            yB = c.inc(insB, "te")
            mv_prev = mv
            inss = None
            for hh in range(8):
                h = h0 + hh
                inss = c.op("tensor", "matmul", p_st[0:64, hh * 64:(hh + 1) * 64],
                            bhat[:, hh * ND:(hh + 1) * ND],
                            xnat[par][:, h * PD:(h + 1) * PD],
                            start=True, stop=True)
            yS = c.inc(inss, "te")
            ymm_prev_g = yS
            if g == 7:
                xcons[par] = yS
                ymm_prev = yS
            c.wait("vector", "te", yS)
            c.wait("vector", "se", epv)
            c.wait("vector", "se", yt_guard)
            c.op("vector", "tensor_tensor", tei[:], p_B[:, 0:512],
                 epair[:], ALU.mult)
            yAB_free = c.opi("vector", "tensor_tensor", yt[:, h0 * PD:(h0 + 8) * PD],
                             p_A[:, 0:512], tei[:], ALU.add)
            yAB_prev = yAB_free
            st_free = c.opi("vector", "tensor_add", snew[:, h0 * PD:(h0 + 8) * PD],
                            stmp[:], p_st[0:64, :])
        c.wait("scalar", "ve", yAB_free)
        y2v = c.opi("scalar", "activation", xdd[:], yt[:], AF.Square)
        y2_prev = y2v
        c.wait("tensor", "se", y2v)
        c.wait("tensor", "ve", pm_free)
        insv = None
        for kk in range(32):
            insv = c.op("tensor", "matmul", p_m[0:1, 0:128], onecb[:],
                        xdd[:, kk * 128:(kk + 1) * 128], start=(kk == 0), stop=(kk == 31))
        vv = c.inc(insv, "te")
        var_te = vv
        c.wait("vector", "te", vv)
        r1 = c.opi("vector", "tensor_scalar", rstd[:], p_m[0:1, 0:128],
                   1.0 / INTER, EPS_MN, ALU.mult, op1=ALU.add)
        c.wait("scalar", "ve", r1)
        c.op("scalar", "activation", rstd[:], rstd[:], AF.Ln)
        r2 = c.opi("scalar", "activation", rstd[:], rstd[:], AF.Exp, scale=-0.5)
        c.wait("tensor", "se", r2)
        rb = c.inc(c.op("tensor", "matmul", p_m[:, 128:256], ones[0:1, :],
                        rstd[0:1, :], start=True, stop=True), "te")
        c.wait("vector", "te", rb)
        pm_free = c.opi("vector", "tensor_copy", rbc[:], p_m[:, 128:256])
        c.wait("vector", "sd", gl)
        if ch >= 2:
            c.wait("vector", "sd", yst[par])
        c.wait("vector", "te", s3b_te[par])
        c.op("vector", "tensor_tensor", yt[:].rearrange("p (k t2) -> p k t2", t2=128),
             yt[:].rearrange("p (k t2) -> p k t2", t2=128),
             rbc[:].unsqueeze(1).broadcast_to([128, 32, 128]), ALU.mult)
        gcons = c.opi("vector", "tensor_mul", yfb[par][:], yt[:], gbuf[:])
        yf_ve[par] = gcons
        c.wait("sync", "ve", gcons)
        yst[par] = c.dma("sync",
                         b["yfT"].rearrange("(k p) t -> p k t", p=128)[:, :, t0:t1],
                         yfb[par][:].rearrange("p (k t2) -> p k t2", t2=128))
    for tt in (NCH - 2, NCH - 1):
        tpar = tt % 2
        c.wait("tensor", "ve", yf_ve[tpar])
        for dc in range(2):
            c.wait("tensor", "ve", max(s3b_ev[0], s3b_ev[1]))
            ins = None
            for k in range(32):
                ins = c.op("tensor", "matmul", p_bf[:, 0:512],
                           yfb[tpar][:, k * 128:(k + 1) * 128],
                           wsb[:, k * 1024 + dc * 512: k * 1024 + (dc + 1) * 512],
                           start=(k == 0), stop=(k == 31))
            mm = c.inc(ins, "te")
            c.wait("vector", "te", mm)
            if dc == 0:
                c.wait("vector", "sd", ob_st)
            s3b_ev[dc] = c.opi("vector", "tensor_copy", ob[:, dc * 512:(dc + 1) * 512],
                               p_bf[:, 0:512])
        c.wait("sync", "ve", s3b_ev[1])
        ob_st = c.dma("sync", b["mham"][tt * 128:(tt + 1) * 128, 0:1024], ob[:])
    c.wait("gpsimd", "te", c.cnt["te"])
    wld = c.dma("gpsimd", wsb[:].rearrange("p (k d) -> p k d", d=1024),
                wsrc[:, :, 1024:2048])
    c.wait("vector", "gd", wld)
    ins = None
    for k in range(32):
        ins = c.op("vector", "tensor_scalar", wsb[:, k * 1024:(k + 1) * 1024],
                   wsb[:, k * 1024:(k + 1) * 1024], nw[:, k:k + 1], None, ALU.mult)
    wok = c.inc(ins, "ve")
    c.wait("tensor", "ve", wok)
    ycons = [0, 0]
    for tt in range(NT):
        tpar = tt % 2
        if tt >= 2:
            c.wait("sync", "te", ycons[tpar])
        yld = c.dma("sync", xnat[tpar][:].rearrange("p (k t) -> p k t", t=128),
                    b["yfT"].rearrange("(k p) t -> p k t", p=128)[:, :, tt * 128:(tt + 1) * 128])
        c.wait("tensor", "sd", yld)
        for dc in range(2):
            c.wait("tensor", "ve", max(s3b_ev[0], s3b_ev[1]))
            ins = None
            for k in range(32):
                ins = c.op("tensor", "matmul", p_bf[:, 0:512],
                           xnat[tpar][:, k * 128:(k + 1) * 128],
                           wsb[:, k * 1024 + dc * 512: k * 1024 + (dc + 1) * 512],
                           start=(k == 0), stop=(k == 31))
            mm = c.inc(ins, "te")
            if dc == 1:
                ycons[tpar] = mm
            c.wait("vector", "te", mm)
            if dc == 0:
                c.wait("vector", "sd", ob_st)
            s3b_ev[dc] = c.opi("vector", "tensor_copy", ob[:, dc * 512:(dc + 1) * 512],
                               p_bf[:, 0:512])
        c.wait("sync", "ve", s3b_ev[1])
        ob_st = c.dma("sync", b["mham"][tt * 128:(tt + 1) * 128, 1024:2048], ob[:])
    c.wait("sync", "sd", c.cnt["sd"])


def emit_dualproj(c, b, wa, wb_, dst, fwht, xn_src):
    xnt, xn_rdy = load_xnT(c, b[xn_src], "dp_xnT")
    h128 = c.sb("dp_h128", [128, 128], BF16)
    if fwht:
        c.dma("sync", h128[:], b["c_h128"])
    wbufs = [c.sb(f"dp_w{i}", [128, NT * 128], BF16) for i in range(4)]
    sbufs = [c.sb(f"dp_s{i}", [128, 512], F32) for i in range(2)]
    hbufs = [c.sb(f"dp_h{i}", [128, 512], BF16) for i in range(2)]
    obufs = [c.sb(f"dp_o{i}", [128, 512], BF16) for i in range(4)]
    psu = [c.ps(f"dp_pu{i}", [128, 512]) for i in range(2)]
    psv = [c.ps(f"dp_pv{i}", [128, 512]) for i in range(2)]
    psw = [c.ps(f"dp_pw{i}", [128, 512]) for i in range(2)]
    wsrca = b[wa].rearrange("(dt p) hh -> p dt hh", p=128)
    wsrcb = b[wb_].rearrange("(dt p) hh -> p dt hh", p=128)
    c.wait("tensor", "sd", xn_rdy)
    wld = [0, 0]
    wdone = [0, 0]
    sil_done = [0, 0]
    mul_done = [0, 0]
    st1_done = [0, 0]
    mul_prev = [0, 0]
    st1_prev = [0, 0]
    ost = [0] * 4
    for a in range(64):
        par = a % 2
        if a >= 2:
            c.wait("gpsimd", "te", wdone[par])
        c.dma("gpsimd", wbufs[par][:].rearrange("p (dt hh) -> p dt hh", hh=128),
              wsrca[:, :, a * 128:(a + 1) * 128])
        wld[par] = c.dma("gpsimd", wbufs[2 + par][:].rearrange("p (dt hh) -> p dt hh", hh=128),
                         wsrcb[:, :, a * 128:(a + 1) * 128])
        c.wait("tensor", "gd", wld[par])
        for tc in range(4):
            oi = (a * 4 + tc) % 4
            bi = (a * 4 + tc) % 2
            sbuf = sbufs[bi]
            hbuf = hbufs[bi]
            c.wait("tensor", "se", sil_done[par])
            ins = None
            for dt_i in range(NT):
                ins = c.op("tensor", "matmul", psu[par][:],
                           wbufs[par][:, dt_i * 128:(dt_i + 1) * 128],
                           xnt(dt_i, tc * 512, (tc + 1) * 512),
                           start=(dt_i == 0), stop=(dt_i == NT - 1))
            mmu = c.inc(ins, "te")
            c.wait("tensor", "ve", mul_done[par])
            for dt_i in range(NT):
                ins = c.op("tensor", "matmul", psv[par][:],
                           wbufs[2 + par][:, dt_i * 128:(dt_i + 1) * 128],
                           xnt(dt_i, tc * 512, (tc + 1) * 512),
                           start=(dt_i == 0), stop=(dt_i == NT - 1))
            mmv = c.inc(ins, "te")
            if tc == 3:
                wdone[par] = mmv
            c.wait("scalar", "te", mmu)
            c.wait("scalar", "ve", mul_prev[bi])
            sil_done[par] = c.opi("scalar", "activation", sbuf[:], psu[par][:], AF.Silu)
            c.wait("vector", "se", sil_done[par])
            c.wait("vector", "te", mmv)
            if fwht:
                c.wait("vector", "te", st1_prev[bi])
                mul_done[par] = c.opi("vector", "tensor_mul", hbuf[:], sbuf[:], psv[par][:])
                mul_prev[bi] = mul_done[par]
                c.wait("tensor", "ve", mul_done[par])
                c.wait("tensor", "ve", st1_done[par])
                mmw = c.inc(c.op("tensor", "matmul", psw[par][:], h128[:], hbuf[:],
                                 start=True, stop=True), "te")
                st1_prev[bi] = mmw
                c.wait("vector", "te", mmw)
                c.wait("vector", "sd", ost[oi])
                st1_done[par] = c.opi("vector", "tensor_copy", obufs[oi][:], psw[par][:])
                dv = st1_done[par]
            else:
                c.wait("vector", "sd", ost[oi])
                mul_done[par] = c.opi("vector", "tensor_mul", obufs[oi][:], sbuf[:], psv[par][:])
                mul_prev[bi] = mul_done[par]
                dv = mul_done[par]
            c.wait("sync", "ve", dv)
            ost[oi] = c.dma("sync", dst[a, :, tc * 512:(tc + 1) * 512], obufs[oi][:])
    c.wait("sync", "sd", c.cnt["sd"])


def emit_s4(c, b):
    emit_dualproj(c, b, "w1T", "w3T", b["vsp"], True, "xn0")


def emit_s9(c, b):
    emit_dualproj(c, b, "wgT", "wuT", b["hid1"], False, "xn1")


def emit_s5(c, b):
    hp = c.sb("hp", [128, 128], BF16)
    v0 = c.dma("sync", hp[:], b["c_hp"])
    w2 = c.sb("w2", [128, 64 * 1024], BF16)
    vt = [c.sb(f"vt{i}", [128, 64 * 128], BF16) for i in range(2)]
    s2 = c.sb("s2", [128, 64 * 128], BF16)
    xb = [c.sb(f"x{i}", [128, 1024], F32) for i in range(2)]
    mb = [c.sb(f"mm{i}", [128, 1024], F32) for i in range(2)]
    ps_s2 = [c.ps(f"ps{i}", [128, 512]) for i in range(2)]
    ps_o = [c.ps(f"po{i}", [128, 512]) for i in range(4)]
    vcons = [0, 0]
    s2done = [0, 0]
    s2cons = [0, 0]
    evd = [0] * 4
    ostd = [0, 0]
    for half in range(2):
        d0 = half * 1024
        if half > 0:
            c.wait("gpsimd", "te", c.cnt["te"])
        wld = c.dma("gpsimd", w2[:].rearrange("p (k d) -> p k d", d=1024),
                    b["w2Tp"][:, :, d0:d0 + 1024].rearrange("k p d -> p k d"))
        c.wait("tensor", "gd", wld)
        c.wait("tensor", "sd", v0)
        for tt in range(NT):
            par = tt % 2
            t0 = tt * 128
            if half + tt >= 2:
                c.wait("sync", "te", vcons[par])
            c.dma("sync", vt[par][0:64, :].rearrange("p (k t) -> p k t", t=128),
                  b["vsp"][:, 0:128:2, t0:t0 + 128])
            vld = c.dma("sync", vt[par][64:128, :].rearrange("p (k t) -> p k t", t=128),
                        b["vsp"][:, 1:128:2, t0:t0 + 128])
            c.wait("sync", "ve", max(evd[par * 2], evd[par * 2 + 1]))
            c.wait("sync", "sd", ostd[par])
            c.dma("sync", xb[par][:], b["x0"][t0:t0 + 128, d0:d0 + 1024])
            ml = c.dma("sync", mb[par][:], b["mham"][t0:t0 + 128, d0:d0 + 1024])
            c.wait("tensor", "sd", vld)
            for g in range(16):
                pi = g % 2
                c.wait("tensor", "ve", s2done[pi])
                ins = None
                for j in range(4):
                    bp = g * 4 + j
                    ins = c.op("tensor", "matmul", ps_s2[pi][:, j * 128:(j + 1) * 128],
                               hp[:], vt[par][:, bp * 128:(bp + 1) * 128],
                               start=True, stop=True)
                mm = c.inc(ins, "te")
                if g == 15:
                    vcons[par] = mm
                c.wait("vector", "te", mm)
                c.wait("vector", "te", s2cons[pi])
                s2done[pi] = c.opi("vector", "tensor_copy",
                                   s2[:, g * 512:(g + 1) * 512], ps_s2[pi][:])
                if g > 0:
                    gp = g - 1
                    c.wait("tensor", "ve", s2done[1 - pi])
                    for dc in range(2):
                        po = par * 2 + dc
                        if gp == 0:
                            c.wait("tensor", "ve", evd[po])
                        ins = None
                        for j in range(4):
                            bp = gp * 4 + j
                            ins = c.op("tensor", "matmul", ps_o[po][:],
                                       s2[:, bp * 128:(bp + 1) * 128],
                                       w2[:, bp * 1024 + dc * 512: bp * 1024 + (dc + 1) * 512],
                                       start=(bp == 0), stop=False)
                    s2cons[gp % 2] = c.inc(ins, "te")
            c.wait("tensor", "ve", s2done[1])
            for dc in range(2):
                po = par * 2 + dc
                ins = None
                for j in range(4):
                    bp = 60 + j
                    ins = c.op("tensor", "matmul", ps_o[po][:],
                               s2[:, bp * 128:(bp + 1) * 128],
                               w2[:, bp * 1024 + dc * 512: bp * 1024 + (dc + 1) * 512],
                               start=False, stop=True)
                mmo = c.inc(ins, "te")
                s2cons[1] = mmo
                c.wait("vector", "te", mmo)
                c.wait("vector", "sd", ml)
                c.op("vector", "tensor_add", xb[par][:, dc * 512:(dc + 1) * 512],
                     xb[par][:, dc * 512:(dc + 1) * 512],
                     mb[par][:, dc * 512:(dc + 1) * 512])
                evd[po] = c.opi("vector", "tensor_add",
                                xb[par][:, dc * 512:(dc + 1) * 512],
                                xb[par][:, dc * 512:(dc + 1) * 512], ps_o[po][:])
            c.wait("sync", "ve", evd[par * 2 + 1])
            ostd[par] = c.dma("sync", b["x1"][t0:t0 + 128, d0:d0 + 1024], xb[par][:])
    c.wait("sync", "sd", c.cnt["sd"])


def emit_s7(c, b):
    xnt, xn_rdy = load_xnT(c, b["xn1"], "xnT")
    onescol = c.sb("oc", [128, 1], BF16)
    onesf = c.sb("on", [1, 128], F32)
    triub = c.sb("tr", [128, 128], BF16)
    c.dma("sync", onescol[:], b["c_onescol_bf"])
    c.dma("sync", onesf[:], b["c_ones_f"])
    v0 = c.dma("sync", triub[:], b["c_triu_bf"])
    wk = [c.sb(f"wk{i}", [128, NT * 128], BF16) for i in range(2)]
    wq = [c.sb(f"wq{i}", [128, NT * 128], BF16) for i in range(2)]
    wv = [c.sb(f"wv{i}", [128, NT * 128], BF16) for i in range(2)]
    kT = c.sb("kT", [128, T], BF16)
    qT = c.sb("qT", [128, T], BF16)
    vn = c.sb("vn", [128, NT * 128], BF16)
    pb = [c.sb(f"pb{i}", [128, 512], BF16) for i in range(2)]
    den = c.sb("den", [1, T], F32)
    rn = c.sb("rn", [1, T], F32)
    rnb = c.sb("rnb", [128, 512], F32)
    zraw = c.sb("zraw", [128, T], BF16)
    zb = [c.sb(f"z{i}", [128, T], BF16) for i in range(2)]
    ps_s = [c.ps(f"ps{i}", [128, 512]) for i in range(2)]
    ps_z = [c.ps(f"pz{i}", [128, 256]) for i in range(2)]
    ps_d = c.ps("pd", [1, 512])
    ps_q = [c.ps(f"pq{i}", [128, 512]) for i in range(2)]
    ps_r = c.ps("pr", [128, 512])
    wks = b["wkT"].rearrange("(dt p) f -> p dt f", p=128)
    wqs = b["wqT"].rearrange("(dt p) f -> p dt f", p=128)
    wvs = b["wvT"].rearrange("(dt p) f -> p dt f", p=128)
    c.wait("tensor", "sd", max(xn_rdy, v0))
    wld = [0, 0]
    wcons = [0, 0]
    kq_ev = [0, 0]
    st_free = [0, 0]
    pb_free = [0, 0]
    z_free = [0, 0]
    d_free = [0, 0]
    r_free = 0
    zst = [0, 0]
    for h in range(AH):
        par = h % 2
        if h >= 2:
            c.wait("gpsimd", "te", wcons[par])
        c.dma("gpsimd", wk[par][:].rearrange("p (dt f) -> p dt f", f=128),
              wks[:, :, h * 128:(h + 1) * 128])
        c.dma("gpsimd", wq[par][:].rearrange("p (dt f) -> p dt f", f=128),
              wqs[:, :, h * 128:(h + 1) * 128])
        wld[par] = c.dma("gpsimd", wv[par][:].rearrange("p (dt f) -> p dt f", f=128),
                         wvs[:, :, h * 128:(h + 1) * 128])
        c.wait("tensor", "gd", wld[par])
        for wbuf, dstb in ((wk, kT), (wq, qT)):
            for tc in range(4):
                pi = tc % 2
                c.wait("tensor", "ve", kq_ev[pi])
                ins = None
                for dt_i in range(NT):
                    ins = c.op("tensor", "matmul", ps_q[pi][:],
                               wbuf[par][:, dt_i * 128:(dt_i + 1) * 128],
                               xnt(dt_i, tc * 512, (tc + 1) * 512),
                               start=(dt_i == 0), stop=(dt_i == NT - 1))
                mm = c.inc(ins, "te")
                c.wait("vector", "te", mm)
                kq_ev[pi] = c.opi("vector", "tensor_copy",
                                  dstb[:, tc * 512:(tc + 1) * 512], ps_q[pi][:])
        for tt2 in range(NT):
            pi = tt2 % 2
            c.wait("tensor", "ve", kq_ev[pi])
            ins = None
            for dt_i in range(NT):
                ins = c.op("tensor", "matmul", ps_q[pi][0:128, 0:128],
                           xnt(dt_i, tt2 * 128, (tt2 + 1) * 128),
                           wv[par][:, dt_i * 128:(dt_i + 1) * 128],
                           start=(dt_i == 0), stop=(dt_i == NT - 1))
            mm = c.inc(ins, "te")
            c.wait("vector", "te", mm)
            kq_ev[pi] = c.opi("vector", "tensor_copy",
                              vn[:, tt2 * 128:(tt2 + 1) * 128], ps_q[pi][0:128, 0:128])
        wcons[par] = c.cnt["te"]
        c.wait("tensor", "ve", max(kq_ev[0], kq_ev[1]))
        for qp in range(8):
            q0 = qp * 2
            njt = 2 * qp + 2
            nblk = (njt + 1) // 2
            zpi = qp % 2
            for blk in range(nblk):
                j0 = blk * 2
                nj = min(2, njt - j0)
                sb_i = blk % 2
                pbi = blk % 2
                c.wait("tensor", "se", st_free[sb_i])
                ins = None
                for jj in range(nj):
                    ins = c.op("tensor", "matmul", ps_s[sb_i][:, jj * 256:(jj + 1) * 256],
                               kT[:, (j0 + jj) * 128:(j0 + jj + 1) * 128],
                               qT[:, q0 * 128:(q0 + 2) * 128], start=True, stop=True)
                sv = c.inc(ins, "te")
                c.wait("scalar", "te", max(sv, pb_free[pbi]))
                ex = c.opi("scalar", "activation", pb[pbi][:, 0:nj * 256],
                           ps_s[sb_i][:, 0:nj * 256], AF.Exp, scale=ASCALE)
                st_free[sb_i] = ex
                mk = ex
                for jj in range(nj):
                    jt = j0 + jj
                    if jt == q0:
                        c.wait("vector", "se", ex)
                        mk = c.opi("vector", "tensor_mul",
                                   pb[pbi][:, jj * 256:jj * 256 + 128],
                                   pb[pbi][:, jj * 256:jj * 256 + 128], triub[:])
                    elif jt == q0 + 1:
                        c.wait("vector", "se", ex)
                        c.op("vector", "memset", pb[pbi][:, jj * 256:jj * 256 + 128], 0)
                        mk = c.opi("vector", "tensor_mul",
                                   pb[pbi][:, jj * 256 + 128:(jj + 1) * 256],
                                   pb[pbi][:, jj * 256 + 128:(jj + 1) * 256], triub[:])
                if mk != ex:
                    c.wait("tensor", "ve", mk)
                else:
                    c.wait("tensor", "se", ex)
                if blk == 0:
                    c.wait("tensor", "ve", max(z_free[zpi], d_free[zpi]))
                insd = None
                insz = None
                for jj in range(nj):
                    last = (blk == nblk - 1 and jj == nj - 1)
                    insz = c.op("tensor", "matmul", ps_z[zpi][:],
                                vn[:, (j0 + jj) * 128:(j0 + jj + 1) * 128],
                                pb[pbi][:, jj * 256:(jj + 1) * 256],
                                start=(blk == 0 and jj == 0), stop=last)
                    insd = c.op("tensor", "matmul", ps_d[0:1, zpi * 256:zpi * 256 + 256],
                                onescol[:], pb[pbi][:, jj * 256:(jj + 1) * 256],
                                start=(blk == 0 and jj == 0), stop=last)
                pb_free[pbi] = c.inc(insd, "te")
                zdone = c.inc(insz, "te")
            c.wait("vector", "te", zdone)
            z_free[zpi] = c.opi("vector", "tensor_copy",
                                zraw[:, q0 * 128:(q0 + 2) * 128], ps_z[zpi][:])
            d_free[zpi] = c.opi("vector", "tensor_copy",
                                den[0:1, q0 * 128:(q0 + 2) * 128],
                                ps_d[0:1, zpi * 256:zpi * 256 + 256])
        zo = h % 2
        if h >= 2:
            c.wait("vector", "sd", zst[zo])
        rv = c.opi("vector", "reciprocal", rn[:], den[:])
        c.wait("tensor", "ve", rv)
        for c4 in range(4):
            c.wait("tensor", "ve", r_free)
            rb = c.inc(c.op("tensor", "matmul", ps_r[:], onesf[0:1, :].broadcast_to([1, 512])
                            if False else onesf[0:1, 0:128],
                            rn[0:1, c4 * 512:(c4 + 1) * 512], start=True, stop=True), "te")
            c.wait("vector", "te", rb)
            r_free = c.opi("vector", "tensor_copy", rnb[:], ps_r[:])
            zst[zo] = c.opi("vector", "tensor_tensor", zb[zo][:, c4 * 512:(c4 + 1) * 512],
                            zraw[:, c4 * 512:(c4 + 1) * 512], rnb[:], ALU.mult)
        zev = zst[zo]
        c.wait("sync", "ve", zev)
        zst[zo] = c.dma("sync", b["zT"][h], zb[zo][:])
    c.wait("sync", "sd", c.cnt["sd"])


def emit_s8(c, b):
    wsb = c.sb("s8_w", [128, AH * D], BF16)
    zbufs = [c.sb(f"s8_z{i}", [128, AH * 128], BF16) for i in range(2)]
    obufs = [c.sb(f"s8_o{i}", [128, D], BF16) for i in range(2)]
    pss = [c.ps(f"s8_p{i}", [128, 512]) for i in range(8)]
    wld = c.dma("gpsimd", wsb[:].rearrange("p (k d) -> p k d", d=D),
                b["woT"].rearrange("(k p) d -> p k d", p=128))
    c.wait("tensor", "gd", wld)
    zld = [0, 0]
    zcons = [0, 0]
    ev = [0] * 8
    ost = [0, 0]
    for tt in range(NT):
        par = tt % 2
        if tt >= 2:
            c.wait("sync", "te", zcons[par])
        zld[par] = c.dma("sync", zbufs[par][:].rearrange("p (k t) -> p k t", t=128),
                         b["zT"][:, :, tt * 128:(tt + 1) * 128].rearrange("k p t -> p k t"))
        c.wait("tensor", "sd", zld[par])
        for dc in range(4):
            pi = par * 4 + dc
            c.wait("tensor", "ve", ev[pi])
            ins = None
            for k in range(AH):
                ins = c.op("tensor", "matmul", pss[pi][:],
                           zbufs[par][:, k * 128:(k + 1) * 128],
                           wsb[:, k * D + dc * 512: k * D + (dc + 1) * 512],
                           start=(k == 0), stop=(k == AH - 1))
            mm = c.inc(ins, "te")
            if dc == 3:
                zcons[par] = mm
            c.wait("vector", "te", mm)
            if dc == 0 and tt >= 2:
                c.wait("vector", "sd", ost[par])
            ev[pi] = c.opi("vector", "tensor_copy",
                           obufs[par][:, dc * 512:(dc + 1) * 512], pss[pi][:])
        c.wait("sync", "ve", ev[par * 4 + 3])
        ost[par] = c.dma("sync", b["matt"][tt * 128:(tt + 1) * 128, :], obufs[par][:])
    c.wait("sync", "sd", c.cnt["sd"])


def emit_s10(c, b):
    wd = c.sb("wd", [128, 64 * 1024], BF16)
    hb = [c.sb(f"h{i}", [128, 64 * 256], BF16) for i in range(2)]
    xb = [c.sb(f"x{i}", [128, 1024], F32) for i in range(2)]
    mb = [c.sb(f"mm{i}", [128, 1024], BF16) for i in range(2)]
    ps_o = [c.ps(f"p{i}", [128, 512]) for i in range(4)]
    hcons = [0, 0]
    evd = [0] * 4
    ostd = [0, 0]
    for half in range(2):
        d0 = half * 1024
        if half > 0:
            c.wait("gpsimd", "te", c.cnt["te"])
        wld = c.dma("gpsimd", wd[:].rearrange("p (k d) -> p k d", d=1024),
                    b["wdT"][:, d0:d0 + 1024].rearrange("(k p) d -> p k d", p=128))
        c.wait("tensor", "gd", wld)
        for tc in range(8):
            par = tc % 2
            if half + tc >= 2:
                c.wait("sync", "te", hcons[par])
            hld = c.dma("sync", hb[par][:].rearrange("p (k t) -> p k t", t=256),
                        b["hid1"][:, :, tc * 256:(tc + 1) * 256].rearrange("k p t -> p k t"))
            for ti in range(2):
                tt = tc * 2 + ti
                t0 = tt * 128
                tpar = tt % 2
                c.wait("sync", "ve", max(evd[tpar * 2], evd[tpar * 2 + 1]))
                c.wait("sync", "sd", ostd[tpar])
                c.dma("sync", xb[tpar][:], b["x1"][t0:t0 + 128, d0:d0 + 1024])
                ml = c.dma("sync", mb[tpar][:], b["matt"][t0:t0 + 128, d0:d0 + 1024])
                c.wait("tensor", "sd", hld)
                for dc in range(2):
                    pi = tpar * 2 + dc
                    c.wait("tensor", "ve", evd[pi])
                    ins = None
                    for k in range(64):
                        ins = c.op("tensor", "matmul", ps_o[pi][:],
                                   hb[par][:, k * 256 + ti * 128: k * 256 + ti * 128 + 128],
                                   wd[:, k * 1024 + dc * 512: k * 1024 + (dc + 1) * 512],
                                   start=(k == 0), stop=(k == 63))
                    mmo = c.inc(ins, "te")
                    if dc == 1 and ti == 1:
                        hcons[par] = mmo
                    c.wait("vector", "te", mmo)
                    c.wait("vector", "sd", ml)
                    c.op("vector", "tensor_add", xb[tpar][:, dc * 512:(dc + 1) * 512],
                         xb[tpar][:, dc * 512:(dc + 1) * 512],
                         mb[tpar][:, dc * 512:(dc + 1) * 512])
                    evd[pi] = c.opi("vector", "tensor_add",
                                    xb[tpar][:, dc * 512:(dc + 1) * 512],
                                    xb[tpar][:, dc * 512:(dc + 1) * 512], ps_o[pi][:])
                c.wait("sync", "ve", evd[tpar * 2 + 1])
                ostd[tpar] = c.dma("sync", b["out"][t0:t0 + 128, d0:d0 + 1024], xb[tpar][:])
    c.wait("sync", "sd", c.cnt["sd"])
    c.wait("gpsimd", "sd", c.cnt["sd"])


EMITTERS = {
    "s0": emit_s0, "s1": emit_s1, "s2": emit_s2, "s3a": emit_s3a,
    "s4": emit_s4, "s5": emit_s5, "s6": emit_s6, "s7": emit_s7, "s8": emit_s8,
    "s9": emit_s9, "s10": emit_s10,
}


def build_program(stages):
    nc = bass.Bass("TRN2", target_bir_lowering=False)
    need_in = set()
    produced = set()
    wneed = set()
    for s in stages:
        ins, outs = STAGE_IO[s]
        for n in ins:
            if n not in produced and n != "x0":
                need_in.add(n)
        produced.update(outs)
        wneed.update(STAGE_W[s])
        if s in STAGE_X:
            wneed.update(STAGE_X[s])
    fused = len(stages) == len(STAGE_IO)
    b = {}
    for name in sorted(wneed):
        if name == "x0":
            b[name] = nc.dram_tensor(name, XIN["x0"][0], XIN["x0"][2],
                                     kind="ExternalInput").ap()
        else:
            shape, _, bdt = WEIGHTS[name]
            b[name] = nc.dram_tensor(name, shape, bdt, kind="ExternalInput").ap()
    spill_out = set()
    for name in sorted(set(SPILLS)):
        shape, _, bdt = SPILLS[name]
        if name in need_in:
            b[name] = nc.dram_tensor(name, shape, bdt, kind="ExternalInput").ap()
        elif name in produced:
            if fused and not globals().get("DEBUG_SPILLS", False):
                b[name] = nc.dram_tensor(name, shape, bdt, kind="Internal").ap()
            else:
                b[name] = nc.dram_tensor(name, shape, bdt, kind="ExternalOutput").ap()
                spill_out.add(name)
    has_out = any("out" in STAGE_IO[s][1] for s in stages)
    if has_out:
        b["out"] = nc.dram_tensor("out", XOUT["out"][0], XOUT["out"][2],
                                  kind="ExternalOutput").ap()
    stage_marks = []
    with ExitStack() as es:
        c = Ctx(nc, es)
        for si, s in enumerate(stages):
            with ExitStack() as stage_es:
                c.es = stage_es
                c.prefix = s + "_"
                if si > 0:
                    c.barrier()
                EMITTERS[s](c, b)
                c.seal()
                stage_marks.append((s, nc.n_instructions()))
            c.es = es
    innames = sorted(need_in | (wneed & set(WEIGHTS)) | ({"x0"} if "x0" in b else set()))
    outnames = sorted(spill_out | ({"out"} if has_out else set()))
    nc._stage_marks = stage_marks
    return nc, innames, outnames


def hadamard(n):
    m = np.ones((1, 1), np.float32)
    while m.shape[0] < n:
        m = np.block([[m, m], [m, -m]])
    return m


def prep_host(inputs):
    g = {}
    g["ipT"] = np.ascontiguousarray(inputs["L0_attn_in_proj_w"].T)
    g["opT"] = np.ascontiguousarray(inputs["L0_attn_out_proj_w"].T)
    g["w1T"] = np.ascontiguousarray(inputs["L0_ffn_W1"].T)
    g["w3T"] = np.ascontiguousarray(inputs["L0_ffn_W3"].T)
    w2v = np.ascontiguousarray(inputs["L0_ffn_W2"].T).reshape(64, 128, D)
    w2p = np.empty((64, 128, D), np.float32)
    q = np.arange(128)
    for p in range(64):
        w2p[p, q[q < 64]] = w2v[q[q < 64], 2 * p]
        w2p[p, q[q >= 64]] = w2v[q[q >= 64] - 64, 2 * p + 1]
    g["w2Tp"] = w2p
    g["wkT"] = np.ascontiguousarray(np.transpose(inputs["L1_attn_W_K"], (2, 0, 1)).reshape(D, AH * ADH))
    g["wqT"] = np.ascontiguousarray(np.transpose(inputs["L1_attn_W_Q"], (2, 0, 1)).reshape(D, AH * ADH))
    g["wvT"] = np.ascontiguousarray(np.transpose(inputs["L1_attn_W_V"], (2, 0, 1)).reshape(D, AH * ADH))
    g["woT"] = np.ascontiguousarray(inputs["L1_attn_W_O"].T)
    g["wgT"] = np.ascontiguousarray(inputs["L1_ffn_w_gate"].T)
    g["wuT"] = np.ascontiguousarray(inputs["L1_ffn_w_up"].T)
    g["wdT"] = np.ascontiguousarray(inputs["L1_ffn_w_down"].T)
    g["convw"] = np.ascontiguousarray(inputs["L0_attn_conv_w"].reshape(33, 128, 4))
    g["convb"] = np.ascontiguousarray(inputs["L0_attn_conv_b"].reshape(33, 128).T)
    g["dtb"] = np.ascontiguousarray(inputs["L0_attn_dt_bias"].reshape(64, 1))
    g["alog"] = np.ascontiguousarray(inputs["L0_attn_A_log"].reshape(64, 1))
    g["dvec"] = np.ascontiguousarray(inputs["L0_attn_D"].reshape(1, 64))
    g["nw"] = np.ascontiguousarray(inputs["L0_attn_norm_weight"].reshape(32, 128).T)
    g["c_id_bf"] = np.eye(128, dtype=NPBF)
    g["c_id_f"] = np.eye(128, dtype=np.float32)
    g["c_h128"] = hadamard(128).astype(NPBF)
    h64 = hadamard(64)
    hp = np.zeros((128, 128), np.float32)
    hp[0:64, 0:64] = h64
    hp[64:128, 64:128] = h64
    g["c_hp"] = (hp / np.sqrt(np.float32(HFF))).astype(NPBF)
    tri = np.triu(np.ones((128, 128), np.float32))
    g["c_triu_f"] = tri
    g["c_triu_bf"] = tri.astype(NPBF)
    g["c_ones_f"] = np.ones((1, 128), np.float32)
    g["c_onescol_bf"] = np.ones((128, 1), NPBF)
    for k in g:
        want = WEIGHTS[k][1]
        if g[k].dtype != want:
            g[k] = np.ascontiguousarray(g[k].astype(want))
    return g


GROUPS = [["s0"], ["s1"], ["s2"], ["s3a"], ["s4"], ["s5"], ["s6"],
          ["s7"], ["s8"], ["s9"], ["s10"]]
FUSED_GROUPS = [["s0", "s1", "s2", "s3a", "s4", "s5", "s6",
                 "s7", "s8", "s9", "s10"]]


def kernel(**inputs):
    x0 = np.ascontiguousarray(inputs["L0_attn_hidden_states"].astype(np.float32))
    ncores = 8
    g = prep_host(inputs)
    state = [dict() for _ in range(ncores)]
    for ci in range(ncores):
        state[ci]["x0"] = np.ascontiguousarray(x0[ci])
    out = np.zeros((ncores, T, D), np.float32)
    for grp in FUSED_GROUPS:
        nc, innames, outnames = build_program(grp)
        in_maps = []
        for ci in range(ncores):
            m = {}
            for n in innames:
                m[n] = g[n] if n in g else state[ci][n]
            in_maps.append(m)
        res = run_bass_kernel_spmd(nc, in_maps, core_ids=list(range(ncores)))
        for ci in range(ncores):
            for n in outnames:
                state[ci][n] = res.results[ci][n]
    for ci in range(ncores):
        out[ci] = np.asarray(state[ci]["out"], np.float32)
    return out
